# version 08
# baseline (speedup 1.0000x reference)
import numpy as np
import ml_dtypes
from contextlib import ExitStack
import concourse.bass as bass
import concourse.mybir as mybir
from concourse.bass_utils import run_bass_kernel_spmd

F32 = mybir.dt.float32
BF16 = mybir.dt.bfloat16
AF = mybir.ActivationFunctionType
ALU = mybir.AluOpType
AX = mybir.AxisListType

NCORES = 8
NT = 8
TPC = NT * 128
D = 4096
NC_D = D // 128
BF = ml_dtypes.bfloat16

REPLICA_PAIRS = [[0, 1], [2, 3], [4, 5], [6, 7]]


class Sem:
    def __init__(self, handle, name):
        self.h = handle
        self.name = name
        self.n = 0

    def bump(self, amt):
        self.n += amt
        return self.n


class Plan:
    ENGINES = ("tensor", "vector", "scalar", "gpsimd", "sync")

    def __init__(self):
        self.streams = {e: [] for e in self.ENGINES}
        self.n_ops = 0

    def op(self, engine, fn, waits=(), inc=None):
        wl = []
        for (s, v) in waits:
            assert v <= s.n, f"wait on {s.name}={v} > planned {s.n}"
            wl.append((s.h, v))
        if inc is not None:
            sem, amt = inc
            sem.bump(amt)
            il = (sem.h, amt)
        else:
            il = None
        self.n_ops += 1

        def run(eng):
            for (h, v) in wl:
                eng.wait_ge(h, v)
            ins = fn(eng)
            if il is not None:
                assert ins is not None
                ins.then_inc(il[0], il[1])
        self.streams[engine].append(run)

    def wait(self, engine, sem, val):
        self.op(engine, lambda e: None, waits=[(sem, val)])

    def fence(self, sems, queues=("sync", "gpsimd", "scalar")):
        keys = ["ld", "ld2", "wt", "wtA", "wtB", "sto", "lh0", "lh1",
                "lv0", "lv1", "st", "pe", "ev", "ac"]
        targets = [(sems[k], sems[k].n) for k in keys]
        for q in queues:
            self.op(q, lambda e: None, waits=targets)

    def barrier(self, sems, dma_sems=()):
        engines = self.ENGINES
        dma_targets = [(s, s.n) for s in dma_sems]
        for e in engines:
            for (s, v) in dma_targets:
                self.wait(e, s, v)
        targets = {}
        for e in engines:
            targets[e] = sems[e].bump(1)
        for e in engines:
            s = sems[e]

            def mk(s=s):
                def run(eng):
                    eng.sem_inc(s.h, 1)
                return run
            self.streams[e].append(mk())
            self.n_ops += 1
        for e in engines:
            for o in engines:
                self.wait(e, sems[o], targets[o])

    def realize(self, nc):
        with nc.Block() as block:
            @block.tensor
            def _(t):
                for fn in self.streams["tensor"]:
                    fn(t)

            @block.vector
            def _(v):
                for fn in self.streams["vector"]:
                    fn(v)

            @block.scalar
            def _(s):
                for fn in self.streams["scalar"]:
                    fn(s)

            @block.gpsimd
            def _(g):
                for fn in self.streams["gpsimd"]:
                    fn(g)

            @block.sync
            def _(sy):
                for fn in self.streams["sync"]:
                    fn(sy)


_NAME_SEQ = [0]


def _uname(n):
    _NAME_SEQ[0] += 1
    return f"{n}_{_NAME_SEQ[0]}"


def row128(ap2d, i):
    return ap2d[i:i + 1, :].rearrange("o (p f) -> (o p) f", p=128)



def _tile_lhst(w, nkc, nm):
    i_, o_ = w.shape
    assert i_ == nkc * 128 and o_ == nm * 128
    t = w.reshape(nkc, 128, nm, 128).transpose(2, 1, 0, 3)
    return np.ascontiguousarray(t.reshape(nm, 128 * nkc * 128))


def _tile_rhs(w, nkc, nob, obw=512):
    i_, o_ = w.shape
    assert i_ == nkc * 128 and o_ == nob * obw
    t = w.reshape(nkc, 128, nob, obw).transpose(2, 1, 0, 3)
    return np.ascontiguousarray(t.reshape(nob, 128 * nkc * obw))


def host_prep_A(inputs):
    x = np.asarray(inputs["L0_attn_x"], dtype=np.float32)
    qkvw = np.asarray(inputs["L0_attn_attn_qkvw"], dtype=np.float32)
    qkvb = np.asarray(inputs["L0_attn_attn_qkvb"], dtype=np.float32)
    ow = np.asarray(inputs["L0_attn_attn_ow"], dtype=np.float32)
    ob = np.asarray(inputs["L0_attn_attn_ob"], dtype=np.float32)

    shared = {
        "w_qk": _tile_lhst(qkvw[:, :8192], 32, 64),
        "w_v": _tile_rhs(qkvw[:, 8192:], 32, 8),
        "b_qk": np.ascontiguousarray(qkvb[:8192].reshape(64, 128).T),
        "b_v_t": np.ascontiguousarray(np.tile(qkvb[8192:][None, :], (128, 1))),
        "w_o": _tile_rhs(ow, 32, 8),
        "ob_t": np.ascontiguousarray(np.tile(ob[None, :], (128, 1))),
        "ident": np.eye(128, dtype=np.float32),
    }
    per_core = []
    for c in range(NCORES):
        b, p = c // 2, c % 2
        tiles = [2 * j + p for j in range(NT)]
        idx = np.concatenate([np.arange(g * 128, g * 128 + 128) for g in tiles])
        xs = x[b, idx, :]
        d = dict(shared)
        d["x_tm"] = np.ascontiguousarray(xs.reshape(NT, 128 * D))
        d["x_fm"] = np.ascontiguousarray(xs.T.reshape(NC_D, 128 * TPC))
        per_core.append(d)
    return per_core


def _hperm():
    tiles = []
    for fc in range(21):
        for r in range(4):
            tiles.append(np.arange(2752 * r + 128 * fc, 2752 * r + 128 * fc + 128))
    tiles.append(np.concatenate([np.arange(2752 * 0 + 2688, 2752 * 0 + 2752),
                                 np.arange(2752 * 1 + 2688, 2752 * 1 + 2752)]))
    tiles.append(np.concatenate([np.arange(2752 * 2 + 2688, 2752 * 2 + 2752),
                                 np.arange(2752 * 3 + 2688, 2752 * 3 + 2752)]))
    return np.concatenate(tiles)


def host_prep_B(inputs):
    w1q = np.asarray(inputs["L0_ffn_w1q"], dtype=np.float32)
    a1 = np.asarray(inputs["L0_ffn_a1"], dtype=np.float32)
    b1 = np.asarray(inputs["L0_ffn_b1"], dtype=np.float32)
    w3q = np.asarray(inputs["L0_ffn_w3q"], dtype=np.float32)
    a3 = np.asarray(inputs["L0_ffn_a3"], dtype=np.float32)
    b3 = np.asarray(inputs["L0_ffn_b3"], dtype=np.float32)
    w2q = np.asarray(inputs["L0_ffn_w2q"], dtype=np.float32)
    a2 = np.asarray(inputs["L0_ffn_a2"], dtype=np.float32)
    b2 = np.asarray(inputs["L0_ffn_b2"], dtype=np.float32)

    hp = _hperm()
    d = {}
    d["w1t"] = _tile_lhst(np.ascontiguousarray(w1q.T[:, hp]), 8, 86)
    d["w3t"] = _tile_lhst(np.ascontiguousarray(w3q.T[:, hp]), 8, 86)
    w2p = np.zeros((2816, 4096), dtype=np.float32)
    w2p[:2752] = w2q.T
    d["w2t"] = _tile_rhs(w2p, 22, 8)
    d["a1t"] = np.ascontiguousarray(a1.reshape(2, 32, 128).transpose(2, 1, 0).reshape(128, 64))
    d["a3t"] = np.ascontiguousarray(a3.reshape(2, 32, 128).transpose(2, 1, 0).reshape(128, 64))
    a2p = a2[:, hp]
    d["a2t"] = np.ascontiguousarray(a2p.reshape(2, 86, 128).transpose(2, 1, 0).reshape(128, 172))
    d["b1T"] = np.ascontiguousarray(b1.T[:, hp])
    d["b3T"] = np.ascontiguousarray(b3.T[:, hp])
    d["b2T"] = np.ascontiguousarray(b2.T)
    f2 = np.zeros((128, 64), dtype=np.float32)
    for k in range(128):
        f2[k, k % 64] = 1.0
    d["fold2"] = f2
    return d


def declare_dram_B(nc, io, debug=False):
    ei = lambda n, shp, dt=F32: io.__setitem__(
        n, nc.dram_tensor(n, shp, dt, kind="ExternalInput").ap())
    ei("w1t", [86, 128 * 8 * 128])
    ei("w3t", [86, 128 * 8 * 128])
    ei("w2t", [8, 128 * 22 * 512])
    ei("a1t", [128, 64])
    ei("a3t", [128, 64])
    ei("a2t", [128, 172])
    ei("b1T", [2, 11008])
    ei("b3T", [2, 11008])
    ei("b2T", [2, 4096])
    ei("fold2", [128, 64])
    io["s1b"] = nc.dram_tensor("s1b", [NT, 128 * D], F32, kind="Internal").ap()
    if debug:
        io["_mkdbg"]("s1b", [NT, 128 * D], F32)
        io["dbg_hpfb"] = nc.dram_tensor(
            "dbg_hpfb", [22, 128 * 1024], BF16, kind="ExternalOutput").ap()
        io["dbg_small"] = nc.dram_tensor(
            "dbg_small", [2, 3 * 1024], F32, kind="ExternalOutput").ap()
        io["dbg_pr"] = nc.dram_tensor(
            "dbg_pr", [3, 128 * 1024], BF16, kind="ExternalOutput").ap()


def phase_B(nc, pl, st, io, sems, glob, x_fm_key="x1_fm", xf_key="x1f_fm",
            out_key="s1b", fm_ready=0):
    ld, ld2, wt = sems["ld"], sems["ld2"], sems["wt"]
    pe, ev, ac = sems["pe"], sems["ev"], sems["ac"]
    sto = sems["sto"]
    wtP = [sems["wtA"], sems["wtB"]]

    sb = lambda n, shp, dt=BF16: st.enter_context(nc.sbuf_tensor(_uname(n), shp, dt))
    psa = lambda n, shp, dt=F32: st.enter_context(nc.psum_tensor(_uname(n), shp, dt))

    NF = 22

    xf_bf = sb("b_xf", [128, NT * 1024])
    hpfb = sb("b_hpfb", [128, NF * 1024])
    b2T = sb("b_b2T", [2, 4096])
    h1s = sb("b_h1s", [2, 1024])
    h3s = sb("b_h3s", [2, 1024])
    h2s = sb("b_h2s", [2, 1024])

    pl.op("sync", lambda s: s.dma_start(
        out=xf_bf[:, :].rearrange("p (c f) -> p c f", c=NT),
        in_=io[xf_key].rearrange("c (p f) -> p c f", p=128)),
        waits=[(sems["st"], fm_ready)], inc=(ld, 16))
    pl.op("gpsimd", lambda g: g.dma_start(out=b2T[:, :], in_=io["b2T"]),
          inc=(wt, 16))
    xf_tgt = ld.n
    pl.op("vector", lambda v: v.memset(
        hpfb[:, :].rearrange("p (c f) -> p c f", c=NF)[64:128, 21, :], 0.0),
        waits=[(ev, ev.n)], inc=(ev, 1))

    with ExitStack() as st2:
        x1fm = st2.enter_context(nc.sbuf_tensor("b_x1fm", [128, NC_D * 1024], BF16))
        pl.fence(sems)
        a1t = st2.enter_context(nc.sbuf_tensor("b_a1t", [128, 64], BF16))
        a3t = st2.enter_context(nc.sbuf_tensor("b_a3t", [128, 64], BF16))
        ps_h = st2.enter_context(nc.psum_tensor("b_psh", [2, 2048], F32))
        pl.op("sync", lambda s: s.dma_start(
            out=x1fm[:, :].rearrange("p (c f) -> p c f", c=NC_D),
            in_=io[x_fm_key].rearrange("c (p f) -> p c f", p=128)),
            waits=[(sems["st"], fm_ready)], inc=(ld, 16))
        pl.op("gpsimd", lambda g: g.dma_start(out=a1t[:, :], in_=io["a1t"]),
              inc=(wt, 16))
        pl.op("gpsimd", lambda g: g.dma_start(out=a3t[:, :], in_=io["a3t"]),
              inc=(wt, 16))
        lw = [(ld, ld.n), (wt, wt.n), (ev, ev.n), (ac, ac.n)]
        for at, off in ((a1t, 0), (a3t, 1024)):
            for kc in range(NC_D):
                for h2i in range(2):
                    pl.op("tensor", lambda t, kc=kc, h2i=h2i, at=at, off=off: t.matmul(
                        ps_h[:, off + h2i * 512: off + (h2i + 1) * 512],
                        at[:, kc * 2:kc * 2 + 2],
                        x1fm[:, kc * 1024 + h2i * 512: kc * 1024 + (h2i + 1) * 512],
                        start=(kc == 0), stop=(kc == 31)),
                        waits=lw if (kc == 0 and h2i == 0 and off == 0) else (),
                        inc=(pe, 1) if (kc == 31 and h2i == 1) else None)
        h_tgt = pe.n
        pl.op("vector", lambda v: v.tensor_scalar(
            h1s[:, :], ps_h[:, 0:1024], 0.5, None, ALU.mult, ALU.bypass),
            waits=[(pe, h_tgt), (ev, ev.n)], inc=(ev, 1))
        pl.op("vector", lambda v: v.tensor_scalar(
            h3s[:, :], ps_h[:, 1024:2048], 0.5, None, ALU.mult, ALU.bypass),
            waits=[(ev, ev.n)], inc=(ev, 1))
        h13_done = ev.n

    with ExitStack() as st1:
        sb1 = lambda n, shp, dt=BF16: st1.enter_context(nc.sbuf_tensor(_uname(n), shp, dt))
        ps1 = lambda n, shp: st1.enter_context(nc.psum_tensor(_uname(n), shp, F32))
        b1T = sb1("b_b1T", [2, 11008])
        b3T = sb1("b_b3T", [2, 11008])
        a2t = sb1("b_a2t", [128, 172])
        eyef = sb1("b_eye", [128, 128])
        fold2 = sb1("b_fold2", [128, 64])
        w1sb = [sb1(f"b_w1_{i}", [128, 8 * 128]) for i in range(2)]
        w3sb = [sb1(f"b_w3_{i}", [128, 8 * 128]) for i in range(2)]
        gstg = [sb1(f"b_g{i}", [128, 512]) for i in range(2)]
        ustg = [sb1(f"b_u{i}", [128, 512]) for i in range(2)]
        prod = [sb1(f"b_p{i}", [128, 512]) for i in range(2)]
        ps_g = [ps1(f"b_psg{i}", [128, 512]) for i in range(2)]
        ps_u = [ps1(f"b_psu{i}", [128, 512]) for i in range(2)]
        ps_h2 = ps1("b_psh2", [2, 1024])
        ps_f = ps1("b_psf", [128, 1024])

        pl.fence(sems)
        first_small = True
        for nm, dst in (("b1T", b1T), ("b3T", b3T), ("a2t", a2t),
                        ("fold2", fold2), ("ident", eyef)):
            pl.op("gpsimd", lambda g, nm=nm, dst=dst: g.dma_start(
                out=dst[:, :], in_=io[nm]),
                waits=[(ev, h13_done)] if first_small else (), inc=(wt, 16))
            first_small = False
        smalls_tgt = wt.n

        w_tgt = [0, 0]
        prod_pe = {0: pe.n, 1: pe.n}
        stg_ac = {0: ac.n, 1: ac.n}
        hpfb_evs = []
        mi = 0
        for fc in range(NF):
            nr = 4 if fc < 21 else 2
            for r in range(nr):
                mh = fc * 4 + r if fc < 21 else 84 + r
                buf = mi % 2
                lw = [(pe, prod_pe[buf])]
                pl.op("gpsimd", lambda g, mh=mh, buf=buf: g.dma_start(
                    out=w1sb[buf][:, :], in_=row128(io["w1t"], mh)),
                    waits=lw, inc=(wtP[buf], 16))
                pl.op("gpsimd", lambda g, mh=mh, buf=buf: g.dma_start(
                    out=w3sb[buf][:, :], in_=row128(io["w3t"], mh)),
                    inc=(wtP[buf], 16))
                w_tgt[buf] = wtP[buf].n
                for th in range(2):
                    mw = [(wtP[buf], w_tgt[buf]), (wt, wt.n), (ld, xf_tgt),
                          (ev, h13_done), (ac, stg_ac[buf])]
                    for wsb, hs, bT, pp in ((w1sb, h1s, b1T, ps_g),
                                            (w3sb, h3s, b3T, ps_u)):
                        for kc in range(NT):
                            pl.op("tensor",
                                  lambda t, kc=kc, th=th, buf=buf, wsb=wsb, pp=pp: t.matmul(
                                      pp[buf][:, :],
                                      wsb[buf][:, kc * 128:(kc + 1) * 128],
                                      xf_bf[:, kc * 1024 + th * 512: kc * 1024 + th * 512 + 512],
                                      start=(kc == 0), stop=False),
                                  waits=mw if (kc == 0 and wsb is w1sb) else ())
                        pl.op("tensor",
                              lambda t, mh=mh, th=th, buf=buf, bT=bT, hs=hs, pp=pp: t.matmul(
                                  pp[buf][:, :], bT[:, mh * 128:(mh + 1) * 128],
                                  hs[:, th * 512:th * 512 + 512],
                                  start=False, stop=True),
                              inc=(pe, 1))
                    gu_tgt = pe.n
                    pl.op("scalar", lambda s, buf=buf: s.activation(
                        gstg[buf][:, :], ps_g[buf][:, :], AF.Silu),
                        waits=[(pe, gu_tgt), (ev, ev.n), (ac, ac.n)], inc=(ac, 1))
                    pl.op("scalar", lambda s, buf=buf: s.activation(
                        ustg[buf][:, :], ps_u[buf][:, :], AF.Copy),
                        waits=[(ac, ac.n)], inc=(ac, 1))
                    stg_ac[buf] = ac.n
                    pl.op("vector", lambda v, buf=buf: v.tensor_mul(
                        prod[buf][:, :], gstg[buf][:, :], ustg[buf][:, :]),
                        waits=[(ac, ac.n), (ev, ev.n), (pe, pe.n)], inc=(ev, 1))
                    prod_tgt = ev.n
                    if "dbg_pr" in io and mi == 0:
                        pl.op("sync", lambda s, th=th, buf=buf: s.dma_start(
                            out=row128(io["dbg_pr"], 0)[:, th * 512:th * 512 + 512],
                            in_=prod[buf][:, :]),
                            waits=[(ev, prod_tgt)], inc=(ld, 16))
                        pl.op("sync", lambda s, th=th, buf=buf: s.dma_start(
                            out=row128(io["dbg_pr"], 1)[:, th * 512:th * 512 + 512],
                            in_=gstg[buf][:, :]), inc=(ld, 16))
                        pl.op("sync", lambda s, th=th, buf=buf: s.dma_start(
                            out=row128(io["dbg_pr"], 2)[:, th * 512:th * 512 + 512],
                            in_=ustg[buf][:, :]), inc=(ld, 16))
                    pl.op("tensor", lambda t, mh=mh, th=th, buf=buf: t.matmul(
                        ps_h2[:, th * 512:th * 512 + 512],
                        a2t[:, mh * 2:mh * 2 + 2], prod[buf][:, :],
                        start=(mi == 0), stop=(mi == 85),
                        skip_group_check=True),
                        waits=[(ev, prod_tgt)], inc=(pe, 1))
                    fmat = eyef if fc < 21 else fold2
                    rows = 128 if fc < 21 else 64
                    fold_start = (r == 0)
                    fold_stop = (r == nr - 1)
                    fw = [(ev, prod_tgt)]
                    if fold_start and hpfb_evs:
                        fw.append((ev, hpfb_evs[-1]))
                    pl.op("tensor", lambda t, th=th, buf=buf, fmat=fmat, rows=rows,
                          fold_start=fold_start, fold_stop=fold_stop: t.matmul(
                              ps_f[0:rows, th * 512:th * 512 + 512],
                              fmat[:, :], prod[buf][:, :],
                              start=fold_start, stop=fold_stop,
                              skip_group_check=True),
                          waits=fw, inc=(pe, 1))
                    prod_pe[buf] = pe.n
                if (fc < 21 and r == 3) or (fc == 21 and r == nr - 1):
                    f_tgt = pe.n
                    rows = 128 if fc < 21 else 64
                    pl.op("vector", lambda v, fc=fc, rows=rows: v.tensor_copy(
                        hpfb[0:rows, fc * 1024:(fc + 1) * 1024],
                        ps_f[0:rows, :]),
                        waits=[(pe, f_tgt), (ev, ev.n)], inc=(ev, 1))
                    hpfb_evs.append(ev.n)
                mi += 1
        b1_done_pe = pe.n
        pl.op("vector", lambda v: v.tensor_scalar(
            h2s[:, :], ps_h2[:, :], 0.5, None, ALU.mult, ALU.bypass),
            waits=[(pe, b1_done_pe), (ev, ev.n)], inc=(ev, 1))
        hpfb_done = ev.n
        if "dbg_hpfb" in io:
            for c in range(NF):
                pl.op("sync", lambda s, c=c: s.dma_start(
                    out=row128(io["dbg_hpfb"], c),
                    in_=hpfb[:, c * 1024:(c + 1) * 1024]),
                    waits=[(ev, hpfb_done)] if c == 0 else (), inc=(ld, 16))
            pl.op("gpsimd", lambda g: g.dma_start(
                out=io["dbg_small"][:, 0:1024], in_=h1s[:, :]),
                waits=[(ev, hpfb_done)], inc=(wt, 16))
            pl.op("gpsimd", lambda g: g.dma_start(
                out=io["dbg_small"][:, 1024:2048], in_=h3s[:, :]), inc=(wt, 16))
            pl.op("gpsimd", lambda g: g.dma_start(
                out=io["dbg_small"][:, 2048:3072], in_=h2s[:, :]), inc=(wt, 16))

    with ExitStack() as st3:
        sb3 = lambda n, shp, dt=BF16: st3.enter_context(nc.sbuf_tensor(_uname(n), shp, dt))
        ps3 = lambda n, shp: st3.enter_context(nc.psum_tensor(_uname(n), shp, F32))
        w2sb = [sb3(f"b_w2_{i}", [128, 22 * 512]) for i in range(2)]
        ostg = [sb3(f"b_o{i}", [128, 512], F32) for i in range(4)]
        pl.fence(sems)
        ps_o = [ps3(f"b_pso{i}", [128, 512]) for i in range(4)]
        w2_tgt = [0, 0]
        pe_b2_0 = pe.n
        for obi in range(8):
            buf = obi % 2
            lw = [(ev, hpfb_done)] if obi < 2 else [(pe, max(pe_b2_0, pe.n - 8))]
            pl.op("gpsimd", lambda g, obi=obi, buf=buf: g.dma_start(
                out=w2sb[buf][:, :], in_=row128(io["w2t"], obi)), waits=lw,
                inc=(wtP[buf], 16))
            w2_tgt[buf] = wtP[buf].n
            for jt in range(NT):
                pp = ps_o[jt % 4]
                mmw = [(wtP[buf], w2_tgt[buf]), (ev, hpfb_done), (ev, ev.n)]
                for kc in range(NF):
                    pl.op("tensor", lambda t, jt=jt, kc=kc, pp=pp, buf=buf: t.matmul(
                        pp[:, :],
                        hpfb[:, kc * 1024 + jt * 128: kc * 1024 + (jt + 1) * 128],
                        w2sb[buf][:, kc * 512:(kc + 1) * 512],
                        start=(kc == 0), stop=False),
                        waits=mmw if kc == 0 else ())
                pl.op("tensor", lambda t, jt=jt, obi=obi, pp=pp: t.matmul(
                    pp[:, :], h2s[:, jt * 128:(jt + 1) * 128],
                    b2T[:, obi * 512:(obi + 1) * 512], start=False, stop=True),
                    inc=(pe, 1))
                mm_tgt = pe.n
                dst = ostg[jt % 4]
                pl.op("vector", lambda v, pp=pp, dst=dst: v.tensor_copy(
                    dst[:, :], pp[:, :]),
                    waits=[(pe, mm_tgt), (sto, sto.n), (ev, ev.n)], inc=(ev, 1))
                ev_tgt = ev.n
                pl.op("sync", lambda s, jt=jt, obi=obi, dst=dst: s.dma_start(
                    out=row128(io[out_key], jt)[:, obi * 512:(obi + 1) * 512],
                    in_=dst[:, :]), waits=[(ev, ev_tgt)], inc=(sto, 16))
    return sto.n



H_L1 = 32
NOPE, ROPE, VD = 128, 64, 128
QKD = NOPE + ROPE
Q_RANK, KV_RANK = 1536, 512
IDX_H, IDX_D, TOPK = 16, 128, 1024
SC_QK = float(QKD ** -0.5)
SC_IDX = float((IDX_H ** -0.5) * (IDX_D ** -0.5))
NEG = -1e30


def host_prep_C(inputs, core=None):
    q_a_w = np.asarray(inputs["L1_attn_q_a_w"], np.float32)
    q_a_nw = np.asarray(inputs["L1_attn_q_a_norm_w"], np.float32)
    q_b_w = np.asarray(inputs["L1_attn_q_b_w"], np.float32)
    kv_a_w = np.asarray(inputs["L1_attn_kv_a_w"], np.float32)
    kv_a_nw = np.asarray(inputs["L1_attn_kv_a_norm_w"], np.float32)
    kv_b_w = np.asarray(inputs["L1_attn_kv_b_w"], np.float32)
    o_w = np.asarray(inputs["L1_attn_o_w"], np.float32)
    idx_wq = np.asarray(inputs["L1_attn_idx_wq_w"], np.float32)
    idx_wk = np.asarray(inputs["L1_attn_idx_wk_w"], np.float32)
    idx_ww = np.asarray(inputs["L1_attn_idx_weights_w"], np.float32)

    d = {}
    d["q_a_t"] = _tile_rhs(q_a_w, 32, 3)
    d["qnw"] = np.ascontiguousarray(q_a_nw.reshape(12, 128).T)
    cols_n = np.concatenate([np.arange(h * 192, h * 192 + 128) for h in range(32)])
    cols_r = np.concatenate([np.arange(h * 192 + 128, (h + 1) * 192) for h in range(32)])
    d["qb_n_t"] = _tile_lhst(q_b_w[:, cols_n], 12, 32)
    d["qb_r_t"] = _tile_lhst(q_b_w[:, cols_r], 12, 16)
    d["kv_a_t"] = np.ascontiguousarray(
        kv_a_w.reshape(32, 128, 576).transpose(1, 0, 2).reshape(128, 32 * 576))
    d["kvnw"] = np.ascontiguousarray(kv_a_nw.reshape(4, 128).T)
    cols_kn = np.concatenate([np.arange(h * 256, h * 256 + 128) for h in range(32)])
    cols_v = np.concatenate([np.arange(h * 256 + 128, (h + 1) * 256) for h in range(32)])
    d["kv_b_n_t"] = _tile_lhst(kv_b_w[:, cols_kn], 4, 32)
    d["kv_b_v_t"] = np.ascontiguousarray(
        kv_b_w[:, cols_v].reshape(4, 128, 4096).transpose(1, 0, 2).reshape(128, 4 * 4096))
    d["o_w_t"] = _tile_rhs(o_w, 32, 8)
    d["idx_wq_t"] = _tile_lhst(idx_wq, 32, 16)
    d["idx_wk_t"] = np.ascontiguousarray(
        idx_wk.reshape(32, 128, 128).transpose(1, 0, 2).reshape(128, 32 * 128))
    d["idx_ww_t"] = np.ascontiguousarray(
        idx_ww.reshape(32, 128, 16).transpose(1, 0, 2).reshape(128, 32 * 16))
    R = np.zeros((64, 64), np.float32)
    for f in range(32):
        R[2 * f + 1, 2 * f] = -1.0
        R[2 * f, 2 * f + 1] = 1.0
    R128 = np.zeros((128, 128), np.float32)
    R128[:64, :64] = R
    R128[64:, 64:] = R
    d["R128"] = R128
    d["zeros_t"] = np.zeros((128, D), np.float32)
    return d


def host_prep_C_core(c):
    p = c % 2
    tiles = [2 * j + p for j in range(NT)]
    pos = np.concatenate([np.arange(g * 128, g * 128 + 128) for g in tiles]).astype(np.float64)
    inv = 1.0 / (10000.0 ** (np.arange(0, ROPE, 2, dtype=np.float64) / ROPE))
    ang = pos[None, :] * inv[:, None]
    cosq = np.cos(ang).astype(np.float32)
    sinq = np.sin(ang).astype(np.float32)
    cqi = np.repeat(cosq, 2, axis=0)
    sqi = np.repeat(sinq, 2, axis=0)
    d = {}
    d["cq_t"] = np.ascontiguousarray(np.concatenate([cqi, cqi], axis=0))
    d["sq_t"] = np.ascontiguousarray(np.concatenate([sqi, sqi], axis=0))
    ctm = np.repeat(cosq.T, 2, axis=1)
    stm = np.repeat(sinq.T, 2, axis=1)
    d["ck_t"] = np.ascontiguousarray(ctm.reshape(NT, 128 * ROPE))
    d["sk_t"] = np.ascontiguousarray(stm.reshape(NT, 128 * ROPE))
    cm = np.zeros((NT, 128, 256), np.float32)
    for j in range(NT):
        gq = 2 * j + p
        qpos = np.arange(gq * 128, gq * 128 + 128)
        kpos = np.arange(2 * j * 128, 2 * j * 128 + 256)
        cm[j] = np.where(qpos[:, None] >= kpos[None, :], 0.0, NEG)
    d["cmask"] = np.ascontiguousarray(cm.reshape(NT, 128 * 256))
    return d


def declare_dram_C(nc, io, debug=False):
    ei = lambda n, shp, dt=F32: io.__setitem__(
        n, nc.dram_tensor(n, shp, dt, kind="ExternalInput").ap())
    internal = lambda n, shp, dt=BF16: io.__setitem__(
        n, nc.dram_tensor(n, shp, dt, kind="Internal").ap())
    ei("q_a_t", [3, 128 * 32 * 512])
    ei("qnw", [128, 12])
    ei("qb_n_t", [32, 128 * 12 * 128])
    ei("qb_r_t", [16, 128 * 12 * 128])
    ei("kv_a_t", [128, 32 * 576])
    ei("kvnw", [128, 4])
    ei("kv_b_n_t", [32, 128 * 4 * 128])
    ei("kv_b_v_t", [128, 4 * 4096])
    ei("o_w_t", [8, 128 * 32 * 512])
    ei("idx_wq_t", [16, 128 * 32 * 128])
    ei("idx_wk_t", [128, 32 * 128])
    ei("idx_ww_t", [128, 32 * 16])
    ei("R128", [128, 128])
    ei("zeros_t", [128, D])
    ei("cq_t", [128, TPC])
    ei("sq_t", [128, TPC])
    ei("ck_t", [NT, 128 * ROPE])
    ei("sk_t", [NT, 128 * ROPE])
    ei("cmask", [NT, 128 * 256])
    internal("qn_dr", [32, 128 * 1024])
    internal("qr_dr", [16, 128 * 1024])
    internal("kn_dr", [32, 128 * 2048])
    internal("v_dr", [16, 128 * 4096])
    internal("cx_src", [NT, 128 * 704])
    internal("cx_dst", [2 * NT, 128 * 704])
    internal("s1c", [NT, 128 * D], F32)
    if debug:
        io["_mkdbg"]("s1c", [NT, 128 * D], F32)
        io["_mkdbg"]("qn_dr", [32, 128 * 1024], BF16)
        io["_mkdbg"]("qr_dr", [16, 128 * 1024], BF16)
        io["_mkdbg"]("kn_dr", [32, 128 * 2048], BF16)
        io["_mkdbg"]("v_dr", [16, 128 * 4096], BF16)
        io["dbg_m01"] = nc.dram_tensor(
            "dbg_m01", [NT, 128 * 2048], BF16, kind="ExternalOutput").ap()
        io["dbg_cidx"] = nc.dram_tensor(
            "dbg_cidx", [NT, 128 * 2048], F32, kind="ExternalOutput").ap()


def phase_C(nc, pl, st, io, sems, glob, x_fm_key="x2_fm", out_key="s1c",
            fm_ready=0):
    ld, ld2, wt = sems["ld"], sems["ld2"], sems["wt"]
    pe, ev, ac, cc, sto = (sems["pe"], sems["ev"], sems["ac"], sems["cc"],
                           sems["sto"])
    lh = [sems["lh0"], sems["lh1"]]
    lv = [sems["lv0"], sems["lv1"]]
    wtP = [sems["wtA"], sems["wtB"]]
    ident = glob["ident"]
    eps = glob["eps"]

    sb = lambda n, shp, dt=BF16: st.enter_context(nc.sbuf_tensor(_uname(n), shp, dt))

    kpe_fm = sb("c_kpe", [64, 2048])

    def vchain(fn, extra=()):
        pl.op("vector", fn, waits=[(ev, ev.n)] + list(extra), inc=(ev, 1))

    with ExitStack() as st1:
        sb1 = lambda n, shp, dt=BF16: st1.enter_context(nc.sbuf_tensor(_uname(n), shp, dt))
        x2fm = sb1("c_x2fm", [128, NC_D * 1024])
        ik_fm = sb1("c_ikfm", [128, 2048])
        pl.op("sync", lambda s: s.dma_start(
            out=x2fm[:, :].rearrange("p (c f) -> p c f", c=NC_D),
            in_=io[x_fm_key].rearrange("c (p f) -> p c f", p=128)),
            waits=[(sems["st"], fm_ready)], inc=(ld, 16))
        x2_tgt = ld.n

        pl.fence(sems)
        st12 = st1.enter_context(ExitStack())
        qlat_fm = st12.enter_context(
            nc.sbuf_tensor("c_qlatfm", [128, 12 * 1024], BF16))
        with ExitStack() as st2:
            sb2 = lambda n, shp, dt=BF16: st2.enter_context(nc.sbuf_tensor(_uname(n), shp, dt))
            ps2 = lambda n, shp: st2.enter_context(nc.psum_tensor(_uname(n), shp, F32))
            qa0 = sb2("c_qa0", [128, 32 * 512])
            qa = [qa0, qa0]
            qlat_tm = sb2("c_qlat_tm", [128, NT * 1536], F32)
            qnw = sb2("c_qnw", [128, 12], F32)
            qlat_bf = sb2("c_qlatbf", [128, 1536])
            c1dmp = sb2("c_dmp", [128, 1536], F32)
            c1ss = sb2("c_ss", [128, 1], F32)
            c1rstd = sb2("c_rstd", [128, 1], F32)
            ps_q = [ps2(f"c_psq{i}", [128, 512]) for i in range(2)]
            ps_t = ps2("c_pst", [128, 512])

            pl.op("sync", lambda s: s.dma_start(out=qnw[:, :], in_=io["qnw"]),
                  inc=(ld, 16))
            qnw_tgt = ld.n
            for ob3 in range(3):
                buf = 0
                lw = [(pe, pe.n)]
                pl.op("gpsimd", lambda g, ob3=ob3, buf=buf: g.dma_start(
                    out=qa[buf][:, :], in_=row128(io["q_a_t"], ob3)),
                    waits=lw, inc=(wtP[buf], 16))
                w_tgt = wtP[buf].n
                for jt in range(NT):
                    pp = ps_q[jt % 2]
                    mw = [(wtP[0], w_tgt), (ld, ld.n), (ev, ev.n)]
                    for kc in range(NC_D):
                        pl.op("tensor", lambda t, kc=kc, jt=jt, buf=buf, pp=pp: t.matmul(
                            pp[:, :],
                            x2fm[:, kc * 1024 + jt * 128: kc * 1024 + (jt + 1) * 128],
                            qa[buf][:, kc * 512:(kc + 1) * 512],
                            start=(kc == 0), stop=(kc == 31)),
                            waits=mw if kc == 0 else (),
                            inc=(pe, 1) if kc == 31 else None)
                    mm_tgt = pe.n
                    pl.op("vector", lambda v, jt=jt, ob3=ob3, pp=pp: v.tensor_copy(
                        qlat_tm[:, jt * 1536 + ob3 * 512: jt * 1536 + (ob3 + 1) * 512],
                        pp[:, :]), waits=[(pe, mm_tgt), (ev, ev.n)], inc=(ev, 1))
            for jt in range(NT):
                qsl = qlat_tm[:, jt * 1536:(jt + 1) * 1536]
                vchain(lambda v, qsl=qsl: v.scalar_tensor_tensor(
                    c1dmp[:, :], qsl, 1.0, qsl, ALU.mult, ALU.mult,
                    accum_out=c1ss[:, :]))
                vchain(lambda v: v.tensor_scalar(
                    c1ss[:, :], c1ss[:, :], 1.0 / Q_RANK, None, ALU.mult, ALU.bypass))
                ss_tgt = ev.n
                pl.op("scalar", lambda s: s.activation(
                    c1rstd[:, :], c1ss[:, :], AF.Ln, bias=eps[:, :]),
                    waits=[(ev, ss_tgt), (ac, ac.n)], inc=(ac, 1))
                pl.op("scalar", lambda s: s.activation(
                    c1rstd[:, :], c1rstd[:, :], AF.Exp, scale=-0.5),
                    waits=[(ac, ac.n)], inc=(ac, 1))
                r_tgt = ac.n
                vchain(lambda v, qsl=qsl: v.tensor_scalar(
                    qlat_bf[:, :], qsl, c1rstd[:, :], None, ALU.mult, ALU.bypass),
                    extra=[(ac, r_tgt), (pe, pe.n)])
                cast_tgt = ev.n
                for c2 in range(12):
                    half = c2 % 2
                    pl.op("tensor", lambda t, c2=c2, half=half: t.matmul(
                        ps_t[:, (c2 % 4) * 128:(c2 % 4) * 128 + 128],
                        qlat_bf[:, c2 * 128:(c2 + 1) * 128], ident[:, :],
                        start=True, stop=True),
                        waits=[(ev, cast_tgt), (ev, ev.n)] if c2 % 4 == 0 else (),
                        inc=(pe, 1) if c2 % 4 == 3 else None)
                    if c2 % 4 == 3:
                        tr_tgt = pe.n
                        for q4 in range(4):
                            cc2 = c2 - 3 + q4
                            pl.op("vector", lambda v, cc2=cc2, q4=q4, jt=jt: v.tensor_scalar(
                                qlat_fm[:, cc2 * 1024 + jt * 128: cc2 * 1024 + (jt + 1) * 128],
                                ps_t[:, q4 * 128:(q4 + 1) * 128],
                                qnw[:, cc2:cc2 + 1], None, ALU.mult, ALU.bypass),
                                waits=[(pe, tr_tgt), (ld, qnw_tgt)] if q4 == 0 else (),
                                inc=(ev, 1) if q4 == 3 else None)
            qlat_done = ev.n

        pl.fence(sems)
        with ExitStack() as st2:
            sb2 = lambda n, shp, dt=BF16: st2.enter_context(nc.sbuf_tensor(_uname(n), shp, dt))
            ps2 = lambda n, shp: st2.enter_context(nc.psum_tensor(_uname(n), shp, F32))
            qb = [sb2(f"c_qb{i}", [128, 12 * 128]) for i in range(2)]
            qstg = [sb2(f"c_qstg{i}", [128, 1024]) for i in range(2)]
            r128 = sb2("c_r128", [128, 128])
            cqt = sb2("c_cqt", [128, 1024], F32)
            sqt = sb2("c_sqt", [128, 1024], F32)
            qr_f = sb2("c_qrf", [128, 1024], F32)
            qr_b = sb2("c_qrb", [128, 1024])
            t1b = sb2("c_t1", [128, 1024], F32)
            t2b = sb2("c_t2", [128, 1024], F32)
            ps_m = [ps2(f"c_psm{i}", [128, 1024]) for i in range(2)]
            ps_rot = ps2("c_psrot", [128, 1024])

            pl.op("gpsimd", lambda g: g.dma_start(out=r128[:, :], in_=io["R128"]),
                  inc=(wt, 16))
            pl.op("sync", lambda s: s.dma_start(out=cqt[:, :], in_=io["cq_t"]),
                  inc=(ld, 16))
            pl.op("sync", lambda s: s.dma_start(out=sqt[:, :], in_=io["sq_t"]),
                  inc=(ld, 16))
            tab_tgt = ld.n

            for kind in range(2):
                nm = 32 if kind == 0 else 16
                wsrc = "qb_n_t" if kind == 0 else "qb_r_t"
                for m in range(nm):
                    buf = m % 2
                    pl.op("gpsimd", lambda g, m=m, buf=buf, wsrc=wsrc: g.dma_start(
                        out=qb[buf][:, :], in_=row128(io[wsrc], m)),
                        waits=[(pe, pe.n)], inc=(wtP[buf], 16))
                    w_tgt = wtP[buf].n
                    pp = ps_m[buf]
                    mw = [(wtP[buf], w_tgt), (ev, qlat_done), (ac, ac.n), (ev, ev.n)]
                    for kc in range(12):
                        for h2 in range(2):
                            pl.op("tensor", lambda t, kc=kc, h2=h2, buf=buf, pp=pp: t.matmul(
                                pp[:, h2 * 512:(h2 + 1) * 512],
                                qb[buf][:, kc * 128:(kc + 1) * 128],
                                qlat_fm[:, kc * 1024 + h2 * 512: kc * 1024 + (h2 + 1) * 512],
                                start=(kc == 0), stop=(kc == 11)),
                                waits=mw if (kc == 0 and h2 == 0) else (),
                                inc=(pe, 1) if (kc == 11 and h2 == 1) else None)
                    mm_tgt = pe.n
                    if kind == 0:
                        dst = qstg[buf]
                        pl.op("vector", lambda v, pp=pp, dst=dst: v.tensor_scalar(
                            dst[:, :], pp[:, :], SC_QK, None, ALU.mult, ALU.bypass),
                            waits=[(pe, mm_tgt), (sto, sto.n), (ev, ev.n)], inc=(ev, 1))
                        e_tgt = ev.n
                        pl.op("sync", lambda s, m=m, dst=dst: s.dma_start(
                            out=row128(io["qn_dr"], m), in_=dst[:, :]),
                            waits=[(ev, e_tgt)], inc=(sto, 16))
                    else:
                        vchain(lambda v, pp=pp: v.tensor_scalar(
                            qr_f[:, :], pp[:, :], SC_QK, None, ALU.mult, ALU.bypass),
                            extra=[(pe, mm_tgt)])
                        vchain(lambda v: v.tensor_copy(qr_b[:, :], qr_f[:, :]))
                        qrb_tgt = ev.n
                        pl.op("tensor", lambda t: t.matmul(
                            ps_rot[:, 0:512], r128[:, :], qr_b[:, 0:512],
                            start=True, stop=True),
                            waits=[(ev, qrb_tgt), (wt, wt.n)])
                        pl.op("tensor", lambda t: t.matmul(
                            ps_rot[:, 512:1024], r128[:, :], qr_b[:, 512:1024],
                            start=True, stop=True), inc=(pe, 1))
                        rot_tgt = pe.n
                        vchain(lambda v: v.tensor_mul(
                            t2b[:, :], ps_rot[:, :], sqt[:, :]),
                            extra=[(pe, rot_tgt), (ld, tab_tgt)])
                        vchain(lambda v: v.tensor_mul(t1b[:, :], qr_f[:, :], cqt[:, :]))
                        dst = qstg[buf]
                        vchain(lambda v, dst=dst: v.tensor_add(
                            dst[:, :], t1b[:, :], t2b[:, :]),
                            extra=[(sto, sto.n)])
                        e_tgt = ev.n
                        pl.op("sync", lambda s, m=m, dst=dst: s.dma_start(
                            out=row128(io["qr_dr"], m), in_=dst[:, :]),
                            waits=[(ev, e_tgt)], inc=(sto, 16))

        st12.close()

        pl.fence(sems)
        with ExitStack() as st2:
            sb2 = lambda n, shp, dt=BF16: st2.enter_context(nc.sbuf_tensor(_uname(n), shp, dt))
            ps2 = lambda n, shp: st2.enter_context(nc.psum_tensor(_uname(n), shp, F32))
            kva = sb2("c_kva", [128, 32 * 576])
            wk = sb2("c_wk", [128, 32 * 128])
            ckv_f = sb2("c_ckvf", [128, 512], F32)
            kpe_f = sb2("c_kpef", [128, 64], F32)
            rot_f = sb2("c_rotf", [128, 64], F32)
            t1k = sb2("c_t1k", [128, 64], F32)
            ckt = sb2("c_ckt", [128, 64], F32)
            skt = sb2("c_skt", [128, 64], F32)
            cx_stg = [sb2(f"c_cxs{i}", [128, 704]) for i in range(2)]
            c3dmp = sb2("c_dmp2", [128, 512], F32)
            c3ss = sb2("c_ss2", [128, 1], F32)
            c3rstd = sb2("c_rstd2", [128, 1], F32)
            ps_c3 = [ps2(f"c_psc{i}", [128, 576]) for i in range(2)]
            ps_ik = ps2("c_psik", [128, 128])

            pl.op("gpsimd", lambda g: g.dma_start(out=kva[:, :], in_=io["kv_a_t"]),
                  inc=(wt, 16))
            pl.op("gpsimd", lambda g: g.dma_start(out=wk[:, :], in_=io["idx_wk_t"]),
                  inc=(wt, 16))
            kva_tgt = wt.n
            for jt in range(NT):
                buf = jt % 2
                pp = ps_c3[buf]
                pl.op("sync", lambda s, jt=jt: s.dma_start(
                    out=ckt[:, :], in_=row128(io["ck_t"], jt)),
                    waits=[(ev, ev.n)], inc=(ld, 16))
                pl.op("sync", lambda s, jt=jt: s.dma_start(
                    out=skt[:, :], in_=row128(io["sk_t"], jt)), inc=(ld, 16))
                tab_tgt = ld.n
                mw = [(wt, kva_tgt), (ld, ld.n), (ev, ev.n), (ac, ac.n)]
                for kc in range(NC_D):
                    for sl, off, w in ((0, 0, 512), (1, 512, 64)):
                        pl.op("tensor", lambda t, kc=kc, jt=jt, pp=pp, off=off, w=w: t.matmul(
                            pp[:, off:off + w],
                            x2fm[:, kc * 1024 + jt * 128: kc * 1024 + (jt + 1) * 128],
                            kva[:, kc * 576 + off: kc * 576 + off + w],
                            start=(kc == 0), stop=(kc == 31)),
                            waits=mw if (kc == 0 and sl == 0) else (),
                            inc=(pe, 1) if (kc == 31 and sl == 1) else None)
                for kc in range(NC_D):
                    pl.op("tensor", lambda t, kc=kc, jt=jt: t.matmul(
                        ps_ik[:, :],
                        x2fm[:, kc * 1024 + jt * 128: kc * 1024 + (jt + 1) * 128],
                        wk[:, kc * 128:(kc + 1) * 128],
                        start=(kc == 0), stop=(kc == 31)),
                        inc=(pe, 1) if kc == 31 else None)
                mm_tgt = pe.n
                vchain(lambda v, pp=pp: v.tensor_copy(ckv_f[:, :], pp[:, 0:512]),
                       extra=[(pe, mm_tgt), (sto, sto.n)])
                vchain(lambda v: v.scalar_tensor_tensor(
                    c3dmp[:, :], ckv_f[:, :], 1.0, ckv_f[:, :], ALU.mult, ALU.mult,
                    accum_out=c3ss[:, :]))
                vchain(lambda v: v.tensor_scalar(
                    c3ss[:, :], c3ss[:, :], 1.0 / KV_RANK, None, ALU.mult, ALU.bypass))
                ss_tgt = ev.n
                pl.op("scalar", lambda s: s.activation(
                    c3rstd[:, :], c3ss[:, :], AF.Ln, bias=eps[:, :]),
                    waits=[(ev, ss_tgt), (ac, ac.n)], inc=(ac, 1))
                pl.op("scalar", lambda s: s.activation(
                    c3rstd[:, :], c3rstd[:, :], AF.Exp, scale=-0.5),
                    waits=[(ac, ac.n)], inc=(ac, 1))
                r_tgt = ac.n
                cxs = cx_stg[buf]
                vchain(lambda v, cxs=cxs: v.tensor_scalar(
                    cxs[:, 0:512], ckv_f[:, :], c3rstd[:, :], None, ALU.mult, ALU.bypass),
                    extra=[(ac, r_tgt), (sto, sto.n)])
                vchain(lambda v, pp=pp: v.tensor_copy(kpe_f[:, :], pp[:, 512:576]))
                vchain(lambda v: v.tensor_scalar(
                    rot_f[:, :].rearrange("p (a b) -> p a b", b=2)[:, :, 0],
                    kpe_f[:, :].rearrange("p (a b) -> p a b", b=2)[:, :, 1],
                    -1.0, None, ALU.mult, ALU.bypass))
                vchain(lambda v: v.tensor_copy(
                    rot_f[:, :].rearrange("p (a b) -> p a b", b=2)[:, :, 1],
                    kpe_f[:, :].rearrange("p (a b) -> p a b", b=2)[:, :, 0]))
                vchain(lambda v: v.tensor_mul(t1k[:, :], kpe_f[:, :], ckt[:, :]),
                       extra=[(ld, tab_tgt)])
                vchain(lambda v: v.tensor_mul(rot_f[:, :], rot_f[:, :], skt[:, :]))
                vchain(lambda v, cxs=cxs: v.tensor_add(
                    cxs[:, 512:576], t1k[:, :], rot_f[:, :]))
                vchain(lambda v, cxs=cxs, pp=pp: v.tensor_copy(
                    cxs[:, 576:704], ps_ik[:, :]), extra=[(pe, mm_tgt)])
                e_tgt = ev.n
                pl.op("sync", lambda s, jt=jt, cxs=cxs: s.dma_start(
                    out=row128(io["cx_src"], jt), in_=cxs[:, :]),
                    waits=[(ev, e_tgt)], inc=(sto, 16))
            pl.op("gpsimd", lambda g: g.collective_compute(
                "AllGather", ALU.bypass, replica_groups=REPLICA_PAIRS,
                ins=[io["cx_src"].opt()], outs=[io["cx_dst"].opt()]),
                waits=[(sto, sto.n), (ld, ld.n)], inc=(cc, 1))
            cx_done = cc.n
            pl.wait("sync", cc, cx_done)
            pl.wait("scalar", cc, cx_done)

        pl.fence(sems)
        with ExitStack() as st2:
            sb2 = lambda n, shp, dt=BF16: st2.enter_context(nc.sbuf_tensor(_uname(n), shp, dt))
            ps2 = lambda n, shp: st2.enter_context(nc.psum_tensor(_uname(n), shp, F32))
            cx_t = [sb2(f"c_cxt{i}", [128, 704]) for i in range(2)]
            kvlat_fm = sb2("c_kvlat", [128, 4 * 2048])
            kvnw = sb2("c_kvnw", [128, 4], F32)
            kbn = [sb2(f"c_kbn{i}", [128, 4 * 128]) for i in range(2)]
            kbv = sb2("c_kbv", [128, 4 * 4096])
            knstg = [sb2(f"c_knst{i}", [128, 2048]) for i in range(2)]
            vstg = [sb2(f"c_vst{i}", [128, 512]) for i in range(2)]
            ps_t4 = ps2("c_pst4", [128, 512])
            ps_kn = ps2("c_pskn", [128, 2048])
            ps_v = [ps2(f"c_psv{i}", [128, 512]) for i in range(2)]

            pl.op("sync", lambda s: s.dma_start(out=kvnw[:, :], in_=io["kvnw"]),
                  waits=[(cc, cx_done)], inc=(ld, 16))
            kvnw_tgt = ld.n
            for g16 in range(16):
                r, j = g16 % 2, g16 // 2
                buf = g16 % 2
                pl.op("sync", lambda s, r=r, j=j, buf=buf: s.dma_start(
                    out=cx_t[buf][:, :], in_=row128(io["cx_dst"], r * 8 + j)),
                    waits=[(cc, cx_done), (pe, pe.n)], inc=(lh[buf], 16))
                c_tgt = lh[buf].n
                for c4 in range(4):
                    pl.op("tensor", lambda t, c4=c4, buf=buf: t.matmul(
                        ps_t4[:, (c4 % 4) * 128:(c4 % 4) * 128 + 128],
                        cx_t[buf][:, c4 * 128:(c4 + 1) * 128], ident[:, :],
                        start=True, stop=True),
                        waits=[(lh[buf], c_tgt), (ev, ev.n), (ac, ac.n)] if c4 == 0 else (),
                        inc=(pe, 1) if c4 == 3 else None)
                tr_tgt = pe.n
                for c4 in range(4):
                    pl.op("vector", lambda v, c4=c4, g16=g16: v.tensor_scalar(
                        kvlat_fm[:, c4 * 2048 + g16 * 128: c4 * 2048 + (g16 + 1) * 128],
                        ps_t4[:, c4 * 128:(c4 + 1) * 128],
                        kvnw[:, c4:c4 + 1], None, ALU.mult, ALU.bypass),
                        waits=[(pe, tr_tgt), (ld, kvnw_tgt), (ev, ev.n)] if c4 == 0 else (),
                        inc=(ev, 1) if c4 == 3 else None)
                kv_ev = ev.n
                pl.op("tensor", lambda t, buf=buf: t.matmul(
                    ps_t4[0:64, 0:128], cx_t[buf][:, 512:576], ident[:, :],
                    start=True, stop=True), waits=[(ev, kv_ev)])
                pl.op("tensor", lambda t, buf=buf: t.matmul(
                    ps_t4[:, 128:256], cx_t[buf][:, 576:704], ident[:, :],
                    start=True, stop=True), inc=(pe, 1))
                tr2_tgt = pe.n
                pl.op("scalar", lambda s, g16=g16: s.activation(
                    kpe_fm[:, g16 * 128:(g16 + 1) * 128], ps_t4[0:64, 0:128],
                    AF.Copy), waits=[(pe, tr2_tgt), (ac, ac.n)], inc=(ac, 1))
                pl.op("scalar", lambda s, g16=g16: s.activation(
                    ik_fm[:, g16 * 128:(g16 + 1) * 128], ps_t4[:, 128:256],
                    AF.Copy), waits=[(ac, ac.n)], inc=(ac, 1))
            fm_done = (ev.n, ac.n)
            for m in range(H_L1):
                buf = m % 2
                pl.op("gpsimd", lambda g, m=m, buf=buf: g.dma_start(
                    out=kbn[buf][:, :], in_=row128(io["kv_b_n_t"], m)),
                    waits=[(pe, pe.n)], inc=(wtP[buf], 16))
                w_tgt = wtP[buf].n
                mw = [(wtP[buf], w_tgt), (ev, fm_done[0]), (ac, ac.n), (ev, ev.n)]
                for kc in range(4):
                    for n5 in range(4):
                        pl.op("tensor", lambda t, kc=kc, n5=n5, buf=buf: t.matmul(
                            ps_kn[:, n5 * 512:(n5 + 1) * 512],
                            kbn[buf][:, kc * 128:(kc + 1) * 128],
                            kvlat_fm[:, kc * 2048 + n5 * 512: kc * 2048 + (n5 + 1) * 512],
                            start=(kc == 0), stop=(kc == 3)),
                            waits=mw if (kc == 0 and n5 == 0) else (),
                            inc=(pe, 1) if (kc == 3 and n5 == 3) else None)
                mm_tgt = pe.n
                dst = knstg[buf]
                pl.op("vector", lambda v, dst=dst: v.tensor_copy(
                    dst[:, :], ps_kn[:, :]),
                    waits=[(pe, mm_tgt), (sto, sto.n), (ev, ev.n)], inc=(ev, 1))
                e_tgt = ev.n
                pl.op("sync", lambda s, m=m, dst=dst: s.dma_start(
                    out=row128(io["kn_dr"], m), in_=dst[:, :]),
                    waits=[(ev, e_tgt)], inc=(sto, 16))
            pl.op("gpsimd", lambda g: g.dma_start(out=kbv[:, :], in_=io["kv_b_v_t"]),
                  inc=(wt, 16))
            kbv_tgt = wt.n
            for kt in range(16):
                for obv in range(8):
                    buf = (kt * 8 + obv) % 2
                    pp = ps_v[buf]
                    mw = [(wt, kbv_tgt), (ev, fm_done[0]), (ev, ev.n)]
                    for kc in range(4):
                        pl.op("tensor", lambda t, kc=kc, kt=kt, obv=obv, pp=pp: t.matmul(
                            pp[:, :],
                            kvlat_fm[:, kc * 2048 + kt * 128: kc * 2048 + (kt + 1) * 128],
                            kbv[:, kc * 4096 + obv * 512: kc * 4096 + (obv + 1) * 512],
                            start=(kc == 0), stop=(kc == 3)),
                            waits=mw if kc == 0 else (),
                            inc=(pe, 1) if kc == 3 else None)
                    mm_tgt = pe.n
                    dst = vstg[buf]
                    pl.op("vector", lambda v, dst=dst, pp=pp: v.tensor_copy(
                        dst[:, :], pp[:, :]),
                        waits=[(pe, mm_tgt), (sto, sto.n), (ev, ev.n)], inc=(ev, 1))
                    e_tgt = ev.n
                    pl.op("sync", lambda s, kt=kt, obv=obv, dst=dst: s.dma_start(
                        out=row128(io["v_dr"], kt)[:, obv * 512:(obv + 1) * 512],
                        in_=dst[:, :]), waits=[(ev, e_tgt)], inc=(sto, 16))

        pl.fence(sems)
        m01 = st.enter_context(
            nc.sbuf_tensor("c_m01", [128, NT * 2048], BF16, side="right"))
        with ExitStack() as st2:
            sb2 = lambda n, shp, dt=BF16: st2.enter_context(nc.sbuf_tensor(_uname(n), shp, dt))
            ps2 = lambda n, shp: st2.enter_context(nc.psum_tensor(_uname(n), shp, F32))
            iq_fm = sb2("c_iqfm", [128, 16 * 1024])
            iw_sb = sb2("c_iwsb", [128, NT * 16], F32)
            wwt = sb2("c_wwt", [128, 32 * 16])
            wq = [sb2(f"c_wq{i}", [128, 32 * 128]) for i in range(2)]
            cidx = sb2("c_cidx", [128, 2048], F32)
            rn_sb = [sb2("c_rn", [128, 2048]) for _ in range(2)]
            dmp5 = sb2("c_dmp5", [128, 2048], F32)
            cmask_sb = sb2("c_cm", [128, 256], F32)
            lo = sb2("c_lo", [128, 1], F32)
            hi = sb2("c_hi", [128, 1], F32)
            mid = sb2("c_mid", [128, 1], F32)
            cnt = sb2("c_cnt", [128, 1], F32)
            cond = sb2("c_cond", [128, 1], F32)
            tda = sb2("c_tda", [128, 1], F32)
            ps_i = [ps2(f"c_psi{i}", [128, 2048]) for i in range(2)]

            pl.op("gpsimd", lambda g: g.dma_start(out=wwt[:, :], in_=io["idx_ww_t"]),
                  inc=(wt, 16))
            ww_tgt = wt.n
            for m in range(16):
                buf = m % 2
                pl.op("gpsimd", lambda g, m=m, buf=buf: g.dma_start(
                    out=wq[buf][:, :], in_=row128(io["idx_wq_t"], m)),
                    waits=[(pe, pe.n)], inc=(wtP[buf], 16))
                w_tgt = wtP[buf].n
                pp = ps_i[buf]
                mw = [(wtP[buf], w_tgt), (ld, ld.n), (ac, ac.n), (ev, ev.n)]
                for kc in range(NC_D):
                    for h2 in range(2):
                        pl.op("tensor", lambda t, kc=kc, h2=h2, buf=buf, pp=pp: t.matmul(
                            pp[:, h2 * 512:(h2 + 1) * 512],
                            wq[buf][:, kc * 128:(kc + 1) * 128],
                            x2fm[:, kc * 1024 + h2 * 512: kc * 1024 + (h2 + 1) * 512],
                            start=(kc == 0), stop=(kc == 31)),
                            waits=mw if (kc == 0 and h2 == 0) else (),
                            inc=(pe, 1) if (kc == 31 and h2 == 1) else None)
                mm_tgt = pe.n
                pl.op("scalar", lambda s, m=m, pp=pp: s.activation(
                    iq_fm[:, m * 1024:(m + 1) * 1024], pp[:, 0:1024], AF.Copy),
                    waits=[(pe, mm_tgt), (ac, ac.n)], inc=(ac, 1))
            for jt in range(NT):
                pp = ps_i[jt % 2]
                mw = [(wt, ww_tgt), (ac, ac.n), (ev, ev.n)]
                for kc in range(NC_D):
                    pl.op("tensor", lambda t, kc=kc, jt=jt, pp=pp: t.matmul(
                        pp[:, 0:16],
                        x2fm[:, kc * 1024 + jt * 128: kc * 1024 + (jt + 1) * 128],
                        wwt[:, kc * 16:(kc + 1) * 16],
                        start=(kc == 0), stop=(kc == 31)),
                        waits=mw if kc == 0 else (),
                        inc=(pe, 1) if kc == 31 else None)
                mm_tgt = pe.n
                pl.op("vector", lambda v, jt=jt, pp=pp: v.tensor_scalar(
                    iw_sb[:, jt * 16:(jt + 1) * 16], pp[:, 0:16], SC_IDX, None,
                    ALU.mult, ALU.bypass),
                    waits=[(pe, mm_tgt), (ev, ev.n)], inc=(ev, 1))
            iw_ev = ev.n
            relu_prev = {0: ac.n, 1: ac.n}
            if True:
                pass
            iq_done = (ac.n, ev.n)
            for jq in range(NT):
                ncau = 256 * (jq + 1)
                nch = (ncau + 511) // 512
                pl.op("sync", lambda s, jq=jq: s.dma_start(
                    out=cmask_sb[:, :], in_=row128(io["cmask"], jq)),
                    waits=[(ev, ev.n)], inc=(ld, 16))
                cm_tgt = ld.n
                for n in range(16):
                    buf = n % 2
                    pp = ps_i[buf]
                    mw = [(ac, relu_prev[buf]), (ev, iw_ev)] if (
                        jq == 0 and n < 2) else [(ac, relu_prev[buf])]
                    for n5 in range(nch):
                        w5 = min(512, ncau - n5 * 512)
                        pl.op("tensor", lambda t, n=n, jq=jq, n5=n5, w5=w5, pp=pp: t.matmul(
                            pp[:, n5 * 512:n5 * 512 + w5],
                            iq_fm[:, n * 1024 + jq * 128: n * 1024 + (jq + 1) * 128],
                            ik_fm[:, n5 * 512:n5 * 512 + w5],
                            start=True, stop=True),
                            waits=mw if n5 == 0 else (),
                            inc=(pe, 1) if n5 == nch - 1 else None)
                    mm_tgt = pe.n
                    pl.op("scalar", lambda s, ncau=ncau, pp=pp, buf=buf: s.activation(
                        rn_sb[buf][:, 0:ncau], pp[:, 0:ncau], AF.Relu),
                        waits=[(pe, mm_tgt), (ev, ev.n)], inc=(ac, 1))
                    relu_prev[buf] = ac.n
                    r_tgt = ac.n
                    if n == 0:
                        vchain(lambda v, jq=jq, ncau=ncau, buf=buf: v.tensor_scalar(
                            cidx[:, 0:ncau], rn_sb[buf][:, 0:ncau],
                            iw_sb[:, jq * 16:jq * 16 + 1], None, ALU.mult, ALU.bypass),
                            extra=[(ac, r_tgt)])
                    else:
                        vchain(lambda v, jq=jq, n=n, ncau=ncau, buf=buf: v.scalar_tensor_tensor(
                            cidx[:, 0:ncau], rn_sb[buf][:, 0:ncau],
                            iw_sb[:, jq * 16 + n:jq * 16 + n + 1],
                            cidx[:, 0:ncau], ALU.mult, ALU.add),
                            extra=[(ac, r_tgt)])
                vchain(lambda v, ncau=ncau: v.tensor_reduce(
                    hi[:, :], cidx[:, 0:ncau], AX.X, ALU.max))
                vchain(lambda v, ncau=ncau: v.tensor_reduce(
                    lo[:, :], cidx[:, 0:ncau], AX.X, ALU.min))
                vchain(lambda v: v.tensor_scalar(
                    hi[:, :], hi[:, :], 1.0, None, ALU.add, ALU.bypass))
                vchain(lambda v: v.tensor_scalar(
                    lo[:, :], lo[:, :], 1.0, None, ALU.subtract, ALU.bypass))
                if ncau < 2048:
                    vchain(lambda v, ncau=ncau: v.memset(cidx[:, ncau:2048], NEG))
                vchain(lambda v, ncau=ncau: v.tensor_add(
                    cidx[:, ncau - 256:ncau], cidx[:, ncau - 256:ncau],
                    cmask_sb[:, :]), extra=[(ld, cm_tgt)])
                for itb in range(26):
                    vchain(lambda v: v.tensor_add(mid[:, :], lo[:, :], hi[:, :]))
                    vchain(lambda v: v.tensor_scalar(
                        mid[:, :], mid[:, :], 0.5, None, ALU.mult, ALU.bypass))
                    vchain(lambda v: v.tensor_scalar(
                        dmp5[:, :], cidx[:, :], mid[:, :], None, ALU.is_ge,
                        ALU.add, accum_out=cnt[:, :]))
                    vchain(lambda v: v.tensor_scalar(
                        cond[:, :], cnt[:, :], float(TOPK), None, ALU.is_ge,
                        ALU.bypass))
                    vchain(lambda v: v.tensor_sub(tda[:, :], mid[:, :], lo[:, :]))
                    vchain(lambda v: v.scalar_tensor_tensor(
                        lo[:, :], tda[:, :], cond[:, :], lo[:, :], ALU.mult, ALU.add))
                    vchain(lambda v: v.tensor_sub(tda[:, :], hi[:, :], mid[:, :]))
                    vchain(lambda v: v.scalar_tensor_tensor(
                        hi[:, :], tda[:, :], cond[:, :], mid[:, :], ALU.mult, ALU.add))
                vchain(lambda v, jq=jq: v.tensor_scalar(
                    m01[:, jq * 2048:(jq + 1) * 2048], cidx[:, :], lo[:, :], None,
                    ALU.is_ge, ALU.bypass))
                if "dbg_m01" in io:
                    e_tgt = ev.n
                    pl.op("sync", lambda s, jq=jq: s.dma_start(
                        out=row128(io["dbg_m01"], jq),
                        in_=m01[:, jq * 2048:(jq + 1) * 2048]),
                        waits=[(ev, e_tgt)], inc=(sto, 16))
                    pl.op("sync", lambda s, jq=jq: s.dma_start(
                        out=row128(io["dbg_cidx"], jq), in_=cidx[:, :]),
                        waits=[(ev, e_tgt)], inc=(sto, 16))
                    vchain(lambda v: v.engine_nop(), extra=[(sto, sto.n)])
            m01_done = ev.n

    pl.fence(sems)
    ctx = st.enter_context(
        nc.sbuf_tensor("c_ctx", [128, 32 * 1024], BF16, side="right"))
    with ExitStack() as st2:
        sb2 = lambda n, shp, dt=BF16: st2.enter_context(nc.sbuf_tensor(_uname(n), shp, dt))
        ps2 = lambda n, shp: st2.enter_context(nc.psum_tensor(_uname(n), shp, F32))
        knh = [sb2(f"c_knh{i}", [128, 2048]) for i in range(2)]
        vhb = [sb2(f"c_vh{i}", [128, 2048]) for i in range(2)]
        qnh = [sb2(f"c_qnh{i}", [128, 1024]) for i in range(2)]
        qrh = [sb2(f"c_qrh{i}", [64, 1024]) for i in range(2)]
        p_raw = [sb2("c_praw", [128, 2048]) for _ in range(2)]
        p_m = [sb2("c_pm", [128, 2048]) for _ in range(2)]
        p_sb = [sb2("c_psb", [128, 2048]) for _ in range(2)]
        s_sb6 = [sb2("c_ssb6", [128, 2048], F32) for _ in range(2)]
        pt_h = sb2("c_pt", [128, 16 * 1024])
        cm6 = sb2("c_cm6", [128, NT * 256], F32)
        nmax = [sb2("c_nmax", [128, 1], F32) for _ in range(2)]
        pmax6 = [sb2("c_pmax6", [128, 1], F32) for _ in range(2)]
        sums = [sb2("c_sums", [128, 1], F32) for _ in range(2)]
        recip = [sb2("c_recip", [128, 1], F32) for _ in range(2)]
        ps_cx = ps2("c_pscx", [128, 512])
        ps_s = ps2("c_pss", [128, 2048])
        ps_pt = ps2("c_pspt", [128, 1024])

        pl.op("sync", lambda s: s.dma_start(
            out=cm6[:, :].rearrange("p (j f) -> p j f", j=NT),
            in_=io["cmask"].rearrange("j (p f) -> p j f", p=128)),
            waits=[(ev, m01_done)], inc=(ld, 16))
        cm6_tgt = ld.n

        def load_head6(h, slot):
            g = [(pe, pe.n)]
            pl.op("sync", lambda s, h=h, slot=slot: s.dma_start(
                out=knh[slot][:, :], in_=row128(io["kn_dr"], h)),
                waits=g, inc=(lh[slot], 16))
            pl.op("sync", lambda s, h=h, slot=slot: s.dma_start(
                out=qnh[slot][:, :], in_=row128(io["qn_dr"], h)),
                inc=(lh[slot], 16))
            pl.op("sync", lambda s, h=h, slot=slot: s.dma_start(
                out=qrh[slot][:, :],
                in_=row128(io["qr_dr"], h // 2)[(h % 2) * 64:(h % 2) * 64 + 64, :]),
                inc=(lh[slot], 16))
            src = io["v_dr"].rearrange("k (t f) -> t k f", t=128)[
                :, :, h * 128:(h + 1) * 128]
            pl.op("scalar", lambda s, src=src, slot=slot: s.dma_start(
                out=vhb[slot][:, :].rearrange("t (k d) -> t k d", k=16), in_=src),
                waits=g, inc=(lv[slot], 16))
            return lh[slot].n, lv[slot].n

        mxev6 = [ev.n]
        ptv6_prev = [ac.n]
        exp6 = {0: ac.n, 1: ac.n}
        tgts = [None, None]
        tgts[0] = load_head6(0, 0)
        for h in range(H_L1):
            slot = h % 2
            ld_t, lv_t = tgts[slot]
            if h + 1 < H_L1:
                tgts[(h + 1) % 2] = load_head6(h + 1, (h + 1) % 2)
            for jq in range(NT):
                ncau = 256 * (jq + 1)
                nch = (ncau + 511) // 512
                w0 = [(lh[slot], ld_t), (lv[slot], lv_t), (ld, cm6_tgt),
                      (ev, mxev6[0])]
                for n5 in range(nch):
                    w5 = min(512, ncau - n5 * 512)
                    pl.op("tensor", lambda t, jq=jq, n5=n5, w5=w5, slot=slot: t.matmul(
                        ps_s[:, n5 * 512:n5 * 512 + w5],
                        qnh[slot][:, jq * 128:(jq + 1) * 128],
                        knh[slot][:, n5 * 512:n5 * 512 + w5],
                        start=True, stop=False),
                        waits=w0 if n5 == 0 else ())
                    pl.op("tensor", lambda t, jq=jq, n5=n5, w5=w5, slot=slot: t.matmul(
                        ps_s[:, n5 * 512:n5 * 512 + w5],
                        qrh[slot][:, jq * 128:(jq + 1) * 128],
                        kpe_fm[:, n5 * 512:n5 * 512 + w5],
                        start=False, stop=True),
                        inc=(pe, 1) if n5 == nch - 1 else None)
                s_tgt = pe.n
                vchain(lambda v, jq=jq, ncau=ncau: v.tensor_add(
                    ps_s[:, ncau - 256:ncau], ps_s[:, ncau - 256:ncau],
                    cm6[:, jq * 256:(jq + 1) * 256]), extra=[(pe, s_tgt)])
                jb = jq % 2
                vchain(lambda v, ncau=ncau, jb=jb: v.tensor_scalar(
                    s_sb6[jb][:, 0:ncau], ps_s[:, 0:ncau], 1.0, None,
                    ALU.mult, ALU.max, accum_out=pmax6[jb][:, :]),
                    extra=[(ac, exp6[jb])])
                mxev6[0] = ev.n
                vchain(lambda v, jb=jb: v.tensor_scalar(
                    nmax[jb][:, :], pmax6[jb][:, :], -1.0, None,
                    ALU.mult, ALU.bypass))
                mx_tgt = ev.n
                pl.op("scalar", lambda s, ncau=ncau, jb=jb: s.activation(
                    p_raw[jb][:, 0:ncau], s_sb6[jb][:, 0:ncau], AF.Exp,
                    bias=nmax[jb][:, :]),
                    waits=[(ev, mx_tgt)], inc=(ac, 1))
                exp6[jb] = ac.n
                exp_tgt = ac.n
                vchain(lambda v, jq=jq, ncau=ncau, jb=jb: v.scalar_tensor_tensor(
                    p_m[jb][:, 0:ncau], m01[:, jq * 2048:jq * 2048 + ncau], 1.0,
                    p_raw[jb][:, 0:ncau], ALU.mult, ALU.mult,
                    accum_out=sums[jb][:, :]), extra=[(ac, exp_tgt)])
                vchain(lambda v, jb=jb: v.reciprocal(recip[jb][:, :], sums[jb][:, :]))
                vchain(lambda v, ncau=ncau, jb=jb: v.tensor_scalar(
                    p_sb[jb][:, 0:ncau], p_m[jb][:, 0:ncau], recip[jb][:, :], None,
                    ALU.mult, ALU.bypass), extra=[(pe, pe.n)])
                nrm_tgt = ev.n
                nkt = 2 * (jq + 1)
                for rnd in range((nkt + 7) // 8):
                    k0 = rnd * 8
                    kn8 = min(8, nkt - k0)
                    tw = [(ev, nrm_tgt), (ac, ptv6_prev[0])]
                    for q4 in range(kn8):
                        kt = k0 + q4
                        pl.op("tensor", lambda t, kt=kt, q4=q4, jb=jb: t.matmul(
                            ps_pt[:, q4 * 128:(q4 + 1) * 128],
                            p_sb[jb][:, kt * 128:(kt + 1) * 128], ident[:, :],
                            start=True, stop=True),
                            waits=tw if q4 == 0 else (),
                            inc=(pe, 1) if q4 == kn8 - 1 else None)
                    tr_tgt = pe.n
                    dst = pt_h[:, :].rearrange("p (kc q) -> p kc q", kc=16)[
                        :, k0:k0 + kn8, jq * 128:(jq + 1) * 128]
                    pl.op("scalar", lambda s, dst=dst, kn8=kn8: s.activation(
                        dst, ps_pt[:, 0:kn8 * 128].rearrange(
                            "p (a b) -> p a b", a=kn8), AF.Copy),
                        waits=[(pe, tr_tgt)], inc=(ac, 1))
                    ptv6_prev[0] = ac.n
                pt_done = ac.n
                cw = [(ac, pt_done), (ev, ev.n)]
                for kt in range(nkt):
                    pl.op("tensor", lambda t, kt=kt, jq=jq, slot=slot: t.matmul(
                        ps_cx[:, 0:128],
                        vhb[slot][:, kt * 128:(kt + 1) * 128],
                        pt_h[:, kt * 1024 + jq * 128: kt * 1024 + (jq + 1) * 128],
                        start=(kt == 0), stop=(kt == nkt - 1)),
                        waits=cw if kt == 0 else (),
                        inc=(pe, 1) if kt == nkt - 1 else None)
                cx_tgt = pe.n
                pl.op("vector", lambda v, h=h, jq=jq: v.tensor_copy(
                    ctx[:, h * 1024 + jq * 128: h * 1024 + (jq + 1) * 128],
                    ps_cx[:, 0:128]).annotate(f"ctxevac_h{h}_jq{jq}"),
                    waits=[(pe, cx_tgt), (ev, ev.n)], inc=(ev, 1))
        ctx_done = ev.n
    return ctx, ctx_done



def host_prep_D(inputs):
    W0 = np.asarray(inputs["L1_ffn_W0"], np.float32)
    b0 = np.asarray(inputs["L1_ffn_b0"], np.float32)
    A = np.asarray(inputs["L1_ffn_A"], np.float32)
    B = np.asarray(inputs["L1_ffn_B"], np.float32)
    d = {}
    d["W0T"] = _tile_rhs(np.ascontiguousarray(W0.T), 32, 8)
    d["b0_t"] = np.ascontiguousarray(np.tile(b0[None, :], (128, 1)))
    d["a_lt"] = np.ascontiguousarray(
        A.reshape(4, 16, 32, 128).transpose(3, 0, 2, 1).reshape(128, 4 * 32 * 16))
    d["BlT"] = np.ascontiguousarray(
        np.concatenate([B[l].T for l in range(4)], axis=1))
    return d


_M8_CACHE = {}


def host_prep_D_core(c):
    if "m8" not in _M8_CACHE:
        import jax
        with jax.default_device(jax.devices("cpu")[0]):
            k42 = jax.random.key(42, impl="threefry2x32")
            u = np.asarray(jax.random.uniform(k42, (4, 2048, 4096)))
        m8 = np.zeros((4, 4, 2048, 4096), np.float32)
        for l in range(4):
            lo_b, hi_b = 0.25 * l, 0.25 * (l + 1)
            m8[l] = ((u >= lo_b) & (u < hi_b)) * 8.0
        _M8_CACHE["m8"] = m8.astype(BF)
    m8 = _M8_CACHE["m8"]
    b, p = c // 2, c % 2
    tiles = [2 * j + p for j in range(NT)]
    idx = np.concatenate([np.arange(g * 128, g * 128 + 128) for g in tiles])
    sl = m8[:, b, idx, :].transpose(1, 0, 2).reshape(NT, 128, 4 * 4096)
    return {"m8": np.ascontiguousarray(sl.reshape(NT, 128 * 4 * 4096))}


def declare_dram_D(nc, io, debug=False):
    ei = lambda n, shp, dt=F32: io.__setitem__(
        n, nc.dram_tensor(n, shp, dt, kind="ExternalInput").ap())
    ei("W0T", [8, 128 * 32 * 512])
    ei("b0_t", [128, D])
    ei("a_lt", [128, 4 * 32 * 16])
    ei("BlT", [16, 4 * 4096])
    ei("m8", [NT, 128 * 4 * 4096], BF16)
    io["s1d"] = nc.dram_tensor("s1d", [NT, 128 * D], F32, kind="Internal").ap()
    if debug:
        io["_mkdbg"]("s1d", [NT, 128 * D], F32)


def phase_D(nc, pl, st, io, sems, glob, x_fm_key="x3_fm", out_key="s1d",
            fm_ready=0):
    ld, wt, pe, ev, ac, sto = (sems["ld"], sems["wt"], sems["pe"], sems["ev"],
                               sems["ac"], sems["sto"])
    lh = [sems["lh0"], sems["lh1"]]
    wtP = [sems["wtA"], sems["wtB"]]
    sb = lambda n, shp, dt=BF16: st.enter_context(nc.sbuf_tensor(_uname(n), shp, dt))
    psa = lambda n, shp: st.enter_context(nc.psum_tensor(_uname(n), shp, F32))

    pl.fence(sems)
    x3fm = sb("d_x3fm", [128, NC_D * 1024])
    hl = sb("d_hl", [16, 4 * 1024])
    blt = sb("d_blt", [16, 4 * 4096])
    alt = sb("d_alt", [128, 4 * 32 * 16])
    b0t = sb("d_b0t", [128, D], F32)
    w0 = [sb(f"d_w0_{i}", [128, 32 * 512]) for i in range(2)]
    msl = [sb(f"d_m{i}", [128, 4 * 512]) for i in range(2)]
    acc = [sb(f"d_acc{i}", [128, 512], F32) for i in range(4)]
    tl = sb("d_tl", [128, 512], F32)
    ps_o = None
    ps_y = None

    pl.op("sync", lambda s: s.dma_start(
        out=x3fm[:, :].rearrange("p (c f) -> p c f", c=NC_D),
        in_=io[x_fm_key].rearrange("c (p f) -> p c f", p=128)),
        waits=[(sems["st"], fm_ready)], inc=(ld, 16))
    pl.op("sync", lambda s: s.dma_start(out=b0t[:, :], in_=io["b0_t"]),
          inc=(ld, 16))
    pl.op("gpsimd", lambda g: g.dma_start(out=blt[:, :], in_=io["BlT"]),
          inc=(wt, 16))
    pl.op("gpsimd", lambda g: g.dma_start(out=alt[:, :], in_=io["a_lt"]),
          inc=(wt, 16))
    x_tgt = ld.n
    sm_tgt = wt.n

    with ExitStack() as sth:
        ps_h = sth.enter_context(nc.psum_tensor("d_psh", [16, 4096], F32))
        lw = [(ld, x_tgt), (wt, sm_tgt), (ev, ev.n), (ac, ac.n)]
        for l in range(4):
            for kc in range(NC_D):
                for h2 in range(2):
                    pl.op("tensor", lambda t, l=l, kc=kc, h2=h2: t.matmul(
                        ps_h[:, l * 1024 + h2 * 512: l * 1024 + (h2 + 1) * 512],
                        alt[:, (l * 32 + kc) * 16: (l * 32 + kc + 1) * 16],
                        x3fm[:, kc * 1024 + h2 * 512: kc * 1024 + h2 * 512 + 512],
                        start=(kc == 0), stop=(kc == 31)),
                        waits=lw if (l == 0 and kc == 0 and h2 == 0) else (),
                        inc=(pe, 1) if (kc == 31 and h2 == 1) else None)
        h_tgt = pe.n
        pl.op("vector", lambda v: v.tensor_copy(hl[:, :], ps_h[:, :]),
              waits=[(pe, h_tgt), (ev, ev.n)], inc=(ev, 1))
        hl_done = ev.n
    ps_o = [psa(f"d_pso{i}", [128, 512]) for i in range(2)]
    ps_y = [psa(f"d_psy{i}", [128, 512]) for i in range(2)]

    w_tgt = [0, 0]
    for obi in range(8):
        buf = obi % 2
        pl.op("gpsimd", lambda g, obi=obi, buf=buf: g.dma_start(
            out=w0[buf][:, :], in_=row128(io["W0T"], obi)),
            waits=[(pe, pe.n)], inc=(wtP[buf], 16))
        w_tgt[buf] = wtP[buf].n
        for jt in range(NT):
            mbuf = jt % 2
            pl.op("sync", lambda s, jt=jt, obi=obi, mbuf=mbuf: s.dma_start(
                out=msl[mbuf][:, :].rearrange("p (l f) -> p l f", l=4),
                in_=row128(io["m8"], jt).rearrange(
                    "p (l f) -> p l f", l=4)[:, :, obi * 512:(obi + 1) * 512]),
                waits=[(ev, ev.n)], inc=(lh[mbuf], 16))
            m_tgt = lh[mbuf].n
            pp = ps_o[buf]
            mw = [(wtP[buf], w_tgt[buf]), (ev, hl_done), (ev, ev.n)]
            for kc in range(NC_D):
                pl.op("tensor", lambda t, jt=jt, kc=kc, pp=pp, buf=buf: t.matmul(
                    pp[:, :],
                    x3fm[:, kc * 1024 + jt * 128: kc * 1024 + (jt + 1) * 128],
                    w0[buf][:, kc * 512:(kc + 1) * 512],
                    start=(kc == 0), stop=(kc == 31)),
                    waits=mw if kc == 0 else (),
                    inc=(pe, 1) if kc == 31 else None)
            mm_tgt = pe.n
            av = acc[(obi * NT + jt) % 4]
            pl.op("vector", lambda v, pp=pp, av=av, obi=obi: v.scalar_tensor_tensor(
                av[:, :], pp[:, :], 1.0, b0t[:, obi * 512:(obi + 1) * 512],
                ALU.mult, ALU.add),
                waits=[(pe, mm_tgt), (sto, sto.n), (ev, ev.n)], inc=(ev, 1))
            for l in range(4):
                ybuf = l % 2
                yw = [(ev, ev.n)]
                pl.op("tensor", lambda t, l=l, jt=jt, obi=obi, ybuf=ybuf: t.matmul(
                    ps_y[ybuf][:, :],
                    hl[:, l * 1024 + jt * 128: l * 1024 + (jt + 1) * 128],
                    blt[:, l * 4096 + obi * 512: l * 4096 + (obi + 1) * 512],
                    start=True, stop=True),
                    waits=[(ev, hl_done), (ev, ev.n)], inc=(pe, 1))
                y_tgt = pe.n
                pl.op("vector", lambda v, l=l, ybuf=ybuf, mbuf=mbuf: v.tensor_mul(
                    tl[:, :], ps_y[ybuf][:, :],
                    msl[mbuf][:, l * 512:(l + 1) * 512]),
                    waits=[(pe, y_tgt), (lh[mbuf], m_tgt), (ev, ev.n)], inc=(ev, 1))
                pl.op("vector", lambda v, av=av: v.tensor_add(
                    av[:, :], av[:, :], tl[:, :]),
                    waits=[(ev, ev.n)], inc=(ev, 1))
            e_tgt = ev.n
            pl.op("sync", lambda s, jt=jt, obi=obi, av=av: s.dma_start(
                out=row128(io[out_key], jt)[:, obi * 512:(obi + 1) * 512],
                in_=av[:, :]), waits=[(ev, e_tgt)], inc=(sto, 16))
    return sto.n



def phase_A1(nc, pl, st, io, sems, glob):
    ld, wt, pe, ev, cc = sems["ld"], sems["wt"], sems["pe"], sems["ev"], sems["cc"]
    sto = sems["sto"]
    wtP = [sems["wtA"], sems["wtB"]]

    sb = lambda n, shp, dt=BF16: st.enter_context(nc.sbuf_tensor(_uname(n), shp, dt))
    psa = lambda n, shp: st.enter_context(nc.psum_tensor(_uname(n), shp, F32))

    x_fm = sb("a_x_fm", [128, NC_D * TPC])
    wqk = [sb(f"a_wqk{i}", [128, 32 * 128]) for i in range(2)]
    wv = [sb(f"a_wv{i}", [128, 32 * 512]) for i in range(2)]
    bqk = sb("a_bqk", [128, 64], F32)
    bv = sb("a_bv", [128, D], F32)
    stg = [sb(f"a_stg{i}", [128, 512]) for i in range(4)]
    ps_a = [psa(f"a_ps{i}", [128, 1024]) for i in range(4)]

    pl.op("gpsimd", lambda g: g.dma_start(
        out=x_fm[:, :].rearrange("p (c f) -> p c f", c=NC_D),
        in_=io["x_fm"].rearrange("c (p f) -> p c f", p=128)),
        inc=(wt, 16))
    x_ready = wt.n
    pl.op("sync", lambda s: s.dma_start(out=bqk[:, :], in_=io["b_qk"]), inc=(ld, 16))
    pl.op("sync", lambda s: s.dma_start(out=bv[:, :], in_=io["b_v_t"]), inc=(ld, 16))
    b_ready = ld.n

    SC = float(1.0 / np.sqrt(128.0))
    n_ev0 = ev.n
    for m in range(64):
        buf = m % 2
        wsb = wqk[buf]
        waits = [(ev, max(n_ev0, ev.n - 1))] if m >= 2 else []
        pl.op("gpsimd", lambda g, m=m, wsb=wsb: g.dma_start(
            out=wsb[:, :], in_=row128(io["w_qk"], m)),
            waits=waits, inc=(wtP[buf], 16))
        w_tgt = wtP[buf].n
        pba, pbb = ps_a[2 * buf], ps_a[2 * buf + 1]
        pe_waits = [(wtP[buf], w_tgt), (wt, wt.n),
                    (ev, max(n_ev0, ev.n - 1))]
        for kc in range(32):
            for h2 in range(2):
                pp = pba if h2 == 0 else pbb
                pl.op("tensor", lambda t, kc=kc, h2=h2, wsb=wsb, pp=pp: t.matmul(
                    pp[:, h2 * 0:512] if False else pp[:, 0:512],
                    wsb[:, kc * 128:(kc + 1) * 128],
                    x_fm[:, kc * TPC + h2 * 512: kc * TPC + (h2 + 1) * 512],
                    start=(kc == 0), stop=(kc == 31)),
                    waits=pe_waits if (kc == 0 and h2 == 0) else (),
                    inc=(pe, 1) if (kc == 31 and h2 == 1) else None)
        mm_tgt = pe.n
        is_q = m < 32
        for h2 in range(2):
            pp = pba if h2 == 0 else pbb
            dst = stg[2 * buf + h2]
            ew = [(pe, mm_tgt), (ld, b_ready), (sto, sto.n)]
            if is_q:
                pl.op("vector", lambda v, pp=pp, dst=dst, m=m: v.tensor_scalar(
                    dst[:, :], pp[:, 0:512], bqk[:, m:m + 1], SC, ALU.add, ALU.mult),
                    waits=ew if h2 == 0 else (),
                    inc=(ev, 1) if h2 == 1 else None)
            else:
                pl.op("vector", lambda v, pp=pp, dst=dst, m=m: v.tensor_scalar(
                    dst[:, :], pp[:, 0:512], bqk[:, m:m + 1], None, ALU.add, ALU.bypass),
                    waits=ew if h2 == 0 else (),
                    inc=(ev, 1) if h2 == 1 else None)
        ev_tgt = ev.n
        drow = row128(io["q_dr"], m) if is_q else row128(io["kv_src"], m - 32)
        pl.op("sync", lambda s, drow=drow, a=stg[2 * buf]: s.dma_start(
            out=drow[:, 0:512], in_=a[:, :]), waits=[(ev, ev_tgt)], inc=(sto, 16))
        pl.op("sync", lambda s, drow=drow, b=stg[2 * buf + 1]: s.dma_start(
            out=drow[:, 512:1024], in_=b[:, :]), inc=(sto, 16))

    for obv in range(8):
        buf = obv % 2
        wsb = wv[buf]
        waits = ([(ev, max(n_ev0, ev.n - 8))] if obv >= 2
                 else [(ev, ev.n)])
        pl.op("gpsimd", lambda g, obv=obv, wsb=wsb: g.dma_start(
            out=wsb[:, :], in_=row128(io["w_v"], obv)),
            waits=waits, inc=(wtP[buf], 16))
        w_tgt = wtP[buf].n
        for jt in range(NT):
            pp = ps_a[2 * buf + (jt % 2)]
            if obv < 2 and jt < 2:
                pe_waits = [(wtP[buf], w_tgt), (ev, ev.n)]
            else:
                pe_waits = [(wtP[buf], w_tgt), (ev, ev.n - 1)]
            for kc in range(32):
                pl.op("tensor", lambda t, jt=jt, kc=kc, wsb=wsb, pp=pp: t.matmul(
                    pp[:, 0:512],
                    x_fm[:, kc * TPC + jt * 128: kc * TPC + (jt + 1) * 128],
                    wsb[:, kc * 512:(kc + 1) * 512],
                    start=(kc == 0), stop=(kc == 31)),
                    waits=pe_waits if kc == 0 else (),
                    inc=(pe, 1) if kc == 31 else None)
            mm_tgt = pe.n
            sidx = 2 * buf + (jt % 2)
            dst = stg[sidx]
            pl.op("vector", lambda v, pp=pp, dst=dst, obv=obv: v.scalar_tensor_tensor(
                dst[:, :], pp[:, 0:512], 1.0, bv[:, obv * 512:(obv + 1) * 512],
                ALU.mult, ALU.add),
                waits=[(pe, mm_tgt), (sto, sto.n)], inc=(ev, 1))
            ev_tgt = ev.n
            vrow = row128(io["kv_src"], 32 + jt * 4 + obv // 2)
            pl.op("sync", lambda s, vrow=vrow, dst=dst, obv=obv: s.dma_start(
                out=vrow[:, (obv % 2) * 512:(obv % 2) * 512 + 512], in_=dst[:, :]),
                waits=[(ev, ev_tgt)], inc=(sto, 16))

    pl.op("gpsimd", lambda g: g.collective_compute(
        "AllGather", ALU.bypass, replica_groups=REPLICA_PAIRS,
        ins=[io["kv_src"].opt()], outs=[io["kv_dst"].opt()]),
        waits=[(ld, ld.n), (sto, sto.n), (ev, ev.n)], inc=(cc, 1))
    return cc.n


def phase_A2(nc, pl, st, io, sems, glob, cc_done, ctx):
    ld, ld2, wt = sems["ld"], sems["ld2"], sems["wt"]
    pe, ev, ac, cc = sems["pe"], sems["ev"], sems["ac"], sems["cc"]

    sb = lambda n, shp, dt=BF16: st.enter_context(nc.sbuf_tensor(_uname(n), shp, dt))
    psa = lambda n, shp: st.enter_context(nc.psum_tensor(_uname(n), shp, F32))

    qh = [sb(f"a2_q{i}", [128, 1024]) for i in range(2)]
    kh = [sb(f"a2_k{i}", [128, 2048]) for i in range(2)]
    vh = [sb(f"a2_v{i}", [128, 2048]) for i in range(2)]
    p_raw = [sb("a2_praw", [128, 2048]) for _ in range(2)]
    p_nrm = [sb("a2_pnrm", [128, 2048]) for _ in range(2)]
    s_sb = [sb("a2_ssb", [128, 2048], F32) for _ in range(2)]
    pt_h = sb("a2_pt", [128, 16 * 1024])
    nmax = [sb("a2_nmax", [128, 1], F32) for _ in range(2)]
    pmax = [sb("a2_pmax", [128, 1], F32) for _ in range(2)]
    sums = [sb("a2_sums", [128, 1], F32) for _ in range(2)]
    recip = [sb("a2_recip", [128, 1], F32) for _ in range(2)]

    ps_s = psa("a2_ps_s", [128, 2048])
    ps_pt = psa("a2_ps_pt", [128, 2048])
    ps_c = ps_s[:, 0:1024]

    ident = glob["ident"]
    id_tgt = glob["id_tgt"]
    kv_dst = io["kv_dst"]

    lh = [sems["lh0"], sems["lh1"]]
    lv = [sems["lv0"], sems["lv1"]]
    idw = wt.n
    wtP = [sems["wtA"], sems["wtB"]]

    def load_head(h, slot):
        g = [(pe, pe.n)]
        for r in range(2):
            pl.op("sync", lambda s, h=h, r=r, slot=slot: s.dma_start(
                out=kh[slot][:, r * 1024:(r + 1) * 1024],
                in_=row128(kv_dst, r * 64 + h)),
                waits=g if r == 0 else (), inc=(lh[slot], 16))
        for r in range(2):
            src = kv_dst[r * 64 + 32 + h // 8: r * 64 + 64: 4, :].rearrange(
                "jk (t f) -> t jk f", t=128)[:, :, (h % 8) * 128:(h % 8) * 128 + 128]
            pl.op("scalar", lambda s, src=src, r=r, slot=slot: s.dma_start(
                out=vh[slot][:, r * 1024:(r + 1) * 1024].rearrange(
                    "t (jk d) -> t jk d", jk=8), in_=src),
                waits=g if r == 0 else (), inc=(lv[slot], 16))
        pl.op("sync", lambda s, h=h, slot=slot: s.dma_start(
            out=qh[slot][:, :], in_=row128(io["q_dr"], h)), inc=(lh[slot], 16))
        return lh[slot].n, lv[slot].n

    pl.fence(sems)
    pl.wait("sync", sems["cc"], cc_done)
    pl.wait("scalar", sems["cc"], cc_done)
    mxev_tgt = [sems["ev"].n]
    ptv_prev = [sems["ac"].n]
    ctxev_prev = [sems["ev"].n]
    exp_prev = {0: sems["ac"].n, 1: sems["ac"].n}
    tgts = [None, None]
    tgts[0] = load_head(0, 0)

    for h in range(32):
        slot = h % 2
        ld_t, ld2_t = tgts[slot]
        if h + 1 < 32:
            tgts[(h + 1) % 2] = load_head(h + 1, (h + 1) % 2)
        for jq in range(NT):
            w0 = [(lh[slot], ld_t), (lv[slot], ld2_t), (wt, idw),
                  (ev, mxev_tgt[0]), (ev, ctxev_prev[0])]
            for n5 in range(4):
                pl.op("tensor", lambda t, jq=jq, n5=n5, slot=slot: t.matmul(
                    ps_s[:, n5 * 512:(n5 + 1) * 512],
                    qh[slot][:, jq * 128:(jq + 1) * 128],
                    kh[slot][:, n5 * 512:(n5 + 1) * 512],
                    start=True, stop=True),
                    waits=w0 if n5 == 0 else (),
                    inc=(pe, 1) if n5 == 3 else None)
            s_tgt = pe.n
            jb = jq % 2
            pl.op("vector", lambda v, jb=jb: v.tensor_scalar(
                s_sb[jb][:, :], ps_s[:, :], 1.0, None, ALU.mult, ALU.max,
                accum_out=pmax[jb][:, :]),
                waits=[(pe, s_tgt), (ac, exp_prev[jb]), (ev, ev.n)], inc=(ev, 1))
            mxev_tgt[0] = ev.n
            pl.op("vector", lambda v, jb=jb: v.tensor_scalar(
                nmax[jb][:, :], pmax[jb][:, :], -1.0, None, ALU.mult, ALU.bypass),
                waits=[(ev, ev.n)], inc=(ev, 1))
            mx_tgt = ev.n
            pl.op("scalar", lambda s, jb=jb: s.activation(
                p_raw[jb][:, :], s_sb[jb][:, :], AF.Exp, bias=nmax[jb][:, :],
                accum_out=sums[jb][:, :]),
                waits=[(ev, mx_tgt)], inc=(ac, 1))
            exp_prev[jb] = ac.n
            exp_tgt = ac.n
            pl.op("vector", lambda v, jb=jb: v.reciprocal(
                recip[jb][:, :], sums[jb][:, :]),
                waits=[(ac, exp_tgt)], inc=(ev, 1))
            pl.op("vector", lambda v, jb=jb: v.tensor_scalar(
                p_nrm[jb][:, :], p_raw[jb][:, :], recip[jb][:, :], None,
                ALU.mult, ALU.bypass),
                waits=[(ev, ev.n), (pe, pe.n)], inc=(ev, 1))
            nrm_tgt = ev.n
            tw = [(ev, nrm_tgt), (ac, ptv_prev[0])]
            for kc in range(16):
                pl.op("tensor", lambda t, kc=kc, jb=jb: t.matmul(
                    ps_pt[:, kc * 128:(kc + 1) * 128],
                    p_nrm[jb][:, kc * 128:(kc + 1) * 128],
                    ident[:, :], start=True, stop=True),
                    waits=tw if kc == 0 else (),
                    inc=(pe, 1) if kc == 15 else None)
            tr_tgt = pe.n
            dst = pt_h[:, :].rearrange("p (kc q) -> p kc q", kc=16)[
                :, :, jq * 128:(jq + 1) * 128]
            pl.op("scalar", lambda s, dst=dst: s.activation(
                dst, ps_pt[:, :].rearrange("p (a b) -> p a b", a=16), AF.Copy),
                waits=[(pe, tr_tgt)], inc=(ac, 1))
            ptv_prev[0] = ac.n
        pt_done = ac.n
        cw = [(ac, pt_done), (ev, ev.n)]
        for kc in range(16):
            for q2 in range(2):
                pl.op("tensor", lambda t, kc=kc, q2=q2, slot=slot: t.matmul(
                    ps_c[:, q2 * 512:(q2 + 1) * 512],
                    vh[slot][:, kc * 128:(kc + 1) * 128],
                    pt_h[:, kc * 1024 + q2 * 512: kc * 1024 + (q2 + 1) * 512],
                    start=(kc == 0), stop=(kc == 15)),
                    waits=cw if (kc == 0 and q2 == 0) else (),
                    inc=(pe, 1) if (kc == 15 and q2 == 1) else None)
        ctx_tgt = pe.n
        pl.op("vector", lambda v, h=h: v.tensor_copy(
            ctx[:, h * 1024:(h + 1) * 1024], ps_c[:, :]),
            waits=[(pe, ctx_tgt)], inc=(ev, 1))
        ctxev_prev[0] = ev.n
    return ev.n


def phase_A3a(nc, pl, st, io, sems, glob, ctx, ctx_done, w_key="w_o",
              bias_key="ob_t", s1a_key="s1a"):
    ld, wt, pe, ev = sems["ld"], sems["wt"], sems["pe"], sems["ev"]
    sto = sems["sto"]
    wtP = [sems["wtA"], sems["wtB"]]

    sb = lambda n, shp, dt=BF16: st.enter_context(nc.sbuf_tensor(_uname(n), shp, dt))
    psa = lambda n, shp: st.enter_context(nc.psum_tensor(_uname(n), shp, F32))

    wo = [sb(f"a3_wo{i}", [128, 32 * 512]) for i in range(2)]
    obt = sb("a3_obt", [128, D], F32)
    pl.fence(sems)
    stg = [sb(f"a3_stg{i}", [128, 512], F32) for i in range(4)]
    ps_o = [psa(f"a3_ps{i}", [128, 512]) for i in range(4)]

    pl.op("sync", lambda s: s.dma_start(out=obt[:, :], in_=io[bias_key]),
          waits=[(ev, ctx_done)], inc=(ld, 16))
    obt_tgt = ld.n

    wo_tgt = [None, None]
    pe0 = pe.n
    for obi in range(8):
        buf = obi % 2
        lw = [(ev, ctx_done)] if obi < 2 else [(pe, max(pe0, pe.n - 8))]
        pl.op("gpsimd", lambda g, obi=obi, buf=buf: g.dma_start(
            out=wo[buf][:, :], in_=row128(io[w_key], obi)), waits=lw,
            inc=(wtP[buf], 16))
        wo_tgt[buf] = wtP[buf].n
        for jt in range(NT):
            pp = ps_o[jt % 4]
            mmw = [(wtP[buf], wo_tgt[buf]), (ev, ctx_done), (ev, ev.n)]
            for kc in range(32):
                pl.op("tensor", lambda t, jt=jt, kc=kc, pp=pp, buf=buf: t.matmul(
                    pp[:, :],
                    ctx[:, kc * 1024 + jt * 128: kc * 1024 + (jt + 1) * 128],
                    wo[buf][:, kc * 512:(kc + 1) * 512],
                    start=(kc == 0), stop=(kc == 31)),
                    waits=mmw if kc == 0 else (),
                    inc=(pe, 1) if kc == 31 else None)
            mm_tgt = pe.n
            dst = stg[jt % 4]
            pl.op("vector", lambda v, pp=pp, dst=dst, obi=obi: v.scalar_tensor_tensor(
                dst[:, :], pp[:, :], 1.0, obt[:, obi * 512:(obi + 1) * 512],
                ALU.mult, ALU.add),
                waits=[(pe, mm_tgt), (ld, obt_tgt), (sto, sto.n)], inc=(ev, 1))
            ev_tgt = ev.n
            pl.op("sync", lambda s, jt=jt, obi=obi, dst=dst: s.dma_start(
                out=row128(io[s1a_key], jt)[:, obi * 512:(obi + 1) * 512],
                in_=dst[:, :]), waits=[(ev, ev_tgt)], inc=(sto, 16))
    return sto.n


def phase_LN(nc, pl, st, io, sems, glob, in_keys, out_prefix, s1a_stores,
             fold=False, want_fm=True):
    ld, wt, pe, ev, ac, stq = (sems["ld"], sems["wt"], sems["pe"], sems["ev"],
                               sems["ac"], sems["st"])

    sb = lambda n, shp, dt=BF16: st.enter_context(nc.sbuf_tensor(_uname(n), shp, dt))
    psa = lambda n, shp: st.enter_context(nc.psum_tensor(_uname(n), shp, F32))

    xa = sb("ln_xa", [128, D], F32)
    xbb = sb("ln_xb", [128, D], F32)
    s1 = sb("ln_s1", [128, D], F32)
    xn = sb("ln_xn", [128, D], F32)
    xnb = sb("ln_xnb", [128, D])
    xf = sb("ln_xf", [128, 1024], F32)
    xfb = sb("ln_xfb", [128, 1024])
    dump = sb("ln_dump", [128, D], F32)
    fm_stg = sb("ln_fmstg", [128, NC_D * 128])
    ff_stg = sb("ln_ffstg", [128, NT * 128])
    mean = sb("ln_mean", [128, 1], F32)
    ssum = sb("ln_ssum", [128, 1], F32)
    var = sb("ln_var", [128, 1], F32)
    rstd = sb("ln_rstd", [128, 1], F32)
    tmp1 = sb("ln_tmp1", [128, 1], F32)
    ps_t = psa("ln_ps_t", [128, 1024])

    ident = glob["ident"]
    ngrp = (10 if fold else 8) if want_fm else 0

    sto = sems["sto"]
    pl.fence(sems)
    for jt in range(NT):
        lw = [(sto, s1a_stores), (ev, ev.n), (ac, ac.n)]
        pl.op("sync", lambda s, jt=jt: s.dma_start(
            out=xa[:, :], in_=row128(io[in_keys[0]], jt)), waits=lw, inc=(ld, 16))
        pl.op("sync", lambda s, jt=jt: s.dma_start(
            out=xbb[:, :], in_=row128(io[in_keys[1]], jt)), inc=(ld, 16))
        x_tgt = ld.n
        def vstep(fn, extra=()):
            pl.op("vector", fn, waits=[(ev, ev.n)] + list(extra), inc=(ev, 1))

        pl.op("vector", lambda v: v.tensor_add(s1[:, :], xa[:, :], xbb[:, :]),
              waits=[(ld, x_tgt), (ev, ev.n)], inc=(ev, 1))
        vstep(lambda v: v.reduce_sum(mean[:, :], s1[:, :], AX.X))
        vstep(lambda v: v.scalar_tensor_tensor(
            dump[:, :], s1[:, :], 1.0, s1[:, :], ALU.mult, ALU.mult,
            accum_out=ssum[:, :]))
        vstep(lambda v: v.tensor_scalar(
            mean[:, :], mean[:, :], 1.0 / D, None, ALU.mult, ALU.bypass))
        vstep(lambda v: v.scalar_tensor_tensor(
            tmp1[:, :], mean[:, :], 1.0, mean[:, :], ALU.mult, ALU.mult))
        vstep(lambda v: v.tensor_scalar(
            var[:, :], ssum[:, :], 1.0 / D, None, ALU.mult, ALU.bypass))
        vstep(lambda v: v.tensor_sub(var[:, :], var[:, :], tmp1[:, :]))
        var_tgt = ev.n
        pl.op("scalar", lambda s: s.activation(
            rstd[:, :], var[:, :], AF.Ln, bias=glob["eps"][:, :]),
            waits=[(ev, var_tgt), (ac, ac.n)], inc=(ac, 1))
        pl.op("scalar", lambda s: s.activation(
            rstd[:, :], rstd[:, :], AF.Exp, scale=-0.5),
            waits=[(ac, ac.n)], inc=(ac, 1))
        rstd_tgt = ac.n
        vstep(lambda v: v.tensor_scalar(
            xn[:, :], s1[:, :], mean[:, :], rstd[:, :], ALU.subtract, ALU.mult),
            extra=[(ac, rstd_tgt), (stq, stq.n)])
        xn_tgt = ev.n
        pl.op("sync", lambda s, jt=jt: s.dma_start(
            out=row128(io[f"{out_prefix}_tm"], jt), in_=xn[:, :]),
            waits=[(ev, xn_tgt)], inc=(stq, 16))
        vstep(lambda v: v.tensor_copy(xnb[:, :], xn[:, :]))
        if fold:
            vstep(lambda v: v.tensor_add(
                xf[:, :], xn[:, 0:1024], xn[:, 1024:2048]))
            vstep(lambda v: v.tensor_add(
                dump[:, 0:1024], xn[:, 2048:3072], xn[:, 3072:4096]))
            vstep(lambda v: v.tensor_add(
                xf[:, :], xf[:, :], dump[:, 0:1024]))
            vstep(lambda v: v.tensor_copy(xfb[:, :], xf[:, :]))
        cast_tgt = ev.n
        for grp in range(ngrp):
            half = grp % 2
            tw = [(ev, cast_tgt), (ac, ac.n)]
            for q4 in range(4):
                cch = grp * 4 + q4
                src = xnb if cch < 32 else xfb
                coff = cch * 128 if cch < 32 else (cch - 32) * 128
                pl.op("tensor", lambda t, src=src, coff=coff, half=half, q4=q4: t.matmul(
                    ps_t[:, half * 512 + q4 * 128: half * 512 + (q4 + 1) * 128],
                    src[:, coff:coff + 128], ident[:, :], start=True, stop=True),
                    waits=tw if q4 == 0 else (),
                    inc=(pe, 1) if q4 == 3 else None)
            tr_tgt = pe.n
            dsts = fm_stg if grp < 8 else ff_stg
            doff = grp * 512 if grp < 8 else (grp - 8) * 512
            pl.op("scalar", lambda s, dsts=dsts, doff=doff, half=half: s.activation(
                dsts[:, doff:doff + 512],
                ps_t[:, half * 512:(half + 1) * 512], AF.Copy),
                waits=[(pe, tr_tgt), (stq, stq.n)], inc=(ac, 1))
        fm_done = ac.n
        if want_fm:
            pl.op("sync", lambda s, jt=jt: s.dma_start(
                out=io[f"{out_prefix}_fm"].rearrange(
                    "c (p f) -> p c f", p=128)[:, :, jt * 128:(jt + 1) * 128],
                in_=fm_stg[:, :].rearrange("p (c f) -> p c f", c=NC_D)),
                waits=[(ac, fm_done)], inc=(stq, 16))
        if fold:
            pl.op("sync", lambda s, jt=jt: s.dma_start(
                out=io[f"{out_prefix}f_fm"].rearrange(
                    "c (p f) -> p c f", p=128)[:, :, jt * 128:(jt + 1) * 128],
                in_=ff_stg[:, :].rearrange("p (c f) -> p c f", c=NT)),
                inc=(stq, 16))
    return stq.n



def make_sems(nc, st):
    names = ["ld", "ld2", "wt", "wtA", "wtB", "pe", "ev", "ac", "cc", "st",
             "sto", "lh0", "lh1", "lv0", "lv1",
             "b_tensor", "b_vector", "b_scalar", "b_gpsimd", "b_sync"]
    sems = {}
    for n in names:
        h = st.enter_context(nc.semaphore(n))
        sems[n] = Sem(h, n)
    sems_b = {e: sems[f"b_{e}"] for e in Plan.ENGINES}
    sems["bar"] = sems_b
    return sems


def declare_dram_A(nc, debug=False):
    io = {}
    ei = lambda n, shp, dt=F32: io.__setitem__(
        n, nc.dram_tensor(n, shp, dt, kind="ExternalInput").ap())
    eo = lambda n, shp, dt=F32: io.__setitem__(
        n, nc.dram_tensor(n, shp, dt, kind="ExternalOutput").ap())
    internal = lambda n, shp, dt=BF16: io.__setitem__(
        n, nc.dram_tensor(n, shp, dt, kind="Internal").ap())
    if debug:
        io["_dbg"] = []
        def dbg(n, shp, dt):
            io.__setitem__("dbg_" + n,
                           nc.dram_tensor("dbg_" + n, shp, dt,
                                          kind="ExternalOutput").ap())
            io["_dbg"].append(n)
        io["_mkdbg"] = dbg
    ei("x_tm", [NT, 128 * D])
    ei("x_fm", [NC_D, 128 * TPC])
    ei("w_qk", [64, 128 * 32 * 128])
    ei("w_v", [8, 128 * 32 * 512])
    ei("b_qk", [128, 64])
    ei("b_v_t", [128, D])
    ei("w_o", [8, 128 * 32 * 512])
    ei("ob_t", [128, D])
    ei("ident", [128, 128])
    internal("q_dr", [32, 128 * 1024])
    internal("kv_src", [64, 128 * 1024])
    internal("kv_dst", [128, 128 * 1024])
    internal("s1a", [NT, 128 * D], F32)
    eo("x1_tm", [NT, 128 * D])
    eo("x1_fm", [NC_D, 128 * TPC], BF16)
    eo("x1f_fm", [NT, 128 * TPC], BF16)
    if debug:
        io["_mkdbg"]("q_dr", [32, 128 * 1024], BF16)
        io["_mkdbg"]("kv_src", [64, 128 * 1024], BF16)
        io["_mkdbg"]("s1a", [NT, 128 * D], F32)
        io["dbg_ctx"] = nc.dram_tensor(
            "dbg_ctx", [32, 128 * 1024], BF16, kind="ExternalOutput").ap()
    return io


def build_program_A(debug=False):
    nc = bass.Bass("TRN2", target_bir_lowering=False)
    io = declare_dram_A(nc, debug=debug)
    pl = Plan()
    with ExitStack() as top:
        sems = make_sems(nc, top)
        glob = {}
        glob["ident"] = top.enter_context(nc.sbuf_tensor("g_ident", [128, 128], BF16))
        pl.op("gpsimd", lambda g: g.dma_start(
            out=glob["ident"][:, :], in_=io["ident"]), inc=(sems["wt"], 16))
        glob["id_tgt"] = sems["wt"].n
        glob["eps"] = top.enter_context(nc.sbuf_tensor("g_eps", [128, 1], F32))
        pl.op("vector", lambda v: v.memset(glob["eps"][:, :], 1e-5),
              inc=(sems["ev"], 1))
        glob["eps_tgt"] = sems["ev"].n

        with ExitStack() as st:
            cc_done = phase_A1(nc, pl, st, io, sems, glob)
            pl.barrier(sems["bar"], dma_sems=[sems["ld"], sems["wt"], sems["wtA"], sems["wtB"], sems["sto"]])
        with ExitStack() as st:
            ctx = st.enter_context(nc.sbuf_tensor("a2_ctx", [128, 32 * 1024], BF16))
            with ExitStack() as st2:
                ctx_done = phase_A2(nc, pl, st2, io, sems, glob, cc_done, ctx)
            if debug:
                for hh in range(32):
                    pl.op("sync", lambda s, hh=hh: s.dma_start(
                        out=row128(io["dbg_ctx"], hh),
                        in_=ctx[:, hh * 1024:(hh + 1) * 1024]),
                        waits=[(sems["ev"], ctx_done)] if hh == 0 else (),
                        inc=(sems["st"], 16))
            s1a_stores = phase_A3a(nc, pl, st, io, sems, glob, ctx, ctx_done)
            pl.barrier(sems["bar"], dma_sems=[
                sems["ld"], sems["ld2"], sems["wt"], sems["wtA"], sems["wtB"],
                sems["sto"],
                sems["lh0"], sems["lh1"], sems["lv0"], sems["lv1"]])
        with ExitStack() as st:
            phase_LN(nc, pl, st, io, sems, glob,
                     ["x_tm", "s1a"], "x1", s1a_stores, fold=True)
        if debug:
            for n in io["_dbg"]:
                pl.op("sync", lambda s, n=n: s.dma_start(
                    out=io["dbg_" + n], in_=io[n]),
                    waits=[(sems["st"], sems["st"].n)], inc=(sems["st"], 16))
        pl.wait("gpsimd", sems["st"], sems["st"].n)
        pl.wait("sync", sems["st"], sems["st"].n)
        pl.realize(nc)
    return nc


def declare_x(nc, io, prefix, kind_tm="ExternalInput"):
    io[f"{prefix}_tm"] = nc.dram_tensor(
        f"{prefix}_tm", [NT, 128 * D], F32, kind=kind_tm).ap()
    io[f"{prefix}_fm"] = nc.dram_tensor(
        f"{prefix}_fm", [NC_D, 128 * TPC], BF16, kind=kind_tm).ap()
    io[f"{prefix}f_fm"] = nc.dram_tensor(
        f"{prefix}f_fm", [NT, 128 * TPC], BF16, kind=kind_tm).ap()


def build_program_B(debug=False):
    nc = bass.Bass("TRN2", target_bir_lowering=False)
    io = {}
    if debug:
        io["_dbg"] = []
        def dbg(n, shp, dt):
            io["dbg_" + n] = nc.dram_tensor(
                "dbg_" + n, shp, dt, kind="ExternalOutput").ap()
            io["_dbg"].append(n)
        io["_mkdbg"] = dbg
    declare_x(nc, io, "x1", "ExternalInput")
    io["ident"] = nc.dram_tensor("ident", [128, 128], F32,
                                 kind="ExternalInput").ap()
    declare_dram_B(nc, io, debug=debug)
    io["x2_tm"] = nc.dram_tensor("x2_tm", [NT, 128 * D], F32,
                                 kind="ExternalOutput").ap()
    io["x2_fm"] = nc.dram_tensor("x2_fm", [NC_D, 128 * TPC], BF16,
                                 kind="ExternalOutput").ap()
    pl = Plan()
    with ExitStack() as top:
        sems = make_sems(nc, top)
        glob = {}
        glob["ident"] = top.enter_context(nc.sbuf_tensor("g_ident", [128, 128], BF16))
        pl.op("gpsimd", lambda g: g.dma_start(
            out=glob["ident"][:, :], in_=io["ident"]), inc=(sems["wt"], 16))
        glob["id_tgt"] = sems["wt"].n
        glob["eps"] = top.enter_context(nc.sbuf_tensor("g_eps", [128, 1], F32))
        pl.op("vector", lambda v: v.memset(glob["eps"][:, :], 1e-5),
              inc=(sems["ev"], 1))

        with ExitStack() as st:
            s1b_stores = phase_B(nc, pl, st, io, sems, glob)
            pl.barrier(sems["bar"], dma_sems=[sems["ld"], sems["wt"], sems["wtA"], sems["wtB"], sems["sto"]])
        with ExitStack() as st:
            phase_LN(nc, pl, st, io, sems, glob,
                     ["x1_tm", "s1b"], "x2", s1b_stores, fold=False)
        if debug:
            for n in io["_dbg"]:
                pl.op("sync", lambda s, n=n: s.dma_start(
                    out=io["dbg_" + n], in_=io[n]),
                    waits=[(sems["st"], sems["st"].n), (sems["ld"], sems["ld"].n)],
                    inc=(sems["st"], 16))
        pl.wait("gpsimd", sems["st"], sems["st"].n)
        pl.wait("sync", sems["st"], sems["st"].n)
        pl.realize(nc)
    return nc


def build_program_C(debug=False):
    nc = bass.Bass("TRN2", target_bir_lowering=False)
    io = {}
    if debug:
        io["_dbg"] = []
        def dbg(n, shp, dt):
            io["dbg_" + n] = nc.dram_tensor(
                "dbg_" + n, shp, dt, kind="ExternalOutput").ap()
            io["_dbg"].append(n)
        io["_mkdbg"] = dbg
    declare_x(nc, io, "x2", "ExternalInput")
    io["ident"] = nc.dram_tensor("ident", [128, 128], F32,
                                 kind="ExternalInput").ap()
    declare_dram_C(nc, io, debug=debug)
    io["x3_tm"] = nc.dram_tensor("x3_tm", [NT, 128 * D], F32,
                                 kind="ExternalOutput").ap()
    io["x3_fm"] = nc.dram_tensor("x3_fm", [NC_D, 128 * TPC], BF16,
                                 kind="ExternalOutput").ap()
    pl = Plan()
    with ExitStack() as top:
        sems = make_sems(nc, top)
        glob = {}
        glob["ident"] = top.enter_context(nc.sbuf_tensor("g_ident", [128, 128], BF16))
        pl.op("gpsimd", lambda g: g.dma_start(
            out=glob["ident"][:, :], in_=io["ident"]), inc=(sems["wt"], 16))
        glob["id_tgt"] = sems["wt"].n
        glob["eps"] = top.enter_context(nc.sbuf_tensor("g_eps", [128, 1], F32))
        pl.op("vector", lambda v: v.memset(glob["eps"][:, :], 1e-5),
              inc=(sems["ev"], 1))

        with ExitStack() as st:
            ctx, ctx_done = phase_C(nc, pl, st, io, sems, glob)
            s1c_stores = phase_A3a(nc, pl, st, io, sems, glob, ctx, ctx_done,
                                   w_key="o_w_t", bias_key="zeros_t",
                                   s1a_key="s1c")
            pl.barrier(sems["bar"], dma_sems=[
                sems["ld"], sems["ld2"], sems["wt"], sems["wtA"], sems["wtB"],
                sems["sto"],
                sems["lh0"], sems["lh1"], sems["lv0"], sems["lv1"]])
        with ExitStack() as st:
            phase_LN(nc, pl, st, io, sems, glob,
                     ["x2_tm", "s1c"], "x3", s1c_stores, fold=False)
        if debug:
            for n in io["_dbg"]:
                pl.op("sync", lambda s, n=n: s.dma_start(
                    out=io["dbg_" + n], in_=io[n]),
                    waits=[(sems["st"], sems["st"].n), (sems["sto"], sems["sto"].n)],
                    inc=(sems["st"], 16))
        pl.wait("gpsimd", sems["st"], sems["st"].n)
        pl.wait("sync", sems["st"], sems["st"].n)
        pl.realize(nc)
    return nc




def build_program_D(debug=False):
    nc = bass.Bass("TRN2", target_bir_lowering=False)
    io = {}
    if debug:
        io["_dbg"] = []
        def dbg(n, shp, dt):
            io["dbg_" + n] = nc.dram_tensor(
                "dbg_" + n, shp, dt, kind="ExternalOutput").ap()
            io["_dbg"].append(n)
        io["_mkdbg"] = dbg
    declare_x(nc, io, "x3", "ExternalInput")
    io["ident"] = nc.dram_tensor("ident", [128, 128], F32,
                                 kind="ExternalInput").ap()
    declare_dram_D(nc, io, debug=debug)
    io["x4_tm"] = nc.dram_tensor("x4_tm", [NT, 128 * D], F32,
                                 kind="ExternalOutput").ap()
    pl = Plan()
    with ExitStack() as top:
        sems = make_sems(nc, top)
        glob = {}
        glob["ident"] = top.enter_context(nc.sbuf_tensor("g_ident", [128, 128], BF16))
        pl.op("gpsimd", lambda g: g.dma_start(
            out=glob["ident"][:, :], in_=io["ident"]), inc=(sems["wt"], 16))
        glob["id_tgt"] = sems["wt"].n
        glob["eps"] = top.enter_context(nc.sbuf_tensor("g_eps", [128, 1], F32))
        pl.op("vector", lambda v: v.memset(glob["eps"][:, :], 1e-5),
              inc=(sems["ev"], 1))
        with ExitStack() as st:
            s1d_stores = phase_D(nc, pl, st, io, sems, glob)
            pl.barrier(sems["bar"], dma_sems=[
                sems["ld"], sems["wt"], sems["wtA"], sems["wtB"], sems["sto"],
                sems["lh0"], sems["lh1"]])
        with ExitStack() as st:
            phase_LN(nc, pl, st, io, sems, glob,
                     ["x3_tm", "s1d"], "x4", s1d_stores, fold=False,
                     want_fm=False)
        if debug:
            for n in io["_dbg"]:
                pl.op("sync", lambda s, n=n: s.dma_start(
                    out=io["dbg_" + n], in_=io[n]),
                    waits=[(sems["st"], sems["st"].n), (sems["sto"], sems["sto"].n)],
                    inc=(sems["st"], 16))
        pl.wait("gpsimd", sems["st"], sems["st"].n)
        pl.wait("sync", sems["st"], sems["st"].n)
        pl.realize(nc)
    return nc


def _declare_full(nc, io, debug):
    ei = lambda n, shp, dt=F32: io.__setitem__(
        n, nc.dram_tensor(n, shp, dt, kind="ExternalInput").ap())
    internal = lambda n, shp, dt=BF16: io.__setitem__(
        n, nc.dram_tensor(n, shp, dt, kind="Internal").ap())
    ei("x_tm", [NT, 128 * D])
    ei("x_fm", [NC_D, 128 * TPC])
    ei("ident", [128, 128])
    ei("w_qk", [64, 128 * 32 * 128])
    ei("w_v", [8, 128 * 32 * 512])
    ei("b_qk", [128, 64])
    ei("b_v_t", [128, D])
    ei("w_o", [8, 128 * 32 * 512])
    ei("ob_t", [128, D])
    internal("q_dr", [32, 128 * 1024])
    internal("kv_src", [64, 128 * 1024])
    internal("kv_dst", [128, 128 * 1024])
    internal("s1a", [NT, 128 * D], F32)
    for p in ("x1", "x2", "x3"):
        internal(f"{p}_tm", [NT, 128 * D], F32)
        internal(f"{p}_fm", [NC_D, 128 * TPC], BF16)
        internal(f"{p}f_fm", [NT, 128 * TPC], BF16)
    declare_dram_B(nc, io, debug=debug)
    declare_dram_C(nc, io, debug=debug)
    declare_dram_D(nc, io, debug=debug)
    io["out_tm"] = nc.dram_tensor("out_tm", [NT, 128 * D], F32,
                                  kind="ExternalOutput").ap()


def build_full(debug=False):
    nc = bass.Bass("TRN2", target_bir_lowering=False)
    io = {}
    if debug:
        io["_dbg"] = []
        def dbg(n, shp, dt):
            io["dbg_" + n] = nc.dram_tensor(
                "dbg_" + n, shp, dt, kind="ExternalOutput").ap()
            io["_dbg"].append(n)
        io["_mkdbg"] = dbg
    _declare_full(nc, io, debug)
    pl = Plan()
    with ExitStack() as top:
        sems = make_sems(nc, top)
        glob = {}
        glob["ident"] = top.enter_context(nc.sbuf_tensor("g_ident", [128, 128], BF16))
        pl.op("gpsimd", lambda g: g.dma_start(
            out=glob["ident"][:, :], in_=io["ident"]), inc=(sems["wt"], 16))
        glob["id_tgt"] = sems["wt"].n
        glob["eps"] = top.enter_context(nc.sbuf_tensor("g_eps", [128, 1], F32))
        pl.op("vector", lambda v: v.memset(glob["eps"][:, :], 1e-5),
              inc=(sems["ev"], 1))

        allsems = [sems[k] for k in ("ld", "ld2", "wt", "wtA", "wtB", "sto",
                                     "lh0", "lh1", "lv0", "lv1", "st")]

        def bar():
            pl.barrier(sems["bar"], dma_sems=allsems)

        with ExitStack() as st:
            cc_done = phase_A1(nc, pl, st, io, sems, glob)
            bar()
        with ExitStack() as st:
            ctx = st.enter_context(nc.sbuf_tensor("a2_ctx", [128, 32 * 1024],
                                                  BF16, side="right"))
            with ExitStack() as st2:
                ctx_done = phase_A2(nc, pl, st2, io, sems, glob, cc_done, ctx)
            s1a_stores = phase_A3a(nc, pl, st, io, sems, glob, ctx, ctx_done)
            bar()
        with ExitStack() as st:
            fm1 = phase_LN(nc, pl, st, io, sems, glob,
                           ["x_tm", "s1a"], "x1", s1a_stores, fold=True)
            bar()
        with ExitStack() as st:
            s1b_stores = phase_B(nc, pl, st, io, sems, glob, fm_ready=fm1)
            bar()
        with ExitStack() as st:
            fm2 = phase_LN(nc, pl, st, io, sems, glob,
                           ["x1_tm", "s1b"], "x2", s1b_stores, fold=False)
            bar()
        with ExitStack() as st:
            ctx2, ctx2_done = phase_C(nc, pl, st, io, sems, glob,
                                      x_fm_key="x2_fm", fm_ready=fm2)
            s1c_stores = phase_A3a(nc, pl, st, io, sems, glob, ctx2, ctx2_done,
                                   w_key="o_w_t", bias_key="zeros_t",
                                   s1a_key="s1c")
            bar()
        with ExitStack() as st:
            fm3 = phase_LN(nc, pl, st, io, sems, glob,
                           ["x2_tm", "s1c"], "x3", s1c_stores, fold=False)
            bar()
        with ExitStack() as st:
            s1d_stores = phase_D(nc, pl, st, io, sems, glob,
                                 x_fm_key="x3_fm", fm_ready=fm3)
            bar()
        with ExitStack() as st:
            phase_LN(nc, pl, st, io, sems, glob,
                     ["x3_tm", "s1d"], "out", s1d_stores, fold=False,
                     want_fm=False)
        if debug:
            for n in io["_dbg"]:
                pl.op("sync", lambda s, n=n: s.dma_start(
                    out=io["dbg_" + n], in_=io[n]),
                    waits=[(sems["st"], sems["st"].n),
                           (sems["sto"], sems["sto"].n)],
                    inc=(sems["st"], 16))
        pl.wait("gpsimd", sems["st"], sems["st"].n)
        pl.wait("sync", sems["st"], sems["st"].n)
        pl.realize(nc)
    return nc


def host_prep_full(inputs):
    per_a = host_prep_A(inputs)
    pb = host_prep_B(inputs)
    pc = host_prep_C(inputs)
    pd = host_prep_D(inputs)
    per_core = []
    for c in range(NCORES):
        d = dict(per_a[c])
        d.update(pb)
        d.update(pc)
        d.update(host_prep_C_core(c))
        d.update(pd)
        d.update(host_prep_D_core(c))
        per_core.append(d)
    return per_core


_PROGS = {}


def _prog(name):
    if name not in _PROGS:
        _PROGS[name] = {
            "A": build_program_A, "B": build_program_B,
            "C": build_program_C, "D": build_program_D,
            "full": build_full,
        }[name](debug=False)
    return _PROGS[name]


def kernel(**inputs):
    per_core = host_prep_full(inputs)
    res = run_bass_kernel_spmd(_prog("full"), per_core,
                               core_ids=list(range(NCORES)))
    out = np.zeros((4, 2048, 4096), np.float32)
    for c in range(NCORES):
        b, p = c // 2, c % 2
        tiles = [2 * j + p for j in range(NT)]
        x4 = res.results[c]["out_tm"].reshape(NT, 128, 4096)
        for j, g in enumerate(tiles):
            out[b, g * 128:(g + 1) * 128, :] = x4[j]
    return out


# version 09
# speedup vs baseline: 1.2251x; 1.2251x over previous
import numpy as np
import ml_dtypes
from contextlib import ExitStack
import concourse.bass as bass
import concourse.mybir as mybir
from concourse.bass_utils import run_bass_kernel_spmd

F32 = mybir.dt.float32
BF16 = mybir.dt.bfloat16
AF = mybir.ActivationFunctionType
ALU = mybir.AluOpType
AX = mybir.AxisListType

NCORES = 8
NT = 8
TPC = NT * 128
D = 4096
NC_D = D // 128
BF = ml_dtypes.bfloat16

REPLICA_PAIRS = [[0, 1], [2, 3], [4, 5], [6, 7]]


class Sem:
    def __init__(self, handle, name):
        self.h = handle
        self.name = name
        self.n = 0

    def bump(self, amt):
        self.n += amt
        return self.n


class Plan:
    ENGINES = ("tensor", "vector", "scalar", "gpsimd", "sync")

    def __init__(self):
        self.streams = {e: [] for e in self.ENGINES}
        self.n_ops = 0

    def op(self, engine, fn, waits=(), inc=None):
        wl = []
        for (s, v) in waits:
            assert v <= s.n, f"wait on {s.name}={v} > planned {s.n}"
            wl.append((s.h, v))
        if inc is not None:
            sem, amt = inc
            sem.bump(amt)
            il = (sem.h, amt)
        else:
            il = None
        self.n_ops += 1

        def run(eng):
            for (h, v) in wl:
                eng.wait_ge(h, v)
            ins = fn(eng)
            if il is not None:
                assert ins is not None
                ins.then_inc(il[0], il[1])
        self.streams[engine].append(run)

    def wait(self, engine, sem, val):
        self.op(engine, lambda e: None, waits=[(sem, val)])

    def fence(self, sems, queues=("sync", "gpsimd", "scalar")):
        keys = ["ld", "ld2", "wt", "wtA", "wtB", "sto", "lh0", "lh1",
                "lv0", "lv1", "st", "pe", "ev", "ac"]
        targets = [(sems[k], sems[k].n) for k in keys]
        for q in queues:
            self.op(q, lambda e: None, waits=targets)

    def barrier(self, sems, dma_sems=()):
        engines = self.ENGINES
        dma_targets = [(s, s.n) for s in dma_sems]
        for e in engines:
            for (s, v) in dma_targets:
                self.wait(e, s, v)
        targets = {}
        for e in engines:
            targets[e] = sems[e].bump(1)
        for e in engines:
            s = sems[e]

            def mk(s=s):
                def run(eng):
                    eng.sem_inc(s.h, 1)
                return run
            self.streams[e].append(mk())
            self.n_ops += 1
        for e in engines:
            for o in engines:
                self.wait(e, sems[o], targets[o])

    def realize(self, nc):
        with nc.Block() as block:
            @block.tensor
            def _(t):
                for fn in self.streams["tensor"]:
                    fn(t)

            @block.vector
            def _(v):
                for fn in self.streams["vector"]:
                    fn(v)

            @block.scalar
            def _(s):
                for fn in self.streams["scalar"]:
                    fn(s)

            @block.gpsimd
            def _(g):
                for fn in self.streams["gpsimd"]:
                    fn(g)

            @block.sync
            def _(sy):
                for fn in self.streams["sync"]:
                    fn(sy)


_NAME_SEQ = [0]


def _uname(n):
    _NAME_SEQ[0] += 1
    return f"{n}_{_NAME_SEQ[0]}"


def row128(ap2d, i):
    return ap2d[i:i + 1, :].rearrange("o (p f) -> (o p) f", p=128)



def _tile_lhst(w, nkc, nm):
    i_, o_ = w.shape
    assert i_ == nkc * 128 and o_ == nm * 128
    t = w.reshape(nkc, 128, nm, 128).transpose(2, 1, 0, 3)
    return np.ascontiguousarray(t.reshape(nm, 128 * nkc * 128))


def _tile_rhs(w, nkc, nob, obw=512):
    i_, o_ = w.shape
    assert i_ == nkc * 128 and o_ == nob * obw
    t = w.reshape(nkc, 128, nob, obw).transpose(2, 1, 0, 3)
    return np.ascontiguousarray(t.reshape(nob, 128 * nkc * obw))


def host_prep_A(inputs):
    x = np.asarray(inputs["L0_attn_x"], dtype=np.float32)
    qkvw = np.asarray(inputs["L0_attn_attn_qkvw"], dtype=np.float32)
    qkvb = np.asarray(inputs["L0_attn_attn_qkvb"], dtype=np.float32)
    ow = np.asarray(inputs["L0_attn_attn_ow"], dtype=np.float32)
    ob = np.asarray(inputs["L0_attn_attn_ob"], dtype=np.float32)

    shared = {
        "w_qk": _tile_lhst(qkvw[:, :8192], 32, 64),
        "w_v": _tile_rhs(qkvw[:, 8192:], 32, 8),
        "b_qk": np.ascontiguousarray(qkvb[:8192].reshape(64, 128).T),
        "b_v_t": np.ascontiguousarray(np.tile(qkvb[8192:][None, :], (128, 1))),
        "w_o": _tile_rhs(ow, 32, 8),
        "ob_t": np.ascontiguousarray(np.tile(ob[None, :], (128, 1))),
        "ident": np.eye(128, dtype=np.float32),
    }
    per_core = []
    for c in range(NCORES):
        b, p = c // 2, c % 2
        tiles = [2 * j + p for j in range(NT)]
        idx = np.concatenate([np.arange(g * 128, g * 128 + 128) for g in tiles])
        xs = x[b, idx, :]
        d = dict(shared)
        d["x_tm"] = np.ascontiguousarray(xs.reshape(NT, 128 * D))
        d["x_fm"] = np.ascontiguousarray(xs.T.reshape(NC_D, 128 * TPC))
        per_core.append(d)
    return per_core


def _hperm():
    tiles = []
    for fc in range(21):
        for r in range(4):
            tiles.append(np.arange(2752 * r + 128 * fc, 2752 * r + 128 * fc + 128))
    tiles.append(np.concatenate([np.arange(2752 * 0 + 2688, 2752 * 0 + 2752),
                                 np.arange(2752 * 1 + 2688, 2752 * 1 + 2752)]))
    tiles.append(np.concatenate([np.arange(2752 * 2 + 2688, 2752 * 2 + 2752),
                                 np.arange(2752 * 3 + 2688, 2752 * 3 + 2752)]))
    return np.concatenate(tiles)


def host_prep_B(inputs):
    w1q = np.asarray(inputs["L0_ffn_w1q"], dtype=np.float32)
    a1 = np.asarray(inputs["L0_ffn_a1"], dtype=np.float32)
    b1 = np.asarray(inputs["L0_ffn_b1"], dtype=np.float32)
    w3q = np.asarray(inputs["L0_ffn_w3q"], dtype=np.float32)
    a3 = np.asarray(inputs["L0_ffn_a3"], dtype=np.float32)
    b3 = np.asarray(inputs["L0_ffn_b3"], dtype=np.float32)
    w2q = np.asarray(inputs["L0_ffn_w2q"], dtype=np.float32)
    a2 = np.asarray(inputs["L0_ffn_a2"], dtype=np.float32)
    b2 = np.asarray(inputs["L0_ffn_b2"], dtype=np.float32)

    hp = _hperm()
    d = {}
    d["w1t"] = _tile_lhst(np.ascontiguousarray(w1q.T[:, hp]), 8, 86)
    d["w3t"] = _tile_lhst(np.ascontiguousarray(w3q.T[:, hp]), 8, 86)
    w2p = np.zeros((2816, 4096), dtype=np.float32)
    w2p[:2752] = w2q.T
    d["w2t"] = _tile_rhs(w2p, 22, 8)
    d["a1t"] = np.ascontiguousarray(a1.reshape(2, 32, 128).transpose(2, 1, 0).reshape(128, 64))
    d["a3t"] = np.ascontiguousarray(a3.reshape(2, 32, 128).transpose(2, 1, 0).reshape(128, 64))
    a2p = a2[:, hp]
    d["a2t"] = np.ascontiguousarray(a2p.reshape(2, 86, 128).transpose(2, 1, 0).reshape(128, 172))
    d["b1T"] = np.ascontiguousarray(b1.T[:, hp])
    d["b3T"] = np.ascontiguousarray(b3.T[:, hp])
    d["b2T"] = np.ascontiguousarray(b2.T)
    f2 = np.zeros((128, 64), dtype=np.float32)
    for k in range(128):
        f2[k, k % 64] = 1.0
    d["fold2"] = f2
    return d


def declare_dram_B(nc, io, debug=False):
    ei = lambda n, shp, dt=F32: io.__setitem__(
        n, nc.dram_tensor(n, shp, dt, kind="ExternalInput").ap())
    ei("w1t", [86, 128 * 8 * 128])
    ei("w3t", [86, 128 * 8 * 128])
    ei("w2t", [8, 128 * 22 * 512])
    ei("a1t", [128, 64])
    ei("a3t", [128, 64])
    ei("a2t", [128, 172])
    ei("b1T", [2, 11008])
    ei("b3T", [2, 11008])
    ei("b2T", [2, 4096])
    ei("fold2", [128, 64])
    io["s1b"] = nc.dram_tensor("s1b", [NT, 128 * D], F32, kind="Internal").ap()
    if debug:
        io["_mkdbg"]("s1b", [NT, 128 * D], F32)
        io["dbg_hpfb"] = nc.dram_tensor(
            "dbg_hpfb", [22, 128 * 1024], BF16, kind="ExternalOutput").ap()
        io["dbg_small"] = nc.dram_tensor(
            "dbg_small", [2, 3 * 1024], F32, kind="ExternalOutput").ap()
        io["dbg_pr"] = nc.dram_tensor(
            "dbg_pr", [3, 128 * 1024], BF16, kind="ExternalOutput").ap()


def phase_B(nc, pl, st, io, sems, glob, x_fm_key="x1_fm", xf_key="x1f_fm",
            out_key="s1b", fm_ready=0):
    ld, ld2, wt = sems["ld"], sems["ld2"], sems["wt"]
    pe, ev, ac = sems["pe"], sems["ev"], sems["ac"]
    sto = sems["sto"]
    wtP = [sems["wtA"], sems["wtB"]]

    sb = lambda n, shp, dt=BF16: st.enter_context(nc.sbuf_tensor(_uname(n), shp, dt))
    psa = lambda n, shp, dt=F32: st.enter_context(nc.psum_tensor(_uname(n), shp, dt))

    NF = 22

    xf_bf = sb("b_xf", [128, NT * 1024])
    hpfb = sb("b_hpfb", [128, NF * 1024])
    b2T = sb("b_b2T", [2, 4096])
    h1s = sb("b_h1s", [2, 1024])
    h3s = sb("b_h3s", [2, 1024])
    h2s = sb("b_h2s", [2, 1024])

    pl.op("sync", lambda s: s.dma_start(
        out=xf_bf[:, :].rearrange("p (c f) -> p c f", c=NT),
        in_=io[xf_key].rearrange("c (p f) -> p c f", p=128)),
        waits=[(sems["st"], fm_ready)], inc=(ld, 16))
    pl.op("gpsimd", lambda g: g.dma_start(out=b2T[:, :], in_=io["b2T"]),
          inc=(wt, 16))
    xf_tgt = ld.n
    pl.op("vector", lambda v: v.memset(
        hpfb[:, :].rearrange("p (c f) -> p c f", c=NF)[64:128, 21, :], 0.0),
        waits=[(ev, ev.n)], inc=(ev, 1))

    with ExitStack() as st2:
        x1fm = st2.enter_context(nc.sbuf_tensor("b_x1fm", [128, NC_D * 1024], BF16))
        pl.fence(sems)
        a1t = st2.enter_context(nc.sbuf_tensor("b_a1t", [128, 64], BF16))
        a3t = st2.enter_context(nc.sbuf_tensor("b_a3t", [128, 64], BF16))
        ps_h = st2.enter_context(nc.psum_tensor("b_psh", [2, 2048], F32))
        pl.op("sync", lambda s: s.dma_start(
            out=x1fm[:, :].rearrange("p (c f) -> p c f", c=NC_D),
            in_=io[x_fm_key].rearrange("c (p f) -> p c f", p=128)),
            waits=[(sems["st"], fm_ready)], inc=(ld, 16))
        pl.op("gpsimd", lambda g: g.dma_start(out=a1t[:, :], in_=io["a1t"]),
              inc=(wt, 16))
        pl.op("gpsimd", lambda g: g.dma_start(out=a3t[:, :], in_=io["a3t"]),
              inc=(wt, 16))
        lw = [(ld, ld.n), (wt, wt.n), (ev, ev.n), (ac, ac.n)]
        for at, off in ((a1t, 0), (a3t, 1024)):
            for kc in range(NC_D):
                for h2i in range(2):
                    pl.op("tensor", lambda t, kc=kc, h2i=h2i, at=at, off=off: t.matmul(
                        ps_h[:, off + h2i * 512: off + (h2i + 1) * 512],
                        at[:, kc * 2:kc * 2 + 2],
                        x1fm[:, kc * 1024 + h2i * 512: kc * 1024 + (h2i + 1) * 512],
                        start=(kc == 0), stop=(kc == 31)),
                        waits=lw if (kc == 0 and h2i == 0 and off == 0) else (),
                        inc=(pe, 1) if (kc == 31 and h2i == 1) else None)
        h_tgt = pe.n
        pl.op("vector", lambda v: v.tensor_scalar(
            h1s[:, :], ps_h[:, 0:1024], 0.5, None, ALU.mult, ALU.bypass),
            waits=[(pe, h_tgt), (ev, ev.n)], inc=(ev, 1))
        pl.op("vector", lambda v: v.tensor_scalar(
            h3s[:, :], ps_h[:, 1024:2048], 0.5, None, ALU.mult, ALU.bypass),
            waits=[(ev, ev.n)], inc=(ev, 1))
        h13_done = ev.n

    with ExitStack() as st1:
        sb1 = lambda n, shp, dt=BF16: st1.enter_context(nc.sbuf_tensor(_uname(n), shp, dt))
        ps1 = lambda n, shp: st1.enter_context(nc.psum_tensor(_uname(n), shp, F32))
        b1T = sb1("b_b1T", [2, 11008])
        b3T = sb1("b_b3T", [2, 11008])
        a2t = sb1("b_a2t", [128, 172])
        eyef = sb1("b_eye", [128, 128])
        fold2 = sb1("b_fold2", [128, 64])
        w1sb = [sb1(f"b_w1_{i}", [128, 8 * 128]) for i in range(2)]
        w3sb = [sb1(f"b_w3_{i}", [128, 8 * 128]) for i in range(2)]
        gstg = [sb1(f"b_g{i}", [128, 512]) for i in range(2)]
        ustg = [sb1(f"b_u{i}", [128, 512]) for i in range(2)]
        prod = [sb1(f"b_p{i}", [128, 512]) for i in range(2)]
        ps_g = [ps1(f"b_psg{i}", [128, 512]) for i in range(2)]
        ps_u = [ps1(f"b_psu{i}", [128, 512]) for i in range(2)]
        ps_h2 = ps1("b_psh2", [2, 1024])
        ps_f = ps1("b_psf", [128, 1024])

        pl.fence(sems)
        first_small = True
        for nm, dst in (("b1T", b1T), ("b3T", b3T), ("a2t", a2t),
                        ("fold2", fold2), ("ident", eyef)):
            pl.op("gpsimd", lambda g, nm=nm, dst=dst: g.dma_start(
                out=dst[:, :], in_=io[nm]),
                waits=[(ev, h13_done)] if first_small else (), inc=(wt, 16))
            first_small = False
        smalls_tgt = wt.n

        w_tgt = [0, 0]
        prod_pe = {0: pe.n, 1: pe.n}
        stg_ac = {0: ac.n, 1: ac.n}
        hpfb_evs = []
        mi = 0
        for fc in range(NF):
            nr = 4 if fc < 21 else 2
            for r in range(nr):
                mh = fc * 4 + r if fc < 21 else 84 + r
                buf = mi % 2
                lw = [(pe, prod_pe[buf])]
                pl.op("gpsimd", lambda g, mh=mh, buf=buf: g.dma_start(
                    out=w1sb[buf][:, :], in_=row128(io["w1t"], mh)),
                    waits=lw, inc=(wtP[buf], 16))
                pl.op("gpsimd", lambda g, mh=mh, buf=buf: g.dma_start(
                    out=w3sb[buf][:, :], in_=row128(io["w3t"], mh)),
                    inc=(wtP[buf], 16))
                w_tgt[buf] = wtP[buf].n
                for th in range(2):
                    mw = [(wtP[buf], w_tgt[buf]), (wt, wt.n), (ld, xf_tgt),
                          (ev, h13_done), (ac, stg_ac[buf])]
                    for wsb, hs, bT, pp in ((w1sb, h1s, b1T, ps_g),
                                            (w3sb, h3s, b3T, ps_u)):
                        for kc in range(NT):
                            pl.op("tensor",
                                  lambda t, kc=kc, th=th, buf=buf, wsb=wsb, pp=pp: t.matmul(
                                      pp[buf][:, :],
                                      wsb[buf][:, kc * 128:(kc + 1) * 128],
                                      xf_bf[:, kc * 1024 + th * 512: kc * 1024 + th * 512 + 512],
                                      start=(kc == 0), stop=False),
                                  waits=mw if (kc == 0 and wsb is w1sb) else ())
                        pl.op("tensor",
                              lambda t, mh=mh, th=th, buf=buf, bT=bT, hs=hs, pp=pp: t.matmul(
                                  pp[buf][:, :], bT[:, mh * 128:(mh + 1) * 128],
                                  hs[:, th * 512:th * 512 + 512],
                                  start=False, stop=True),
                              inc=(pe, 1))
                    gu_tgt = pe.n
                    pl.op("scalar", lambda s, buf=buf: s.activation(
                        gstg[buf][:, :], ps_g[buf][:, :], AF.Silu),
                        waits=[(pe, gu_tgt), (ev, ev.n), (ac, ac.n)], inc=(ac, 1))
                    pl.op("scalar", lambda s, buf=buf: s.activation(
                        ustg[buf][:, :], ps_u[buf][:, :], AF.Copy),
                        waits=[(ac, ac.n)], inc=(ac, 1))
                    stg_ac[buf] = ac.n
                    pl.op("vector", lambda v, buf=buf: v.tensor_mul(
                        prod[buf][:, :], gstg[buf][:, :], ustg[buf][:, :]),
                        waits=[(ac, ac.n), (ev, ev.n), (pe, pe.n)], inc=(ev, 1))
                    prod_tgt = ev.n
                    if "dbg_pr" in io and mi == 0:
                        pl.op("sync", lambda s, th=th, buf=buf: s.dma_start(
                            out=row128(io["dbg_pr"], 0)[:, th * 512:th * 512 + 512],
                            in_=prod[buf][:, :]),
                            waits=[(ev, prod_tgt)], inc=(ld, 16))
                        pl.op("sync", lambda s, th=th, buf=buf: s.dma_start(
                            out=row128(io["dbg_pr"], 1)[:, th * 512:th * 512 + 512],
                            in_=gstg[buf][:, :]), inc=(ld, 16))
                        pl.op("sync", lambda s, th=th, buf=buf: s.dma_start(
                            out=row128(io["dbg_pr"], 2)[:, th * 512:th * 512 + 512],
                            in_=ustg[buf][:, :]), inc=(ld, 16))
                    pl.op("tensor", lambda t, mh=mh, th=th, buf=buf: t.matmul(
                        ps_h2[:, th * 512:th * 512 + 512],
                        a2t[:, mh * 2:mh * 2 + 2], prod[buf][:, :],
                        start=(mi == 0), stop=(mi == 85),
                        skip_group_check=True),
                        waits=[(ev, prod_tgt)], inc=(pe, 1))
                    fmat = eyef if fc < 21 else fold2
                    rows = 128 if fc < 21 else 64
                    fold_start = (r == 0)
                    fold_stop = (r == nr - 1)
                    fw = [(ev, prod_tgt)]
                    if fold_start and hpfb_evs:
                        fw.append((ev, hpfb_evs[-1]))
                    pl.op("tensor", lambda t, th=th, buf=buf, fmat=fmat, rows=rows,
                          fold_start=fold_start, fold_stop=fold_stop: t.matmul(
                              ps_f[0:rows, th * 512:th * 512 + 512],
                              fmat[:, :], prod[buf][:, :],
                              start=fold_start, stop=fold_stop,
                              skip_group_check=True),
                          waits=fw, inc=(pe, 1))
                    prod_pe[buf] = pe.n
                if (fc < 21 and r == 3) or (fc == 21 and r == nr - 1):
                    f_tgt = pe.n
                    rows = 128 if fc < 21 else 64
                    pl.op("vector", lambda v, fc=fc, rows=rows: v.tensor_copy(
                        hpfb[0:rows, fc * 1024:(fc + 1) * 1024],
                        ps_f[0:rows, :]),
                        waits=[(pe, f_tgt), (ev, ev.n)], inc=(ev, 1))
                    hpfb_evs.append(ev.n)
                mi += 1
        b1_done_pe = pe.n
        pl.op("vector", lambda v: v.tensor_scalar(
            h2s[:, :], ps_h2[:, :], 0.5, None, ALU.mult, ALU.bypass),
            waits=[(pe, b1_done_pe), (ev, ev.n)], inc=(ev, 1))
        hpfb_done = ev.n
        if "dbg_hpfb" in io:
            for c in range(NF):
                pl.op("sync", lambda s, c=c: s.dma_start(
                    out=row128(io["dbg_hpfb"], c),
                    in_=hpfb[:, c * 1024:(c + 1) * 1024]),
                    waits=[(ev, hpfb_done)] if c == 0 else (), inc=(ld, 16))
            pl.op("gpsimd", lambda g: g.dma_start(
                out=io["dbg_small"][:, 0:1024], in_=h1s[:, :]),
                waits=[(ev, hpfb_done)], inc=(wt, 16))
            pl.op("gpsimd", lambda g: g.dma_start(
                out=io["dbg_small"][:, 1024:2048], in_=h3s[:, :]), inc=(wt, 16))
            pl.op("gpsimd", lambda g: g.dma_start(
                out=io["dbg_small"][:, 2048:3072], in_=h2s[:, :]), inc=(wt, 16))

    with ExitStack() as st3:
        sb3 = lambda n, shp, dt=BF16: st3.enter_context(nc.sbuf_tensor(_uname(n), shp, dt))
        ps3 = lambda n, shp: st3.enter_context(nc.psum_tensor(_uname(n), shp, F32))
        w2sb = [sb3(f"b_w2_{i}", [128, 22 * 512]) for i in range(2)]
        ostg = [sb3(f"b_o{i}", [128, 512], F32) for i in range(4)]
        pl.fence(sems)
        ps_o = [ps3(f"b_pso{i}", [128, 512]) for i in range(4)]
        w2_tgt = [0, 0]
        pe_b2_0 = pe.n
        for obi in range(8):
            buf = obi % 2
            lw = [(ev, hpfb_done)] if obi < 2 else [(pe, max(pe_b2_0, pe.n - 8))]
            pl.op("gpsimd", lambda g, obi=obi, buf=buf: g.dma_start(
                out=w2sb[buf][:, :], in_=row128(io["w2t"], obi)), waits=lw,
                inc=(wtP[buf], 16))
            w2_tgt[buf] = wtP[buf].n
            for jt in range(NT):
                pp = ps_o[jt % 4]
                mmw = [(wtP[buf], w2_tgt[buf]), (ev, hpfb_done), (ev, ev.n)]
                for kc in range(NF):
                    pl.op("tensor", lambda t, jt=jt, kc=kc, pp=pp, buf=buf: t.matmul(
                        pp[:, :],
                        hpfb[:, kc * 1024 + jt * 128: kc * 1024 + (jt + 1) * 128],
                        w2sb[buf][:, kc * 512:(kc + 1) * 512],
                        start=(kc == 0), stop=False),
                        waits=mmw if kc == 0 else ())
                pl.op("tensor", lambda t, jt=jt, obi=obi, pp=pp: t.matmul(
                    pp[:, :], h2s[:, jt * 128:(jt + 1) * 128],
                    b2T[:, obi * 512:(obi + 1) * 512], start=False, stop=True),
                    inc=(pe, 1))
                mm_tgt = pe.n
                dst = ostg[jt % 4]
                pl.op("vector", lambda v, pp=pp, dst=dst: v.tensor_copy(
                    dst[:, :], pp[:, :]),
                    waits=[(pe, mm_tgt), (sto, sto.n), (ev, ev.n)], inc=(ev, 1))
                ev_tgt = ev.n
                pl.op("sync", lambda s, jt=jt, obi=obi, dst=dst: s.dma_start(
                    out=row128(io[out_key], jt)[:, obi * 512:(obi + 1) * 512],
                    in_=dst[:, :]), waits=[(ev, ev_tgt)], inc=(sto, 16))
    return sto.n



H_L1 = 32
NOPE, ROPE, VD = 128, 64, 128
QKD = NOPE + ROPE
Q_RANK, KV_RANK = 1536, 512
IDX_H, IDX_D, TOPK = 16, 128, 1024
SC_QK = float(QKD ** -0.5)
SC_IDX = float((IDX_H ** -0.5) * (IDX_D ** -0.5))
NEG = -1e30


def host_prep_C(inputs, core=None):
    q_a_w = np.asarray(inputs["L1_attn_q_a_w"], np.float32)
    q_a_nw = np.asarray(inputs["L1_attn_q_a_norm_w"], np.float32)
    q_b_w = np.asarray(inputs["L1_attn_q_b_w"], np.float32)
    kv_a_w = np.asarray(inputs["L1_attn_kv_a_w"], np.float32)
    kv_a_nw = np.asarray(inputs["L1_attn_kv_a_norm_w"], np.float32)
    kv_b_w = np.asarray(inputs["L1_attn_kv_b_w"], np.float32)
    o_w = np.asarray(inputs["L1_attn_o_w"], np.float32)
    idx_wq = np.asarray(inputs["L1_attn_idx_wq_w"], np.float32)
    idx_wk = np.asarray(inputs["L1_attn_idx_wk_w"], np.float32)
    idx_ww = np.asarray(inputs["L1_attn_idx_weights_w"], np.float32)

    d = {}
    d["q_a_t"] = _tile_rhs(q_a_w, 32, 3)
    d["qnw"] = np.ascontiguousarray(q_a_nw.reshape(12, 128).T)
    cols_n = np.concatenate([np.arange(h * 192, h * 192 + 128) for h in range(32)])
    cols_r = np.concatenate([np.arange(h * 192 + 128, (h + 1) * 192) for h in range(32)])
    d["qb_n_t"] = _tile_lhst(q_b_w[:, cols_n], 12, 32)
    d["qb_r_t"] = _tile_lhst(q_b_w[:, cols_r], 12, 16)
    d["kv_a_t"] = np.ascontiguousarray(
        kv_a_w.reshape(32, 128, 576).transpose(1, 0, 2).reshape(128, 32 * 576))
    d["kvnw"] = np.ascontiguousarray(kv_a_nw.reshape(4, 128).T)
    cols_kn = np.concatenate([np.arange(h * 256, h * 256 + 128) for h in range(32)])
    cols_v = np.concatenate([np.arange(h * 256 + 128, (h + 1) * 256) for h in range(32)])
    d["kv_b_n_t"] = _tile_lhst(kv_b_w[:, cols_kn], 4, 32)
    d["kv_b_v_t"] = np.ascontiguousarray(
        kv_b_w[:, cols_v].reshape(4, 128, 4096).transpose(1, 0, 2).reshape(128, 4 * 4096))
    d["o_w_t"] = _tile_rhs(o_w, 32, 8)
    d["idx_wq_t"] = _tile_lhst(idx_wq, 32, 16)
    d["idx_wk_t"] = np.ascontiguousarray(
        idx_wk.reshape(32, 128, 128).transpose(1, 0, 2).reshape(128, 32 * 128))
    d["idx_ww_t"] = np.ascontiguousarray(
        idx_ww.reshape(32, 128, 16).transpose(1, 0, 2).reshape(128, 32 * 16))
    R = np.zeros((64, 64), np.float32)
    for f in range(32):
        R[2 * f + 1, 2 * f] = -1.0
        R[2 * f, 2 * f + 1] = 1.0
    R128 = np.zeros((128, 128), np.float32)
    R128[:64, :64] = R
    R128[64:, 64:] = R
    d["R128"] = R128
    d["zeros_t"] = np.zeros((128, D), np.float32)
    return d


def host_prep_C_core(c):
    p = c % 2
    tiles = [2 * j + p for j in range(NT)]
    pos = np.concatenate([np.arange(g * 128, g * 128 + 128) for g in tiles]).astype(np.float64)
    inv = 1.0 / (10000.0 ** (np.arange(0, ROPE, 2, dtype=np.float64) / ROPE))
    ang = pos[None, :] * inv[:, None]
    cosq = np.cos(ang).astype(np.float32)
    sinq = np.sin(ang).astype(np.float32)
    cqi = np.repeat(cosq, 2, axis=0)
    sqi = np.repeat(sinq, 2, axis=0)
    d = {}
    d["cq_t"] = np.ascontiguousarray(np.concatenate([cqi, cqi], axis=0))
    d["sq_t"] = np.ascontiguousarray(np.concatenate([sqi, sqi], axis=0))
    ctm = np.repeat(cosq.T, 2, axis=1)
    stm = np.repeat(sinq.T, 2, axis=1)
    d["ck_t"] = np.ascontiguousarray(ctm.reshape(NT, 128 * ROPE))
    d["sk_t"] = np.ascontiguousarray(stm.reshape(NT, 128 * ROPE))
    cm = np.zeros((NT, 128, 256), np.float32)
    for j in range(NT):
        gq = 2 * j + p
        qpos = np.arange(gq * 128, gq * 128 + 128)
        kpos = np.arange(2 * j * 128, 2 * j * 128 + 256)
        cm[j] = np.where(qpos[:, None] >= kpos[None, :], 0.0, NEG)
    d["cmask"] = np.ascontiguousarray(cm.reshape(NT, 128 * 256))
    return d


def declare_dram_C(nc, io, debug=False):
    ei = lambda n, shp, dt=F32: io.__setitem__(
        n, nc.dram_tensor(n, shp, dt, kind="ExternalInput").ap())
    internal = lambda n, shp, dt=BF16: io.__setitem__(
        n, nc.dram_tensor(n, shp, dt, kind="Internal").ap())
    ei("q_a_t", [3, 128 * 32 * 512])
    ei("qnw", [128, 12])
    ei("qb_n_t", [32, 128 * 12 * 128])
    ei("qb_r_t", [16, 128 * 12 * 128])
    ei("kv_a_t", [128, 32 * 576])
    ei("kvnw", [128, 4])
    ei("kv_b_n_t", [32, 128 * 4 * 128])
    ei("kv_b_v_t", [128, 4 * 4096])
    ei("o_w_t", [8, 128 * 32 * 512])
    ei("idx_wq_t", [16, 128 * 32 * 128])
    ei("idx_wk_t", [128, 32 * 128])
    ei("idx_ww_t", [128, 32 * 16])
    ei("R128", [128, 128])
    ei("zeros_t", [128, D])
    ei("cq_t", [128, TPC])
    ei("sq_t", [128, TPC])
    ei("ck_t", [NT, 128 * ROPE])
    ei("sk_t", [NT, 128 * ROPE])
    ei("cmask", [NT, 128 * 256])
    internal("qn_dr", [32, 128 * 1024])
    internal("qr_dr", [16, 128 * 1024])
    internal("kn_dr", [32, 128 * 2048])
    internal("v_dr", [16, 128 * 4096])
    internal("cx_src", [NT, 128 * 704])
    internal("cx_dst", [2 * NT, 128 * 704])
    internal("s1c", [NT, 128 * D], F32)
    if debug:
        io["_mkdbg"]("s1c", [NT, 128 * D], F32)
        io["_mkdbg"]("qn_dr", [32, 128 * 1024], BF16)
        io["_mkdbg"]("qr_dr", [16, 128 * 1024], BF16)
        io["_mkdbg"]("kn_dr", [32, 128 * 2048], BF16)
        io["_mkdbg"]("v_dr", [16, 128 * 4096], BF16)
        io["dbg_m01"] = nc.dram_tensor(
            "dbg_m01", [NT, 128 * 2048], BF16, kind="ExternalOutput").ap()
        io["dbg_cidx"] = nc.dram_tensor(
            "dbg_cidx", [NT, 128 * 2048], F32, kind="ExternalOutput").ap()


def phase_C(nc, pl, st, io, sems, glob, x_fm_key="x2_fm", out_key="s1c",
            fm_ready=0):
    ld, ld2, wt = sems["ld"], sems["ld2"], sems["wt"]
    pe, ev, ac, cc, sto = (sems["pe"], sems["ev"], sems["ac"], sems["cc"],
                           sems["sto"])
    lh = [sems["lh0"], sems["lh1"]]
    lv = [sems["lv0"], sems["lv1"]]
    wtP = [sems["wtA"], sems["wtB"]]
    ident = glob["ident"]
    eps = glob["eps"]

    sb = lambda n, shp, dt=BF16: st.enter_context(nc.sbuf_tensor(_uname(n), shp, dt))

    kpe_fm = sb("c_kpe", [64, 2048])

    def vchain(fn, extra=()):
        pl.op("vector", fn, waits=[(ev, ev.n)] + list(extra), inc=(ev, 1))

    with ExitStack() as st1:
        sb1 = lambda n, shp, dt=BF16: st1.enter_context(nc.sbuf_tensor(_uname(n), shp, dt))
        x2fm = sb1("c_x2fm", [128, NC_D * 1024])
        ik_fm = sb1("c_ikfm", [128, 2048])
        pl.op("sync", lambda s: s.dma_start(
            out=x2fm[:, :].rearrange("p (c f) -> p c f", c=NC_D),
            in_=io[x_fm_key].rearrange("c (p f) -> p c f", p=128)),
            waits=[(sems["st"], fm_ready)], inc=(ld, 16))
        x2_tgt = ld.n

        pl.fence(sems)
        st12 = st1.enter_context(ExitStack())
        qlat_fm = st12.enter_context(
            nc.sbuf_tensor("c_qlatfm", [128, 12 * 1024], BF16))
        with ExitStack() as st2:
            sb2 = lambda n, shp, dt=BF16: st2.enter_context(nc.sbuf_tensor(_uname(n), shp, dt))
            ps2 = lambda n, shp: st2.enter_context(nc.psum_tensor(_uname(n), shp, F32))
            qa0 = sb2("c_qa0", [128, 32 * 512])
            qa = [qa0, qa0]
            qlat_tm = sb2("c_qlat_tm", [128, NT * 1536], F32)
            qnw = sb2("c_qnw", [128, 12], F32)
            qlat_bf = sb2("c_qlatbf", [128, 1536])
            c1dmp = sb2("c_dmp", [128, 1536], F32)
            c1ss = sb2("c_ss", [128, 1], F32)
            c1rstd = sb2("c_rstd", [128, 1], F32)
            ps_q = [ps2(f"c_psq{i}", [128, 512]) for i in range(2)]
            ps_t = ps2("c_pst", [128, 512])

            pl.op("sync", lambda s: s.dma_start(out=qnw[:, :], in_=io["qnw"]),
                  inc=(ld, 16))
            qnw_tgt = ld.n
            for ob3 in range(3):
                buf = 0
                lw = [(pe, pe.n)]
                pl.op("gpsimd", lambda g, ob3=ob3, buf=buf: g.dma_start(
                    out=qa[buf][:, :], in_=row128(io["q_a_t"], ob3)),
                    waits=lw, inc=(wtP[buf], 16))
                w_tgt = wtP[buf].n
                for jt in range(NT):
                    pp = ps_q[jt % 2]
                    mw = [(wtP[0], w_tgt), (ld, ld.n), (ev, ev.n)]
                    for kc in range(NC_D):
                        pl.op("tensor", lambda t, kc=kc, jt=jt, buf=buf, pp=pp: t.matmul(
                            pp[:, :],
                            x2fm[:, kc * 1024 + jt * 128: kc * 1024 + (jt + 1) * 128],
                            qa[buf][:, kc * 512:(kc + 1) * 512],
                            start=(kc == 0), stop=(kc == 31)),
                            waits=mw if kc == 0 else (),
                            inc=(pe, 1) if kc == 31 else None)
                    mm_tgt = pe.n
                    pl.op("vector", lambda v, jt=jt, ob3=ob3, pp=pp: v.tensor_copy(
                        qlat_tm[:, jt * 1536 + ob3 * 512: jt * 1536 + (ob3 + 1) * 512],
                        pp[:, :]), waits=[(pe, mm_tgt), (ev, ev.n)], inc=(ev, 1))
            for jt in range(NT):
                qsl = qlat_tm[:, jt * 1536:(jt + 1) * 1536]
                vchain(lambda v, qsl=qsl: v.scalar_tensor_tensor(
                    c1dmp[:, :], qsl, 1.0, qsl, ALU.mult, ALU.mult,
                    accum_out=c1ss[:, :]))
                vchain(lambda v: v.tensor_scalar(
                    c1ss[:, :], c1ss[:, :], 1.0 / Q_RANK, None, ALU.mult, ALU.bypass))
                ss_tgt = ev.n
                pl.op("scalar", lambda s: s.activation(
                    c1rstd[:, :], c1ss[:, :], AF.Ln, bias=eps[:, :]),
                    waits=[(ev, ss_tgt), (ac, ac.n)], inc=(ac, 1))
                pl.op("scalar", lambda s: s.activation(
                    c1rstd[:, :], c1rstd[:, :], AF.Exp, scale=-0.5),
                    waits=[(ac, ac.n)], inc=(ac, 1))
                r_tgt = ac.n
                vchain(lambda v, qsl=qsl: v.tensor_scalar(
                    qlat_bf[:, :], qsl, c1rstd[:, :], None, ALU.mult, ALU.bypass),
                    extra=[(ac, r_tgt), (pe, pe.n)])
                cast_tgt = ev.n
                for c2 in range(12):
                    half = c2 % 2
                    pl.op("tensor", lambda t, c2=c2, half=half: t.matmul(
                        ps_t[:, (c2 % 4) * 128:(c2 % 4) * 128 + 128],
                        qlat_bf[:, c2 * 128:(c2 + 1) * 128], ident[:, :],
                        start=True, stop=True),
                        waits=[(ev, cast_tgt), (ev, ev.n)] if c2 % 4 == 0 else (),
                        inc=(pe, 1) if c2 % 4 == 3 else None)
                    if c2 % 4 == 3:
                        tr_tgt = pe.n
                        for q4 in range(4):
                            cc2 = c2 - 3 + q4
                            pl.op("vector", lambda v, cc2=cc2, q4=q4, jt=jt: v.tensor_scalar(
                                qlat_fm[:, cc2 * 1024 + jt * 128: cc2 * 1024 + (jt + 1) * 128],
                                ps_t[:, q4 * 128:(q4 + 1) * 128],
                                qnw[:, cc2:cc2 + 1], None, ALU.mult, ALU.bypass),
                                waits=[(pe, tr_tgt), (ld, qnw_tgt)] if q4 == 0 else (),
                                inc=(ev, 1) if q4 == 3 else None)
            qlat_done = ev.n

        pl.fence(sems)
        with ExitStack() as st2:
            sb2 = lambda n, shp, dt=BF16: st2.enter_context(nc.sbuf_tensor(_uname(n), shp, dt))
            ps2 = lambda n, shp: st2.enter_context(nc.psum_tensor(_uname(n), shp, F32))
            qb = [sb2(f"c_qb{i}", [128, 12 * 128]) for i in range(2)]
            qstg = [sb2(f"c_qstg{i}", [128, 1024]) for i in range(2)]
            r128 = sb2("c_r128", [128, 128])
            cqt = sb2("c_cqt", [128, 1024], F32)
            sqt = sb2("c_sqt", [128, 1024], F32)
            qr_f = sb2("c_qrf", [128, 1024], F32)
            qr_b = sb2("c_qrb", [128, 1024])
            t1b = sb2("c_t1", [128, 1024], F32)
            t2b = sb2("c_t2", [128, 1024], F32)
            ps_m = [ps2(f"c_psm{i}", [128, 1024]) for i in range(2)]
            ps_rot = ps2("c_psrot", [128, 1024])

            pl.op("gpsimd", lambda g: g.dma_start(out=r128[:, :], in_=io["R128"]),
                  inc=(wt, 16))
            pl.op("sync", lambda s: s.dma_start(out=cqt[:, :], in_=io["cq_t"]),
                  inc=(ld, 16))
            pl.op("sync", lambda s: s.dma_start(out=sqt[:, :], in_=io["sq_t"]),
                  inc=(ld, 16))
            tab_tgt = ld.n

            for kind in range(2):
                nm = 32 if kind == 0 else 16
                wsrc = "qb_n_t" if kind == 0 else "qb_r_t"
                for m in range(nm):
                    buf = m % 2
                    pl.op("gpsimd", lambda g, m=m, buf=buf, wsrc=wsrc: g.dma_start(
                        out=qb[buf][:, :], in_=row128(io[wsrc], m)),
                        waits=[(pe, pe.n)], inc=(wtP[buf], 16))
                    w_tgt = wtP[buf].n
                    pp = ps_m[buf]
                    mw = [(wtP[buf], w_tgt), (ev, qlat_done), (ac, ac.n), (ev, ev.n)]
                    for kc in range(12):
                        for h2 in range(2):
                            pl.op("tensor", lambda t, kc=kc, h2=h2, buf=buf, pp=pp: t.matmul(
                                pp[:, h2 * 512:(h2 + 1) * 512],
                                qb[buf][:, kc * 128:(kc + 1) * 128],
                                qlat_fm[:, kc * 1024 + h2 * 512: kc * 1024 + (h2 + 1) * 512],
                                start=(kc == 0), stop=(kc == 11)),
                                waits=mw if (kc == 0 and h2 == 0) else (),
                                inc=(pe, 1) if (kc == 11 and h2 == 1) else None)
                    mm_tgt = pe.n
                    if kind == 0:
                        dst = qstg[buf]
                        pl.op("vector", lambda v, pp=pp, dst=dst: v.tensor_scalar(
                            dst[:, :], pp[:, :], SC_QK, None, ALU.mult, ALU.bypass),
                            waits=[(pe, mm_tgt), (sto, sto.n), (ev, ev.n)], inc=(ev, 1))
                        e_tgt = ev.n
                        pl.op("sync", lambda s, m=m, dst=dst: s.dma_start(
                            out=row128(io["qn_dr"], m), in_=dst[:, :]),
                            waits=[(ev, e_tgt)], inc=(sto, 16))
                    else:
                        vchain(lambda v, pp=pp: v.tensor_scalar(
                            qr_f[:, :], pp[:, :], SC_QK, None, ALU.mult, ALU.bypass),
                            extra=[(pe, mm_tgt)])
                        vchain(lambda v: v.tensor_copy(qr_b[:, :], qr_f[:, :]))
                        qrb_tgt = ev.n
                        pl.op("tensor", lambda t: t.matmul(
                            ps_rot[:, 0:512], r128[:, :], qr_b[:, 0:512],
                            start=True, stop=True),
                            waits=[(ev, qrb_tgt), (wt, wt.n)])
                        pl.op("tensor", lambda t: t.matmul(
                            ps_rot[:, 512:1024], r128[:, :], qr_b[:, 512:1024],
                            start=True, stop=True), inc=(pe, 1))
                        rot_tgt = pe.n
                        vchain(lambda v: v.tensor_mul(
                            t2b[:, :], ps_rot[:, :], sqt[:, :]),
                            extra=[(pe, rot_tgt), (ld, tab_tgt)])
                        vchain(lambda v: v.tensor_mul(t1b[:, :], qr_f[:, :], cqt[:, :]))
                        dst = qstg[buf]
                        vchain(lambda v, dst=dst: v.tensor_add(
                            dst[:, :], t1b[:, :], t2b[:, :]),
                            extra=[(sto, sto.n)])
                        e_tgt = ev.n
                        pl.op("sync", lambda s, m=m, dst=dst: s.dma_start(
                            out=row128(io["qr_dr"], m), in_=dst[:, :]),
                            waits=[(ev, e_tgt)], inc=(sto, 16))

        st12.close()

        pl.fence(sems)
        with ExitStack() as st2:
            sb2 = lambda n, shp, dt=BF16: st2.enter_context(nc.sbuf_tensor(_uname(n), shp, dt))
            ps2 = lambda n, shp: st2.enter_context(nc.psum_tensor(_uname(n), shp, F32))
            kva = sb2("c_kva", [128, 32 * 576])
            wk = sb2("c_wk", [128, 32 * 128])
            ckv_f = sb2("c_ckvf", [128, 512], F32)
            kpe_f = sb2("c_kpef", [128, 64], F32)
            rot_f = sb2("c_rotf", [128, 64], F32)
            t1k = sb2("c_t1k", [128, 64], F32)
            ckt = sb2("c_ckt", [128, 64], F32)
            skt = sb2("c_skt", [128, 64], F32)
            cx_stg = [sb2(f"c_cxs{i}", [128, 704]) for i in range(2)]
            c3dmp = sb2("c_dmp2", [128, 512], F32)
            c3ss = sb2("c_ss2", [128, 1], F32)
            c3rstd = sb2("c_rstd2", [128, 1], F32)
            ps_c3 = [ps2(f"c_psc{i}", [128, 576]) for i in range(2)]
            ps_ik = ps2("c_psik", [128, 128])

            pl.op("gpsimd", lambda g: g.dma_start(out=kva[:, :], in_=io["kv_a_t"]),
                  inc=(wt, 16))
            pl.op("gpsimd", lambda g: g.dma_start(out=wk[:, :], in_=io["idx_wk_t"]),
                  inc=(wt, 16))
            kva_tgt = wt.n
            for jt in range(NT):
                buf = jt % 2
                pp = ps_c3[buf]
                pl.op("sync", lambda s, jt=jt: s.dma_start(
                    out=ckt[:, :], in_=row128(io["ck_t"], jt)),
                    waits=[(ev, ev.n)], inc=(ld, 16))
                pl.op("sync", lambda s, jt=jt: s.dma_start(
                    out=skt[:, :], in_=row128(io["sk_t"], jt)), inc=(ld, 16))
                tab_tgt = ld.n
                mw = [(wt, kva_tgt), (ld, ld.n), (ev, ev.n), (ac, ac.n)]
                for kc in range(NC_D):
                    for sl, off, w in ((0, 0, 512), (1, 512, 64)):
                        pl.op("tensor", lambda t, kc=kc, jt=jt, pp=pp, off=off, w=w: t.matmul(
                            pp[:, off:off + w],
                            x2fm[:, kc * 1024 + jt * 128: kc * 1024 + (jt + 1) * 128],
                            kva[:, kc * 576 + off: kc * 576 + off + w],
                            start=(kc == 0), stop=(kc == 31)),
                            waits=mw if (kc == 0 and sl == 0) else (),
                            inc=(pe, 1) if (kc == 31 and sl == 1) else None)
                for kc in range(NC_D):
                    pl.op("tensor", lambda t, kc=kc, jt=jt: t.matmul(
                        ps_ik[:, :],
                        x2fm[:, kc * 1024 + jt * 128: kc * 1024 + (jt + 1) * 128],
                        wk[:, kc * 128:(kc + 1) * 128],
                        start=(kc == 0), stop=(kc == 31)),
                        inc=(pe, 1) if kc == 31 else None)
                mm_tgt = pe.n
                vchain(lambda v, pp=pp: v.tensor_copy(ckv_f[:, :], pp[:, 0:512]),
                       extra=[(pe, mm_tgt), (sto, sto.n)])
                vchain(lambda v: v.scalar_tensor_tensor(
                    c3dmp[:, :], ckv_f[:, :], 1.0, ckv_f[:, :], ALU.mult, ALU.mult,
                    accum_out=c3ss[:, :]))
                vchain(lambda v: v.tensor_scalar(
                    c3ss[:, :], c3ss[:, :], 1.0 / KV_RANK, None, ALU.mult, ALU.bypass))
                ss_tgt = ev.n
                pl.op("scalar", lambda s: s.activation(
                    c3rstd[:, :], c3ss[:, :], AF.Ln, bias=eps[:, :]),
                    waits=[(ev, ss_tgt), (ac, ac.n)], inc=(ac, 1))
                pl.op("scalar", lambda s: s.activation(
                    c3rstd[:, :], c3rstd[:, :], AF.Exp, scale=-0.5),
                    waits=[(ac, ac.n)], inc=(ac, 1))
                r_tgt = ac.n
                cxs = cx_stg[buf]
                vchain(lambda v, cxs=cxs: v.tensor_scalar(
                    cxs[:, 0:512], ckv_f[:, :], c3rstd[:, :], None, ALU.mult, ALU.bypass),
                    extra=[(ac, r_tgt), (sto, sto.n)])
                vchain(lambda v, pp=pp: v.tensor_copy(kpe_f[:, :], pp[:, 512:576]))
                vchain(lambda v: v.tensor_scalar(
                    rot_f[:, :].rearrange("p (a b) -> p a b", b=2)[:, :, 0],
                    kpe_f[:, :].rearrange("p (a b) -> p a b", b=2)[:, :, 1],
                    -1.0, None, ALU.mult, ALU.bypass))
                vchain(lambda v: v.tensor_copy(
                    rot_f[:, :].rearrange("p (a b) -> p a b", b=2)[:, :, 1],
                    kpe_f[:, :].rearrange("p (a b) -> p a b", b=2)[:, :, 0]))
                vchain(lambda v: v.tensor_mul(t1k[:, :], kpe_f[:, :], ckt[:, :]),
                       extra=[(ld, tab_tgt)])
                vchain(lambda v: v.tensor_mul(rot_f[:, :], rot_f[:, :], skt[:, :]))
                vchain(lambda v, cxs=cxs: v.tensor_add(
                    cxs[:, 512:576], t1k[:, :], rot_f[:, :]))
                vchain(lambda v, cxs=cxs, pp=pp: v.tensor_copy(
                    cxs[:, 576:704], ps_ik[:, :]), extra=[(pe, mm_tgt)])
                e_tgt = ev.n
                pl.op("sync", lambda s, jt=jt, cxs=cxs: s.dma_start(
                    out=row128(io["cx_src"], jt), in_=cxs[:, :]),
                    waits=[(ev, e_tgt)], inc=(sto, 16))
            pl.op("gpsimd", lambda g: g.collective_compute(
                "AllGather", ALU.bypass, replica_groups=REPLICA_PAIRS,
                ins=[io["cx_src"].opt()], outs=[io["cx_dst"].opt()]),
                waits=[(sto, sto.n), (ld, ld.n)], inc=(cc, 1))
            cx_done = cc.n
            pl.wait("sync", cc, cx_done)
            pl.wait("scalar", cc, cx_done)

        pl.fence(sems)
        with ExitStack() as st2:
            sb2 = lambda n, shp, dt=BF16: st2.enter_context(nc.sbuf_tensor(_uname(n), shp, dt))
            ps2 = lambda n, shp: st2.enter_context(nc.psum_tensor(_uname(n), shp, F32))
            cx_t = [sb2(f"c_cxt{i}", [128, 704]) for i in range(2)]
            kvlat_fm = sb2("c_kvlat", [128, 4 * 2048])
            kvnw = sb2("c_kvnw", [128, 4], F32)
            kbn = [sb2(f"c_kbn{i}", [128, 4 * 128]) for i in range(2)]
            kbv = sb2("c_kbv", [128, 4 * 4096])
            knstg = [sb2(f"c_knst{i}", [128, 2048]) for i in range(2)]
            vstg = [sb2(f"c_vst{i}", [128, 512]) for i in range(2)]
            ps_t4 = ps2("c_pst4", [128, 512])
            ps_kn = ps2("c_pskn", [128, 2048])
            ps_v = [ps2(f"c_psv{i}", [128, 512]) for i in range(2)]

            pl.op("sync", lambda s: s.dma_start(out=kvnw[:, :], in_=io["kvnw"]),
                  waits=[(cc, cx_done)], inc=(ld, 16))
            kvnw_tgt = ld.n
            for g16 in range(16):
                r, j = g16 % 2, g16 // 2
                buf = g16 % 2
                pl.op("sync", lambda s, r=r, j=j, buf=buf: s.dma_start(
                    out=cx_t[buf][:, :], in_=row128(io["cx_dst"], r * 8 + j)),
                    waits=[(cc, cx_done), (pe, pe.n)], inc=(lh[buf], 16))
                c_tgt = lh[buf].n
                for c4 in range(4):
                    pl.op("tensor", lambda t, c4=c4, buf=buf: t.matmul(
                        ps_t4[:, (c4 % 4) * 128:(c4 % 4) * 128 + 128],
                        cx_t[buf][:, c4 * 128:(c4 + 1) * 128], ident[:, :],
                        start=True, stop=True),
                        waits=[(lh[buf], c_tgt), (ev, ev.n), (ac, ac.n)] if c4 == 0 else (),
                        inc=(pe, 1) if c4 == 3 else None)
                tr_tgt = pe.n
                for c4 in range(4):
                    pl.op("vector", lambda v, c4=c4, g16=g16: v.tensor_scalar(
                        kvlat_fm[:, c4 * 2048 + g16 * 128: c4 * 2048 + (g16 + 1) * 128],
                        ps_t4[:, c4 * 128:(c4 + 1) * 128],
                        kvnw[:, c4:c4 + 1], None, ALU.mult, ALU.bypass),
                        waits=[(pe, tr_tgt), (ld, kvnw_tgt), (ev, ev.n)] if c4 == 0 else (),
                        inc=(ev, 1) if c4 == 3 else None)
                kv_ev = ev.n
                pl.op("tensor", lambda t, buf=buf: t.matmul(
                    ps_t4[0:64, 0:128], cx_t[buf][:, 512:576], ident[:, :],
                    start=True, stop=True), waits=[(ev, kv_ev)])
                pl.op("tensor", lambda t, buf=buf: t.matmul(
                    ps_t4[:, 128:256], cx_t[buf][:, 576:704], ident[:, :],
                    start=True, stop=True), inc=(pe, 1))
                tr2_tgt = pe.n
                pl.op("scalar", lambda s, g16=g16: s.activation(
                    kpe_fm[:, g16 * 128:(g16 + 1) * 128], ps_t4[0:64, 0:128],
                    AF.Copy), waits=[(pe, tr2_tgt), (ac, ac.n)], inc=(ac, 1))
                pl.op("scalar", lambda s, g16=g16: s.activation(
                    ik_fm[:, g16 * 128:(g16 + 1) * 128], ps_t4[:, 128:256],
                    AF.Copy), waits=[(ac, ac.n)], inc=(ac, 1))
            fm_done = (ev.n, ac.n)
            for m in range(H_L1):
                buf = m % 2
                pl.op("gpsimd", lambda g, m=m, buf=buf: g.dma_start(
                    out=kbn[buf][:, :], in_=row128(io["kv_b_n_t"], m)),
                    waits=[(pe, pe.n)], inc=(wtP[buf], 16))
                w_tgt = wtP[buf].n
                mw = [(wtP[buf], w_tgt), (ev, fm_done[0]), (ac, ac.n), (ev, ev.n)]
                for kc in range(4):
                    for n5 in range(4):
                        pl.op("tensor", lambda t, kc=kc, n5=n5, buf=buf: t.matmul(
                            ps_kn[:, n5 * 512:(n5 + 1) * 512],
                            kbn[buf][:, kc * 128:(kc + 1) * 128],
                            kvlat_fm[:, kc * 2048 + n5 * 512: kc * 2048 + (n5 + 1) * 512],
                            start=(kc == 0), stop=(kc == 3)),
                            waits=mw if (kc == 0 and n5 == 0) else (),
                            inc=(pe, 1) if (kc == 3 and n5 == 3) else None)
                mm_tgt = pe.n
                dst = knstg[buf]
                pl.op("vector", lambda v, dst=dst: v.tensor_copy(
                    dst[:, :], ps_kn[:, :]),
                    waits=[(pe, mm_tgt), (sto, sto.n), (ev, ev.n)], inc=(ev, 1))
                e_tgt = ev.n
                pl.op("sync", lambda s, m=m, dst=dst: s.dma_start(
                    out=row128(io["kn_dr"], m), in_=dst[:, :]),
                    waits=[(ev, e_tgt)], inc=(sto, 16))
            pl.op("gpsimd", lambda g: g.dma_start(out=kbv[:, :], in_=io["kv_b_v_t"]),
                  inc=(wt, 16))
            kbv_tgt = wt.n
            for kt in range(16):
                for obv in range(8):
                    buf = (kt * 8 + obv) % 2
                    pp = ps_v[buf]
                    mw = [(wt, kbv_tgt), (ev, fm_done[0]), (ev, ev.n)]
                    for kc in range(4):
                        pl.op("tensor", lambda t, kc=kc, kt=kt, obv=obv, pp=pp: t.matmul(
                            pp[:, :],
                            kvlat_fm[:, kc * 2048 + kt * 128: kc * 2048 + (kt + 1) * 128],
                            kbv[:, kc * 4096 + obv * 512: kc * 4096 + (obv + 1) * 512],
                            start=(kc == 0), stop=(kc == 3)),
                            waits=mw if kc == 0 else (),
                            inc=(pe, 1) if kc == 3 else None)
                    mm_tgt = pe.n
                    dst = vstg[buf]
                    pl.op("vector", lambda v, dst=dst, pp=pp: v.tensor_copy(
                        dst[:, :], pp[:, :]),
                        waits=[(pe, mm_tgt), (sto, sto.n), (ev, ev.n)], inc=(ev, 1))
                    e_tgt = ev.n
                    pl.op("sync", lambda s, kt=kt, obv=obv, dst=dst: s.dma_start(
                        out=row128(io["v_dr"], kt)[:, obv * 512:(obv + 1) * 512],
                        in_=dst[:, :]), waits=[(ev, e_tgt)], inc=(sto, 16))

        pl.fence(sems)
        m01 = st.enter_context(
            nc.sbuf_tensor("c_m01", [128, NT * 2048], BF16, side="right"))
        with ExitStack() as st2:
            sb2 = lambda n, shp, dt=BF16: st2.enter_context(nc.sbuf_tensor(_uname(n), shp, dt))
            ps2 = lambda n, shp: st2.enter_context(nc.psum_tensor(_uname(n), shp, F32))
            iq_fm = sb2("c_iqfm", [128, 16 * 1024])
            iw_sb = sb2("c_iwsb", [128, NT * 16], F32)
            wwt = sb2("c_wwt", [128, 32 * 16])
            wq = [sb2(f"c_wq{i}", [128, 32 * 128]) for i in range(2)]
            cidx = sb2("c_cidx", [128, 2048], F32)
            rn_sb = [sb2("c_rn", [128, 2048]) for _ in range(2)]
            dmp5 = sb2("c_dmp5", [128, 2048], F32)
            cmask_sb = sb2("c_cm", [128, 256], F32)
            lo = sb2("c_lo", [128, 1], F32)
            hi = sb2("c_hi", [128, 1], F32)
            mid = sb2("c_mid", [128, 1], F32)
            cnt = sb2("c_cnt", [128, 1], F32)
            cond = sb2("c_cond", [128, 1], F32)
            tda = sb2("c_tda", [128, 1], F32)
            ps_i = [ps2(f"c_psi{i}", [128, 2048]) for i in range(2)]

            pl.op("gpsimd", lambda g: g.dma_start(out=wwt[:, :], in_=io["idx_ww_t"]),
                  inc=(wt, 16))
            ww_tgt = wt.n
            for m in range(16):
                buf = m % 2
                pl.op("gpsimd", lambda g, m=m, buf=buf: g.dma_start(
                    out=wq[buf][:, :], in_=row128(io["idx_wq_t"], m)),
                    waits=[(pe, pe.n)], inc=(wtP[buf], 16))
                w_tgt = wtP[buf].n
                pp = ps_i[buf]
                mw = [(wtP[buf], w_tgt), (ld, ld.n), (ac, ac.n), (ev, ev.n)]
                for kc in range(NC_D):
                    for h2 in range(2):
                        pl.op("tensor", lambda t, kc=kc, h2=h2, buf=buf, pp=pp: t.matmul(
                            pp[:, h2 * 512:(h2 + 1) * 512],
                            wq[buf][:, kc * 128:(kc + 1) * 128],
                            x2fm[:, kc * 1024 + h2 * 512: kc * 1024 + (h2 + 1) * 512],
                            start=(kc == 0), stop=(kc == 31)),
                            waits=mw if (kc == 0 and h2 == 0) else (),
                            inc=(pe, 1) if (kc == 31 and h2 == 1) else None)
                mm_tgt = pe.n
                pl.op("scalar", lambda s, m=m, pp=pp: s.activation(
                    iq_fm[:, m * 1024:(m + 1) * 1024], pp[:, 0:1024], AF.Copy),
                    waits=[(pe, mm_tgt), (ac, ac.n)], inc=(ac, 1))
            for jt in range(NT):
                pp = ps_i[jt % 2]
                mw = [(wt, ww_tgt), (ac, ac.n), (ev, ev.n)]
                for kc in range(NC_D):
                    pl.op("tensor", lambda t, kc=kc, jt=jt, pp=pp: t.matmul(
                        pp[:, 0:16],
                        x2fm[:, kc * 1024 + jt * 128: kc * 1024 + (jt + 1) * 128],
                        wwt[:, kc * 16:(kc + 1) * 16],
                        start=(kc == 0), stop=(kc == 31)),
                        waits=mw if kc == 0 else (),
                        inc=(pe, 1) if kc == 31 else None)
                mm_tgt = pe.n
                pl.op("vector", lambda v, jt=jt, pp=pp: v.tensor_scalar(
                    iw_sb[:, jt * 16:(jt + 1) * 16], pp[:, 0:16], SC_IDX, None,
                    ALU.mult, ALU.bypass),
                    waits=[(pe, mm_tgt), (ev, ev.n)], inc=(ev, 1))
            iw_ev = ev.n
            relu_prev = {0: ac.n, 1: ac.n}
            if True:
                pass
            iq_done = (ac.n, ev.n)
            for jq in range(NT):
                ncau = 256 * (jq + 1)
                nch = (ncau + 511) // 512
                pl.op("sync", lambda s, jq=jq: s.dma_start(
                    out=cmask_sb[:, :], in_=row128(io["cmask"], jq)),
                    waits=[(ev, ev.n)], inc=(ld, 16))
                cm_tgt = ld.n
                for n in range(16):
                    buf = n % 2
                    pp = ps_i[buf]
                    mw = [(ac, relu_prev[buf]), (ev, iw_ev)] if (
                        jq == 0 and n < 2) else [(ac, relu_prev[buf])]
                    for n5 in range(nch):
                        w5 = min(512, ncau - n5 * 512)
                        pl.op("tensor", lambda t, n=n, jq=jq, n5=n5, w5=w5, pp=pp: t.matmul(
                            pp[:, n5 * 512:n5 * 512 + w5],
                            iq_fm[:, n * 1024 + jq * 128: n * 1024 + (jq + 1) * 128],
                            ik_fm[:, n5 * 512:n5 * 512 + w5],
                            start=True, stop=True),
                            waits=mw if n5 == 0 else (),
                            inc=(pe, 1) if n5 == nch - 1 else None)
                    mm_tgt = pe.n
                    pl.op("scalar", lambda s, ncau=ncau, pp=pp, buf=buf: s.activation(
                        rn_sb[buf][:, 0:ncau], pp[:, 0:ncau], AF.Relu),
                        waits=[(pe, mm_tgt), (ev, ev.n)], inc=(ac, 1))
                    relu_prev[buf] = ac.n
                    r_tgt = ac.n
                    if n == 0:
                        vchain(lambda v, jq=jq, ncau=ncau, buf=buf: v.tensor_scalar(
                            cidx[:, 0:ncau], rn_sb[buf][:, 0:ncau],
                            iw_sb[:, jq * 16:jq * 16 + 1], None, ALU.mult, ALU.bypass),
                            extra=[(ac, r_tgt)])
                    else:
                        vchain(lambda v, jq=jq, n=n, ncau=ncau, buf=buf: v.scalar_tensor_tensor(
                            cidx[:, 0:ncau], rn_sb[buf][:, 0:ncau],
                            iw_sb[:, jq * 16 + n:jq * 16 + n + 1],
                            cidx[:, 0:ncau], ALU.mult, ALU.add),
                            extra=[(ac, r_tgt)])
                vchain(lambda v, ncau=ncau: v.tensor_reduce(
                    hi[:, :], cidx[:, 0:ncau], AX.X, ALU.max))
                vchain(lambda v, ncau=ncau: v.tensor_reduce(
                    lo[:, :], cidx[:, 0:ncau], AX.X, ALU.min))
                vchain(lambda v: v.tensor_scalar(
                    hi[:, :], hi[:, :], 1.0, None, ALU.add, ALU.bypass))
                vchain(lambda v: v.tensor_scalar(
                    lo[:, :], lo[:, :], 1.0, None, ALU.subtract, ALU.bypass))
                if ncau < 2048:
                    vchain(lambda v, ncau=ncau: v.memset(cidx[:, ncau:2048], NEG))
                vchain(lambda v, ncau=ncau: v.tensor_add(
                    cidx[:, ncau - 256:ncau], cidx[:, ncau - 256:ncau],
                    cmask_sb[:, :]), extra=[(ld, cm_tgt)])
                for itb in range(26):
                    vchain(lambda v: v.tensor_add(mid[:, :], lo[:, :], hi[:, :]))
                    vchain(lambda v: v.tensor_scalar(
                        mid[:, :], mid[:, :], 0.5, None, ALU.mult, ALU.bypass))
                    vchain(lambda v: v.tensor_scalar(
                        dmp5[:, :], cidx[:, :], mid[:, :], None, ALU.is_ge,
                        ALU.add, accum_out=cnt[:, :]))
                    vchain(lambda v: v.tensor_scalar(
                        cond[:, :], cnt[:, :], float(TOPK), None, ALU.is_ge,
                        ALU.bypass))
                    vchain(lambda v: v.tensor_sub(tda[:, :], mid[:, :], lo[:, :]))
                    vchain(lambda v: v.scalar_tensor_tensor(
                        lo[:, :], tda[:, :], cond[:, :], lo[:, :], ALU.mult, ALU.add))
                    vchain(lambda v: v.tensor_sub(tda[:, :], hi[:, :], mid[:, :]))
                    vchain(lambda v: v.scalar_tensor_tensor(
                        hi[:, :], tda[:, :], cond[:, :], mid[:, :], ALU.mult, ALU.add))
                vchain(lambda v, jq=jq: v.tensor_scalar(
                    m01[:, jq * 2048:(jq + 1) * 2048], cidx[:, :], lo[:, :], None,
                    ALU.is_ge, ALU.bypass))
                if "dbg_m01" in io:
                    e_tgt = ev.n
                    pl.op("sync", lambda s, jq=jq: s.dma_start(
                        out=row128(io["dbg_m01"], jq),
                        in_=m01[:, jq * 2048:(jq + 1) * 2048]),
                        waits=[(ev, e_tgt)], inc=(sto, 16))
                    pl.op("sync", lambda s, jq=jq: s.dma_start(
                        out=row128(io["dbg_cidx"], jq), in_=cidx[:, :]),
                        waits=[(ev, e_tgt)], inc=(sto, 16))
                    vchain(lambda v: v.engine_nop(), extra=[(sto, sto.n)])
            m01_done = ev.n

    pl.fence(sems)
    ctx = st.enter_context(
        nc.sbuf_tensor("c_ctx", [128, 32 * 1024], BF16, side="right"))
    with ExitStack() as st2:
        sb2 = lambda n, shp, dt=BF16: st2.enter_context(nc.sbuf_tensor(_uname(n), shp, dt))
        ps2 = lambda n, shp: st2.enter_context(nc.psum_tensor(_uname(n), shp, F32))
        knh = [sb2(f"c_knh{i}", [128, 2048]) for i in range(2)]
        vhb = [sb2(f"c_vh{i}", [128, 2048]) for i in range(2)]
        qnh = [sb2(f"c_qnh{i}", [128, 1024]) for i in range(2)]
        qrh = [sb2(f"c_qrh{i}", [64, 1024]) for i in range(2)]
        p_raw = [sb2("c_praw", [128, 2048]) for _ in range(2)]
        p_m = [sb2("c_pm", [128, 2048]) for _ in range(2)]
        p_sb = [sb2("c_psb", [128, 2048]) for _ in range(2)]
        s_sb6 = [sb2("c_ssb6", [128, 2048], F32) for _ in range(2)]
        pt_h = sb2("c_pt", [128, 16 * 1024])
        cm6 = sb2("c_cm6", [128, NT * 256], F32)
        nmax = [sb2("c_nmax", [128, 1], F32) for _ in range(2)]
        pmax6 = [sb2("c_pmax6", [128, 1], F32) for _ in range(2)]
        sums = [sb2("c_sums", [128, 1], F32) for _ in range(2)]
        recip = [sb2("c_recip", [128, 1], F32) for _ in range(2)]
        ps_cx = ps2("c_pscx", [128, 512])
        ps_s = ps2("c_pss", [128, 2048])
        ps_pt = ps2("c_pspt", [128, 1024])

        pl.op("sync", lambda s: s.dma_start(
            out=cm6[:, :].rearrange("p (j f) -> p j f", j=NT),
            in_=io["cmask"].rearrange("j (p f) -> p j f", p=128)),
            waits=[(ev, m01_done)], inc=(ld, 16))
        cm6_tgt = ld.n

        def load_head6(h, slot):
            g = [(pe, pe.n)]
            pl.op("sync", lambda s, h=h, slot=slot: s.dma_start(
                out=knh[slot][:, :], in_=row128(io["kn_dr"], h)),
                waits=g, inc=(lh[slot], 16))
            pl.op("sync", lambda s, h=h, slot=slot: s.dma_start(
                out=qnh[slot][:, :], in_=row128(io["qn_dr"], h)),
                inc=(lh[slot], 16))
            pl.op("sync", lambda s, h=h, slot=slot: s.dma_start(
                out=qrh[slot][:, :],
                in_=row128(io["qr_dr"], h // 2)[(h % 2) * 64:(h % 2) * 64 + 64, :]),
                inc=(lh[slot], 16))
            src = io["v_dr"].rearrange("k (t f) -> t k f", t=128)[
                :, :, h * 128:(h + 1) * 128]
            pl.op("scalar", lambda s, src=src, slot=slot: s.dma_start(
                out=vhb[slot][:, :].rearrange("t (k d) -> t k d", k=16), in_=src),
                waits=g, inc=(lv[slot], 16))
            return lh[slot].n, lv[slot].n

        mxev6 = [ev.n]
        ptv6_prev = [ac.n]
        exp6 = {0: ac.n, 1: ac.n}
        tgts = [None, None]
        tgts[0] = load_head6(0, 0)

        def emit_pt6(pjb, pnrm, pjq, pslot):
            nkt = 2 * (pjq + 1)
            for rnd in range((nkt + 7) // 8):
                k0 = rnd * 8
                kn8 = min(8, nkt - k0)
                tw = [(ev, pnrm), (ac, ptv6_prev[0])]
                for q4 in range(kn8):
                    kt = k0 + q4
                    pl.op("tensor", lambda t, kt=kt, q4=q4, pjb=pjb: t.matmul(
                        ps_pt[:, q4 * 128:(q4 + 1) * 128],
                        p_sb[pjb][:, kt * 128:(kt + 1) * 128], ident[:, :],
                        start=True, stop=True),
                        waits=tw if q4 == 0 else (),
                        inc=(pe, 1) if q4 == kn8 - 1 else None)
                tr_tgt = pe.n
                dst = pt_h[:, :].rearrange("p (kc q) -> p kc q", kc=16)[
                    :, k0:k0 + kn8, pjq * 128:(pjq + 1) * 128]
                pl.op("scalar", lambda s, dst=dst, kn8=kn8: s.activation(
                    dst, ps_pt[:, 0:kn8 * 128].rearrange(
                        "p (a b) -> p a b", a=kn8), AF.Copy),
                    waits=[(pe, tr_tgt)], inc=(ac, 1))
                ptv6_prev[0] = ac.n
            pt_done = ac.n
            cw = [(ac, pt_done), (ev, ev.n)]
            for kt in range(nkt):
                pl.op("tensor", lambda t, kt=kt, pjq=pjq, pslot=pslot: t.matmul(
                    ps_cx[:, 0:128],
                    vhb[pslot][:, kt * 128:(kt + 1) * 128],
                    pt_h[:, kt * 1024 + pjq * 128: kt * 1024 + (pjq + 1) * 128],
                    start=(kt == 0), stop=(kt == nkt - 1)),
                    waits=cw if kt == 0 else (),
                    inc=(pe, 1) if kt == nkt - 1 else None)
            cx_tgt = pe.n
            pl.op("vector", lambda v, pjq=pjq, hc=hcur[0]: v.tensor_copy(
                ctx[:, hc * 1024 + pjq * 128: hc * 1024 + (pjq + 1) * 128],
                ps_cx[:, 0:128]),
                waits=[(pe, cx_tgt), (ev, ev.n)], inc=(ev, 1))

        hcur = [0]
        pending = None
        for h in range(H_L1):
            slot = h % 2
            ld_t, lv_t = tgts[slot]
            if h + 1 < H_L1:
                tgts[(h + 1) % 2] = load_head6(h + 1, (h + 1) % 2)
            for jq in range(NT):
                ncau = 256 * (jq + 1)
                nch = (ncau + 511) // 512
                w0 = [(lh[slot], ld_t), (lv[slot], lv_t), (ld, cm6_tgt),
                      (ev, mxev6[0])]
                for n5 in range(nch):
                    w5 = min(512, ncau - n5 * 512)
                    pl.op("tensor", lambda t, jq=jq, n5=n5, w5=w5, slot=slot: t.matmul(
                        ps_s[:, n5 * 512:n5 * 512 + w5],
                        qnh[slot][:, jq * 128:(jq + 1) * 128],
                        knh[slot][:, n5 * 512:n5 * 512 + w5],
                        start=True, stop=False),
                        waits=w0 if n5 == 0 else ())
                    pl.op("tensor", lambda t, jq=jq, n5=n5, w5=w5, slot=slot: t.matmul(
                        ps_s[:, n5 * 512:n5 * 512 + w5],
                        qrh[slot][:, jq * 128:(jq + 1) * 128],
                        kpe_fm[:, n5 * 512:n5 * 512 + w5],
                        start=False, stop=True),
                        inc=(pe, 1) if n5 == nch - 1 else None)
                s_tgt = pe.n
                if pending is not None:
                    emit_pt6(*pending)
                    pending = None
                jb = jq % 2
                vchain(lambda v, jq=jq, ncau=ncau: v.tensor_add(
                    ps_s[:, ncau - 256:ncau], ps_s[:, ncau - 256:ncau],
                    cm6[:, jq * 256:(jq + 1) * 256]), extra=[(pe, s_tgt)])
                vchain(lambda v, ncau=ncau, jb=jb: v.tensor_scalar(
                    s_sb6[jb][:, 0:ncau], ps_s[:, 0:ncau], 1.0, None,
                    ALU.mult, ALU.max, accum_out=pmax6[jb][:, :]),
                    extra=[(ac, exp6[jb])])
                mxev6[0] = ev.n
                vchain(lambda v, jb=jb: v.tensor_scalar(
                    nmax[jb][:, :], pmax6[jb][:, :], -1.0, None,
                    ALU.mult, ALU.bypass))
                mx_tgt = ev.n
                pl.op("scalar", lambda s, ncau=ncau, jb=jb: s.activation(
                    p_raw[jb][:, 0:ncau], s_sb6[jb][:, 0:ncau], AF.Exp,
                    bias=nmax[jb][:, :]),
                    waits=[(ev, mx_tgt)], inc=(ac, 1))
                exp6[jb] = ac.n
                exp_tgt = ac.n
                vchain(lambda v, jq=jq, ncau=ncau, jb=jb: v.scalar_tensor_tensor(
                    p_m[jb][:, 0:ncau], m01[:, jq * 2048:jq * 2048 + ncau], 1.0,
                    p_raw[jb][:, 0:ncau], ALU.mult, ALU.mult,
                    accum_out=sums[jb][:, :]), extra=[(ac, exp_tgt)])
                vchain(lambda v, jb=jb: v.reciprocal(recip[jb][:, :], sums[jb][:, :]))
                vchain(lambda v, ncau=ncau, jb=jb: v.tensor_scalar(
                    p_sb[jb][:, 0:ncau], p_m[jb][:, 0:ncau], recip[jb][:, :], None,
                    ALU.mult, ALU.bypass))
                pending = (jb, ev.n, jq, slot)
            emit_pt6(*pending)
            pending = None
            hcur[0] = h + 1
        ctx_done = ev.n
    return ctx, ctx_done



def host_prep_D(inputs):
    W0 = np.asarray(inputs["L1_ffn_W0"], np.float32)
    b0 = np.asarray(inputs["L1_ffn_b0"], np.float32)
    A = np.asarray(inputs["L1_ffn_A"], np.float32)
    B = np.asarray(inputs["L1_ffn_B"], np.float32)
    d = {}
    d["W0T"] = _tile_rhs(np.ascontiguousarray(W0.T), 32, 8)
    d["b0_t"] = np.ascontiguousarray(np.tile(b0[None, :], (128, 1)))
    d["a_lt"] = np.ascontiguousarray(
        A.reshape(4, 16, 32, 128).transpose(3, 0, 2, 1).reshape(128, 4 * 32 * 16))
    d["BlT"] = np.ascontiguousarray(
        np.concatenate([B[l].T for l in range(4)], axis=1))
    return d


_M8_CACHE = {}


def host_prep_D_core(c):
    if "m8" not in _M8_CACHE:
        import jax
        with jax.default_device(jax.devices("cpu")[0]):
            k42 = jax.random.key(42, impl="threefry2x32")
            u = np.asarray(jax.random.uniform(k42, (4, 2048, 4096)))
        m8 = np.zeros((4, 4, 2048, 4096), np.float32)
        for l in range(4):
            lo_b, hi_b = 0.25 * l, 0.25 * (l + 1)
            m8[l] = ((u >= lo_b) & (u < hi_b)) * 8.0
        _M8_CACHE["m8"] = m8.astype(BF)
    m8 = _M8_CACHE["m8"]
    b, p = c // 2, c % 2
    tiles = [2 * j + p for j in range(NT)]
    idx = np.concatenate([np.arange(g * 128, g * 128 + 128) for g in tiles])
    sl = m8[:, b, idx, :].transpose(1, 0, 2).reshape(NT, 128, 4 * 4096)
    return {"m8": np.ascontiguousarray(sl.reshape(NT, 128 * 4 * 4096))}


def declare_dram_D(nc, io, debug=False):
    ei = lambda n, shp, dt=F32: io.__setitem__(
        n, nc.dram_tensor(n, shp, dt, kind="ExternalInput").ap())
    ei("W0T", [8, 128 * 32 * 512])
    ei("b0_t", [128, D])
    ei("a_lt", [128, 4 * 32 * 16])
    ei("BlT", [16, 4 * 4096])
    ei("m8", [NT, 128 * 4 * 4096], BF16)
    io["s1d"] = nc.dram_tensor("s1d", [NT, 128 * D], F32, kind="Internal").ap()
    if debug:
        io["_mkdbg"]("s1d", [NT, 128 * D], F32)


def phase_D(nc, pl, st, io, sems, glob, x_fm_key="x3_fm", out_key="s1d",
            fm_ready=0):
    ld, wt, pe, ev, ac, sto = (sems["ld"], sems["wt"], sems["pe"], sems["ev"],
                               sems["ac"], sems["sto"])
    lh = [sems["lh0"], sems["lh1"]]
    wtP = [sems["wtA"], sems["wtB"]]
    sb = lambda n, shp, dt=BF16: st.enter_context(nc.sbuf_tensor(_uname(n), shp, dt))
    psa = lambda n, shp: st.enter_context(nc.psum_tensor(_uname(n), shp, F32))

    pl.fence(sems)
    x3fm = sb("d_x3fm", [128, NC_D * 1024])
    hl = sb("d_hl", [16, 4 * 1024])
    blt = sb("d_blt", [16, 4 * 4096])
    alt = sb("d_alt", [128, 4 * 32 * 16])
    b0t = sb("d_b0t", [128, D], F32)
    w0 = [sb(f"d_w0_{i}", [128, 32 * 512]) for i in range(2)]
    msl = [sb(f"d_m{i}", [128, 4 * 512]) for i in range(2)]
    acc = [sb(f"d_acc{i}", [128, 512], F32) for i in range(4)]
    tl = sb("d_tl", [128, 512], F32)
    ps_o = None
    ps_y = None

    pl.op("sync", lambda s: s.dma_start(
        out=x3fm[:, :].rearrange("p (c f) -> p c f", c=NC_D),
        in_=io[x_fm_key].rearrange("c (p f) -> p c f", p=128)),
        waits=[(sems["st"], fm_ready)], inc=(ld, 16))
    pl.op("sync", lambda s: s.dma_start(out=b0t[:, :], in_=io["b0_t"]),
          inc=(ld, 16))
    pl.op("gpsimd", lambda g: g.dma_start(out=blt[:, :], in_=io["BlT"]),
          inc=(wt, 16))
    pl.op("gpsimd", lambda g: g.dma_start(out=alt[:, :], in_=io["a_lt"]),
          inc=(wt, 16))
    x_tgt = ld.n
    sm_tgt = wt.n

    with ExitStack() as sth:
        ps_h = sth.enter_context(nc.psum_tensor("d_psh", [16, 4096], F32))
        lw = [(ld, x_tgt), (wt, sm_tgt), (ev, ev.n), (ac, ac.n)]
        for l in range(4):
            for kc in range(NC_D):
                for h2 in range(2):
                    pl.op("tensor", lambda t, l=l, kc=kc, h2=h2: t.matmul(
                        ps_h[:, l * 1024 + h2 * 512: l * 1024 + (h2 + 1) * 512],
                        alt[:, (l * 32 + kc) * 16: (l * 32 + kc + 1) * 16],
                        x3fm[:, kc * 1024 + h2 * 512: kc * 1024 + h2 * 512 + 512],
                        start=(kc == 0), stop=(kc == 31)),
                        waits=lw if (l == 0 and kc == 0 and h2 == 0) else (),
                        inc=(pe, 1) if (kc == 31 and h2 == 1) else None)
        h_tgt = pe.n
        pl.op("vector", lambda v: v.tensor_copy(hl[:, :], ps_h[:, :]),
              waits=[(pe, h_tgt), (ev, ev.n)], inc=(ev, 1))
        hl_done = ev.n
    ps_o = [psa(f"d_pso{i}", [128, 512]) for i in range(2)]
    ps_y = [psa(f"d_psy{i}", [128, 512]) for i in range(2)]

    w_tgt = [0, 0]
    for obi in range(8):
        buf = obi % 2
        pl.op("gpsimd", lambda g, obi=obi, buf=buf: g.dma_start(
            out=w0[buf][:, :], in_=row128(io["W0T"], obi)),
            waits=[(pe, pe.n)], inc=(wtP[buf], 16))
        w_tgt[buf] = wtP[buf].n
        for jt in range(NT):
            mbuf = jt % 2
            pl.op("sync", lambda s, jt=jt, obi=obi, mbuf=mbuf: s.dma_start(
                out=msl[mbuf][:, :].rearrange("p (l f) -> p l f", l=4),
                in_=row128(io["m8"], jt).rearrange(
                    "p (l f) -> p l f", l=4)[:, :, obi * 512:(obi + 1) * 512]),
                waits=[(ev, ev.n)], inc=(lh[mbuf], 16))
            m_tgt = lh[mbuf].n
            pp = ps_o[buf]
            mw = [(wtP[buf], w_tgt[buf]), (ev, hl_done), (ev, ev.n)]
            for kc in range(NC_D):
                pl.op("tensor", lambda t, jt=jt, kc=kc, pp=pp, buf=buf: t.matmul(
                    pp[:, :],
                    x3fm[:, kc * 1024 + jt * 128: kc * 1024 + (jt + 1) * 128],
                    w0[buf][:, kc * 512:(kc + 1) * 512],
                    start=(kc == 0), stop=(kc == 31)),
                    waits=mw if kc == 0 else (),
                    inc=(pe, 1) if kc == 31 else None)
            mm_tgt = pe.n
            av = acc[(obi * NT + jt) % 4]
            pl.op("vector", lambda v, pp=pp, av=av, obi=obi: v.scalar_tensor_tensor(
                av[:, :], pp[:, :], 1.0, b0t[:, obi * 512:(obi + 1) * 512],
                ALU.mult, ALU.add),
                waits=[(pe, mm_tgt), (sto, sto.n), (ev, ev.n)], inc=(ev, 1))
            for l in range(4):
                ybuf = l % 2
                yw = [(ev, ev.n)]
                pl.op("tensor", lambda t, l=l, jt=jt, obi=obi, ybuf=ybuf: t.matmul(
                    ps_y[ybuf][:, :],
                    hl[:, l * 1024 + jt * 128: l * 1024 + (jt + 1) * 128],
                    blt[:, l * 4096 + obi * 512: l * 4096 + (obi + 1) * 512],
                    start=True, stop=True),
                    waits=[(ev, hl_done), (ev, ev.n)], inc=(pe, 1))
                y_tgt = pe.n
                pl.op("vector", lambda v, l=l, ybuf=ybuf, mbuf=mbuf: v.tensor_mul(
                    tl[:, :], ps_y[ybuf][:, :],
                    msl[mbuf][:, l * 512:(l + 1) * 512]),
                    waits=[(pe, y_tgt), (lh[mbuf], m_tgt), (ev, ev.n)], inc=(ev, 1))
                pl.op("vector", lambda v, av=av: v.tensor_add(
                    av[:, :], av[:, :], tl[:, :]),
                    waits=[(ev, ev.n)], inc=(ev, 1))
            e_tgt = ev.n
            pl.op("sync", lambda s, jt=jt, obi=obi, av=av: s.dma_start(
                out=row128(io[out_key], jt)[:, obi * 512:(obi + 1) * 512],
                in_=av[:, :]), waits=[(ev, e_tgt)], inc=(sto, 16))
    return sto.n



def phase_A1(nc, pl, st, io, sems, glob):
    ld, wt, pe, ev, cc = sems["ld"], sems["wt"], sems["pe"], sems["ev"], sems["cc"]
    sto = sems["sto"]
    wtP = [sems["wtA"], sems["wtB"]]

    sb = lambda n, shp, dt=BF16: st.enter_context(nc.sbuf_tensor(_uname(n), shp, dt))
    psa = lambda n, shp: st.enter_context(nc.psum_tensor(_uname(n), shp, F32))

    x_fm = sb("a_x_fm", [128, NC_D * TPC])
    wqk = [sb(f"a_wqk{i}", [128, 32 * 128]) for i in range(2)]
    wv = [sb(f"a_wv{i}", [128, 32 * 512]) for i in range(2)]
    bqk = sb("a_bqk", [128, 64], F32)
    bv = sb("a_bv", [128, D], F32)
    stg = [sb(f"a_stg{i}", [128, 512]) for i in range(4)]
    ps_a = [psa(f"a_ps{i}", [128, 1024]) for i in range(4)]

    pl.op("gpsimd", lambda g: g.dma_start(
        out=x_fm[:, :].rearrange("p (c f) -> p c f", c=NC_D),
        in_=io["x_fm"].rearrange("c (p f) -> p c f", p=128)),
        inc=(wt, 16))
    x_ready = wt.n
    pl.op("sync", lambda s: s.dma_start(out=bqk[:, :], in_=io["b_qk"]), inc=(ld, 16))
    pl.op("sync", lambda s: s.dma_start(out=bv[:, :], in_=io["b_v_t"]), inc=(ld, 16))
    b_ready = ld.n

    SC = float(1.0 / np.sqrt(128.0))
    n_ev0 = ev.n
    for m in range(64):
        buf = m % 2
        wsb = wqk[buf]
        waits = [(ev, max(n_ev0, ev.n - 1))] if m >= 2 else []
        pl.op("gpsimd", lambda g, m=m, wsb=wsb: g.dma_start(
            out=wsb[:, :], in_=row128(io["w_qk"], m)),
            waits=waits, inc=(wtP[buf], 16))
        w_tgt = wtP[buf].n
        pba, pbb = ps_a[2 * buf], ps_a[2 * buf + 1]
        pe_waits = [(wtP[buf], w_tgt), (wt, wt.n),
                    (ev, max(n_ev0, ev.n - 1))]
        for kc in range(32):
            for h2 in range(2):
                pp = pba if h2 == 0 else pbb
                pl.op("tensor", lambda t, kc=kc, h2=h2, wsb=wsb, pp=pp: t.matmul(
                    pp[:, h2 * 0:512] if False else pp[:, 0:512],
                    wsb[:, kc * 128:(kc + 1) * 128],
                    x_fm[:, kc * TPC + h2 * 512: kc * TPC + (h2 + 1) * 512],
                    start=(kc == 0), stop=(kc == 31)),
                    waits=pe_waits if (kc == 0 and h2 == 0) else (),
                    inc=(pe, 1) if (kc == 31 and h2 == 1) else None)
        mm_tgt = pe.n
        is_q = m < 32
        for h2 in range(2):
            pp = pba if h2 == 0 else pbb
            dst = stg[2 * buf + h2]
            ew = [(pe, mm_tgt), (ld, b_ready), (sto, sto.n)]
            if is_q:
                pl.op("vector", lambda v, pp=pp, dst=dst, m=m: v.tensor_scalar(
                    dst[:, :], pp[:, 0:512], bqk[:, m:m + 1], SC, ALU.add, ALU.mult),
                    waits=ew if h2 == 0 else (),
                    inc=(ev, 1) if h2 == 1 else None)
            else:
                pl.op("vector", lambda v, pp=pp, dst=dst, m=m: v.tensor_scalar(
                    dst[:, :], pp[:, 0:512], bqk[:, m:m + 1], None, ALU.add, ALU.bypass),
                    waits=ew if h2 == 0 else (),
                    inc=(ev, 1) if h2 == 1 else None)
        ev_tgt = ev.n
        drow = row128(io["q_dr"], m) if is_q else row128(io["kv_src"], m - 32)
        pl.op("sync", lambda s, drow=drow, a=stg[2 * buf]: s.dma_start(
            out=drow[:, 0:512], in_=a[:, :]), waits=[(ev, ev_tgt)], inc=(sto, 16))
        pl.op("sync", lambda s, drow=drow, b=stg[2 * buf + 1]: s.dma_start(
            out=drow[:, 512:1024], in_=b[:, :]), inc=(sto, 16))

    for obv in range(8):
        buf = obv % 2
        wsb = wv[buf]
        waits = ([(ev, max(n_ev0, ev.n - 8))] if obv >= 2
                 else [(ev, ev.n)])
        pl.op("gpsimd", lambda g, obv=obv, wsb=wsb: g.dma_start(
            out=wsb[:, :], in_=row128(io["w_v"], obv)),
            waits=waits, inc=(wtP[buf], 16))
        w_tgt = wtP[buf].n
        for jt in range(NT):
            pp = ps_a[2 * buf + (jt % 2)]
            if obv < 2 and jt < 2:
                pe_waits = [(wtP[buf], w_tgt), (ev, ev.n)]
            else:
                pe_waits = [(wtP[buf], w_tgt), (ev, ev.n - 1)]
            for kc in range(32):
                pl.op("tensor", lambda t, jt=jt, kc=kc, wsb=wsb, pp=pp: t.matmul(
                    pp[:, 0:512],
                    x_fm[:, kc * TPC + jt * 128: kc * TPC + (jt + 1) * 128],
                    wsb[:, kc * 512:(kc + 1) * 512],
                    start=(kc == 0), stop=(kc == 31)),
                    waits=pe_waits if kc == 0 else (),
                    inc=(pe, 1) if kc == 31 else None)
            mm_tgt = pe.n
            sidx = 2 * buf + (jt % 2)
            dst = stg[sidx]
            pl.op("vector", lambda v, pp=pp, dst=dst, obv=obv: v.scalar_tensor_tensor(
                dst[:, :], pp[:, 0:512], 1.0, bv[:, obv * 512:(obv + 1) * 512],
                ALU.mult, ALU.add),
                waits=[(pe, mm_tgt), (sto, sto.n)], inc=(ev, 1))
            ev_tgt = ev.n
            vrow = row128(io["kv_src"], 32 + jt * 4 + obv // 2)
            pl.op("sync", lambda s, vrow=vrow, dst=dst, obv=obv: s.dma_start(
                out=vrow[:, (obv % 2) * 512:(obv % 2) * 512 + 512], in_=dst[:, :]),
                waits=[(ev, ev_tgt)], inc=(sto, 16))

    pl.op("gpsimd", lambda g: g.collective_compute(
        "AllGather", ALU.bypass, replica_groups=REPLICA_PAIRS,
        ins=[io["kv_src"].opt()], outs=[io["kv_dst"].opt()]),
        waits=[(ld, ld.n), (sto, sto.n), (ev, ev.n)], inc=(cc, 1))
    return cc.n


def phase_A2(nc, pl, st, io, sems, glob, cc_done, ctx):
    ld, ld2, wt = sems["ld"], sems["ld2"], sems["wt"]
    pe, ev, ac, cc = sems["pe"], sems["ev"], sems["ac"], sems["cc"]

    sb = lambda n, shp, dt=BF16: st.enter_context(nc.sbuf_tensor(_uname(n), shp, dt))
    psa = lambda n, shp: st.enter_context(nc.psum_tensor(_uname(n), shp, F32))

    qh = [sb(f"a2_q{i}", [128, 1024]) for i in range(2)]
    kh = [sb(f"a2_k{i}", [128, 2048]) for i in range(2)]
    vh = [sb(f"a2_v{i}", [128, 2048]) for i in range(2)]
    p_raw = [sb("a2_praw", [128, 2048]) for _ in range(2)]
    p_nrm = [sb("a2_pnrm", [128, 2048]) for _ in range(2)]
    s_sb = [sb("a2_ssb", [128, 2048], F32) for _ in range(2)]
    pt_h = sb("a2_pt", [128, 16 * 1024])
    nmax = [sb("a2_nmax", [128, 1], F32) for _ in range(2)]
    pmax = [sb("a2_pmax", [128, 1], F32) for _ in range(2)]
    sums = [sb("a2_sums", [128, 1], F32) for _ in range(2)]
    recip = [sb("a2_recip", [128, 1], F32) for _ in range(2)]

    ps_s = psa("a2_ps_s", [128, 2048])
    ps_pt = psa("a2_ps_pt", [128, 2048])
    ps_c = ps_pt[:, 0:1024]

    ident = glob["ident"]
    id_tgt = glob["id_tgt"]
    kv_dst = io["kv_dst"]

    lh = [sems["lh0"], sems["lh1"]]
    lv = [sems["lv0"], sems["lv1"]]
    idw = wt.n
    wtP = [sems["wtA"], sems["wtB"]]

    def load_head(h, slot):
        g = [(pe, pe.n)]
        for r in range(2):
            pl.op("sync", lambda s, h=h, r=r, slot=slot: s.dma_start(
                out=kh[slot][:, r * 1024:(r + 1) * 1024],
                in_=row128(kv_dst, r * 64 + h)),
                waits=g if r == 0 else (), inc=(lh[slot], 16))
        for r in range(2):
            src = kv_dst[r * 64 + 32 + h // 8: r * 64 + 64: 4, :].rearrange(
                "jk (t f) -> t jk f", t=128)[:, :, (h % 8) * 128:(h % 8) * 128 + 128]
            pl.op("scalar", lambda s, src=src, r=r, slot=slot: s.dma_start(
                out=vh[slot][:, r * 1024:(r + 1) * 1024].rearrange(
                    "t (jk d) -> t jk d", jk=8), in_=src),
                waits=g if r == 0 else (), inc=(lv[slot], 16))
        pl.op("sync", lambda s, h=h, slot=slot: s.dma_start(
            out=qh[slot][:, :], in_=row128(io["q_dr"], h)), inc=(lh[slot], 16))
        return lh[slot].n, lv[slot].n

    pl.fence(sems)
    pl.wait("sync", sems["cc"], cc_done)
    pl.wait("scalar", sems["cc"], cc_done)
    mxev_tgt = [sems["ev"].n]
    ptv_prev = [sems["ac"].n]
    ctxev_prev = [sems["ev"].n]
    exp_prev = {0: sems["ac"].n, 1: sems["ac"].n}
    tgts = [None, None]
    tgts[0] = load_head(0, 0)

    def emit_pt(pjb, pnrm, pjq):
        tw = [(ev, pnrm), (ac, ptv_prev[0]), (ev, ctxev_prev[0])]
        for kc in range(16):
            pl.op("tensor", lambda t, kc=kc, pjb=pjb: t.matmul(
                ps_pt[:, kc * 128:(kc + 1) * 128],
                p_nrm[pjb][:, kc * 128:(kc + 1) * 128],
                ident[:, :], start=True, stop=True),
                waits=tw if kc == 0 else (),
                inc=(pe, 1) if kc == 15 else None)
        tr_tgt = pe.n
        dst = pt_h[:, :].rearrange("p (kc q) -> p kc q", kc=16)[
            :, :, pjq * 128:(pjq + 1) * 128]
        pl.op("scalar", lambda s, dst=dst: s.activation(
            dst, ps_pt[:, :].rearrange("p (a b) -> p a b", a=16), AF.Copy),
            waits=[(pe, tr_tgt)], inc=(ac, 1))
        ptv_prev[0] = ac.n

    pending = None
    for h in range(32):
        slot = h % 2
        ld_t, ld2_t = tgts[slot]
        if h + 1 < 32:
            tgts[(h + 1) % 2] = load_head(h + 1, (h + 1) % 2)
        for jq in range(NT):
            w0 = [(lh[slot], ld_t), (lv[slot], ld2_t), (wt, idw),
                  (ev, mxev_tgt[0])]
            for n5 in range(4):
                pl.op("tensor", lambda t, jq=jq, n5=n5, slot=slot: t.matmul(
                    ps_s[:, n5 * 512:(n5 + 1) * 512],
                    qh[slot][:, jq * 128:(jq + 1) * 128],
                    kh[slot][:, n5 * 512:(n5 + 1) * 512],
                    start=True, stop=True),
                    waits=w0 if n5 == 0 else (),
                    inc=(pe, 1) if n5 == 3 else None)
            s_tgt = pe.n
            if pending is not None:
                emit_pt(*pending)
                pending = None
            jb = jq % 2
            pl.op("vector", lambda v, jb=jb: v.tensor_scalar(
                s_sb[jb][:, :], ps_s[:, :], 1.0, None, ALU.mult, ALU.max,
                accum_out=pmax[jb][:, :]),
                waits=[(pe, s_tgt), (ac, exp_prev[jb]), (ev, ev.n)], inc=(ev, 1))
            mxev_tgt[0] = ev.n
            pl.op("vector", lambda v, jb=jb: v.tensor_scalar(
                nmax[jb][:, :], pmax[jb][:, :], -1.0, None, ALU.mult, ALU.bypass),
                waits=[(ev, ev.n)], inc=(ev, 1))
            mx_tgt = ev.n
            pl.op("scalar", lambda s, jb=jb: s.activation(
                p_raw[jb][:, :], s_sb[jb][:, :], AF.Exp, bias=nmax[jb][:, :],
                accum_out=sums[jb][:, :]),
                waits=[(ev, mx_tgt)], inc=(ac, 1))
            exp_prev[jb] = ac.n
            exp_tgt = ac.n
            pl.op("vector", lambda v, jb=jb: v.reciprocal(
                recip[jb][:, :], sums[jb][:, :]),
                waits=[(ac, exp_tgt)], inc=(ev, 1))
            pl.op("vector", lambda v, jb=jb: v.tensor_scalar(
                p_nrm[jb][:, :], p_raw[jb][:, :], recip[jb][:, :], None,
                ALU.mult, ALU.bypass),
                waits=[(ev, ev.n), (pe, pe.n)], inc=(ev, 1))
            pending = (jb, ev.n, jq)
        emit_pt(*pending)
        pending = None
        pt_done = ac.n
        cw = [(ac, pt_done), (ev, ev.n)]
        for kc in range(16):
            for q2 in range(2):
                pl.op("tensor", lambda t, kc=kc, q2=q2, slot=slot: t.matmul(
                    ps_c[:, q2 * 512:(q2 + 1) * 512],
                    vh[slot][:, kc * 128:(kc + 1) * 128],
                    pt_h[:, kc * 1024 + q2 * 512: kc * 1024 + (q2 + 1) * 512],
                    start=(kc == 0), stop=(kc == 15)),
                    waits=cw if (kc == 0 and q2 == 0) else (),
                    inc=(pe, 1) if (kc == 15 and q2 == 1) else None)
        ctx_tgt = pe.n
        pl.op("vector", lambda v, h=h: v.tensor_copy(
            ctx[:, h * 1024:(h + 1) * 1024], ps_c[:, :]),
            waits=[(pe, ctx_tgt)], inc=(ev, 1))
        ctxev_prev[0] = ev.n
    return ev.n


def phase_A3a(nc, pl, st, io, sems, glob, ctx, ctx_done, w_key="w_o",
              bias_key="ob_t", s1a_key="s1a"):
    ld, wt, pe, ev = sems["ld"], sems["wt"], sems["pe"], sems["ev"]
    sto = sems["sto"]
    wtP = [sems["wtA"], sems["wtB"]]

    sb = lambda n, shp, dt=BF16: st.enter_context(nc.sbuf_tensor(_uname(n), shp, dt))
    psa = lambda n, shp: st.enter_context(nc.psum_tensor(_uname(n), shp, F32))

    wo = [sb(f"a3_wo{i}", [128, 32 * 512]) for i in range(2)]
    obt = sb("a3_obt", [128, D], F32)
    pl.fence(sems)
    stg = [sb(f"a3_stg{i}", [128, 512], F32) for i in range(4)]
    ps_o = [psa(f"a3_ps{i}", [128, 512]) for i in range(4)]

    pl.op("sync", lambda s: s.dma_start(out=obt[:, :], in_=io[bias_key]),
          waits=[(ev, ctx_done)], inc=(ld, 16))
    obt_tgt = ld.n

    wo_tgt = [None, None]
    pe0 = pe.n
    for obi in range(8):
        buf = obi % 2
        lw = [(ev, ctx_done)] if obi < 2 else [(pe, max(pe0, pe.n - 8))]
        pl.op("gpsimd", lambda g, obi=obi, buf=buf: g.dma_start(
            out=wo[buf][:, :], in_=row128(io[w_key], obi)), waits=lw,
            inc=(wtP[buf], 16))
        wo_tgt[buf] = wtP[buf].n
        for jt in range(NT):
            pp = ps_o[jt % 4]
            mmw = [(wtP[buf], wo_tgt[buf]), (ev, ctx_done), (ev, ev.n)]
            for kc in range(32):
                pl.op("tensor", lambda t, jt=jt, kc=kc, pp=pp, buf=buf: t.matmul(
                    pp[:, :],
                    ctx[:, kc * 1024 + jt * 128: kc * 1024 + (jt + 1) * 128],
                    wo[buf][:, kc * 512:(kc + 1) * 512],
                    start=(kc == 0), stop=(kc == 31)),
                    waits=mmw if kc == 0 else (),
                    inc=(pe, 1) if kc == 31 else None)
            mm_tgt = pe.n
            dst = stg[jt % 4]
            pl.op("vector", lambda v, pp=pp, dst=dst, obi=obi: v.scalar_tensor_tensor(
                dst[:, :], pp[:, :], 1.0, obt[:, obi * 512:(obi + 1) * 512],
                ALU.mult, ALU.add),
                waits=[(pe, mm_tgt), (ld, obt_tgt), (sto, sto.n)], inc=(ev, 1))
            ev_tgt = ev.n
            pl.op("sync", lambda s, jt=jt, obi=obi, dst=dst: s.dma_start(
                out=row128(io[s1a_key], jt)[:, obi * 512:(obi + 1) * 512],
                in_=dst[:, :]), waits=[(ev, ev_tgt)], inc=(sto, 16))
    return sto.n


def phase_LN(nc, pl, st, io, sems, glob, in_keys, out_prefix, s1a_stores,
             fold=False, want_fm=True):
    ld, wt, pe, ev, ac, stq = (sems["ld"], sems["wt"], sems["pe"], sems["ev"],
                               sems["ac"], sems["st"])

    sb = lambda n, shp, dt=BF16: st.enter_context(nc.sbuf_tensor(_uname(n), shp, dt))
    psa = lambda n, shp: st.enter_context(nc.psum_tensor(_uname(n), shp, F32))

    xa = sb("ln_xa", [128, D], F32)
    xbb = sb("ln_xb", [128, D], F32)
    s1 = sb("ln_s1", [128, D], F32)
    xn = sb("ln_xn", [128, D], F32)
    xnb = sb("ln_xnb", [128, D])
    xf = sb("ln_xf", [128, 1024], F32)
    xfb = sb("ln_xfb", [128, 1024])
    dump = sb("ln_dump", [128, D], F32)
    fm_stg = sb("ln_fmstg", [128, NC_D * 128])
    ff_stg = sb("ln_ffstg", [128, NT * 128])
    mean = sb("ln_mean", [128, 1], F32)
    ssum = sb("ln_ssum", [128, 1], F32)
    var = sb("ln_var", [128, 1], F32)
    rstd = sb("ln_rstd", [128, 1], F32)
    tmp1 = sb("ln_tmp1", [128, 1], F32)
    ps_t = psa("ln_ps_t", [128, 1024])

    ident = glob["ident"]
    ngrp = (10 if fold else 8) if want_fm else 0

    sto = sems["sto"]
    pl.fence(sems)
    for jt in range(NT):
        lw = [(sto, s1a_stores), (ev, ev.n), (ac, ac.n)]
        pl.op("sync", lambda s, jt=jt: s.dma_start(
            out=xa[:, :], in_=row128(io[in_keys[0]], jt)), waits=lw, inc=(ld, 16))
        pl.op("sync", lambda s, jt=jt: s.dma_start(
            out=xbb[:, :], in_=row128(io[in_keys[1]], jt)), inc=(ld, 16))
        x_tgt = ld.n
        def vstep(fn, extra=()):
            pl.op("vector", fn, waits=[(ev, ev.n)] + list(extra), inc=(ev, 1))

        pl.op("vector", lambda v: v.tensor_add(s1[:, :], xa[:, :], xbb[:, :]),
              waits=[(ld, x_tgt), (ev, ev.n)], inc=(ev, 1))
        vstep(lambda v: v.reduce_sum(mean[:, :], s1[:, :], AX.X))
        vstep(lambda v: v.scalar_tensor_tensor(
            dump[:, :], s1[:, :], 1.0, s1[:, :], ALU.mult, ALU.mult,
            accum_out=ssum[:, :]))
        vstep(lambda v: v.tensor_scalar(
            mean[:, :], mean[:, :], 1.0 / D, None, ALU.mult, ALU.bypass))
        vstep(lambda v: v.scalar_tensor_tensor(
            tmp1[:, :], mean[:, :], 1.0, mean[:, :], ALU.mult, ALU.mult))
        vstep(lambda v: v.tensor_scalar(
            var[:, :], ssum[:, :], 1.0 / D, None, ALU.mult, ALU.bypass))
        vstep(lambda v: v.tensor_sub(var[:, :], var[:, :], tmp1[:, :]))
        var_tgt = ev.n
        pl.op("scalar", lambda s: s.activation(
            rstd[:, :], var[:, :], AF.Ln, bias=glob["eps"][:, :]),
            waits=[(ev, var_tgt), (ac, ac.n)], inc=(ac, 1))
        pl.op("scalar", lambda s: s.activation(
            rstd[:, :], rstd[:, :], AF.Exp, scale=-0.5),
            waits=[(ac, ac.n)], inc=(ac, 1))
        rstd_tgt = ac.n
        vstep(lambda v: v.tensor_scalar(
            xn[:, :], s1[:, :], mean[:, :], rstd[:, :], ALU.subtract, ALU.mult),
            extra=[(ac, rstd_tgt), (stq, stq.n)])
        xn_tgt = ev.n
        pl.op("sync", lambda s, jt=jt: s.dma_start(
            out=row128(io[f"{out_prefix}_tm"], jt), in_=xn[:, :]),
            waits=[(ev, xn_tgt)], inc=(stq, 16))
        vstep(lambda v: v.tensor_copy(xnb[:, :], xn[:, :]))
        if fold:
            vstep(lambda v: v.tensor_add(
                xf[:, :], xn[:, 0:1024], xn[:, 1024:2048]))
            vstep(lambda v: v.tensor_add(
                dump[:, 0:1024], xn[:, 2048:3072], xn[:, 3072:4096]))
            vstep(lambda v: v.tensor_add(
                xf[:, :], xf[:, :], dump[:, 0:1024]))
            vstep(lambda v: v.tensor_copy(xfb[:, :], xf[:, :]))
        cast_tgt = ev.n
        for grp in range(ngrp):
            half = grp % 2
            tw = [(ev, cast_tgt), (ac, ac.n)]
            for q4 in range(4):
                cch = grp * 4 + q4
                src = xnb if cch < 32 else xfb
                coff = cch * 128 if cch < 32 else (cch - 32) * 128
                pl.op("tensor", lambda t, src=src, coff=coff, half=half, q4=q4: t.matmul(
                    ps_t[:, half * 512 + q4 * 128: half * 512 + (q4 + 1) * 128],
                    src[:, coff:coff + 128], ident[:, :], start=True, stop=True),
                    waits=tw if q4 == 0 else (),
                    inc=(pe, 1) if q4 == 3 else None)
            tr_tgt = pe.n
            dsts = fm_stg if grp < 8 else ff_stg
            doff = grp * 512 if grp < 8 else (grp - 8) * 512
            pl.op("scalar", lambda s, dsts=dsts, doff=doff, half=half: s.activation(
                dsts[:, doff:doff + 512],
                ps_t[:, half * 512:(half + 1) * 512], AF.Copy),
                waits=[(pe, tr_tgt), (stq, stq.n)], inc=(ac, 1))
        fm_done = ac.n
        if want_fm:
            pl.op("sync", lambda s, jt=jt: s.dma_start(
                out=io[f"{out_prefix}_fm"].rearrange(
                    "c (p f) -> p c f", p=128)[:, :, jt * 128:(jt + 1) * 128],
                in_=fm_stg[:, :].rearrange("p (c f) -> p c f", c=NC_D)),
                waits=[(ac, fm_done)], inc=(stq, 16))
        if fold:
            pl.op("sync", lambda s, jt=jt: s.dma_start(
                out=io[f"{out_prefix}f_fm"].rearrange(
                    "c (p f) -> p c f", p=128)[:, :, jt * 128:(jt + 1) * 128],
                in_=ff_stg[:, :].rearrange("p (c f) -> p c f", c=NT)),
                inc=(stq, 16))
    return stq.n



def make_sems(nc, st):
    names = ["ld", "ld2", "wt", "wtA", "wtB", "pe", "ev", "ac", "cc", "st",
             "sto", "lh0", "lh1", "lv0", "lv1",
             "b_tensor", "b_vector", "b_scalar", "b_gpsimd", "b_sync"]
    sems = {}
    for n in names:
        h = st.enter_context(nc.semaphore(n))
        sems[n] = Sem(h, n)
    sems_b = {e: sems[f"b_{e}"] for e in Plan.ENGINES}
    sems["bar"] = sems_b
    return sems


def declare_dram_A(nc, debug=False):
    io = {}
    ei = lambda n, shp, dt=F32: io.__setitem__(
        n, nc.dram_tensor(n, shp, dt, kind="ExternalInput").ap())
    eo = lambda n, shp, dt=F32: io.__setitem__(
        n, nc.dram_tensor(n, shp, dt, kind="ExternalOutput").ap())
    internal = lambda n, shp, dt=BF16: io.__setitem__(
        n, nc.dram_tensor(n, shp, dt, kind="Internal").ap())
    if debug:
        io["_dbg"] = []
        def dbg(n, shp, dt):
            io.__setitem__("dbg_" + n,
                           nc.dram_tensor("dbg_" + n, shp, dt,
                                          kind="ExternalOutput").ap())
            io["_dbg"].append(n)
        io["_mkdbg"] = dbg
    ei("x_tm", [NT, 128 * D])
    ei("x_fm", [NC_D, 128 * TPC])
    ei("w_qk", [64, 128 * 32 * 128])
    ei("w_v", [8, 128 * 32 * 512])
    ei("b_qk", [128, 64])
    ei("b_v_t", [128, D])
    ei("w_o", [8, 128 * 32 * 512])
    ei("ob_t", [128, D])
    ei("ident", [128, 128])
    internal("q_dr", [32, 128 * 1024])
    internal("kv_src", [64, 128 * 1024])
    internal("kv_dst", [128, 128 * 1024])
    internal("s1a", [NT, 128 * D], F32)
    eo("x1_tm", [NT, 128 * D])
    eo("x1_fm", [NC_D, 128 * TPC], BF16)
    eo("x1f_fm", [NT, 128 * TPC], BF16)
    if debug:
        io["_mkdbg"]("q_dr", [32, 128 * 1024], BF16)
        io["_mkdbg"]("kv_src", [64, 128 * 1024], BF16)
        io["_mkdbg"]("s1a", [NT, 128 * D], F32)
        io["dbg_ctx"] = nc.dram_tensor(
            "dbg_ctx", [32, 128 * 1024], BF16, kind="ExternalOutput").ap()
    return io


def build_program_A(debug=False):
    nc = bass.Bass("TRN2", target_bir_lowering=False)
    io = declare_dram_A(nc, debug=debug)
    pl = Plan()
    with ExitStack() as top:
        sems = make_sems(nc, top)
        glob = {}
        glob["ident"] = top.enter_context(nc.sbuf_tensor("g_ident", [128, 128], BF16))
        pl.op("gpsimd", lambda g: g.dma_start(
            out=glob["ident"][:, :], in_=io["ident"]), inc=(sems["wt"], 16))
        glob["id_tgt"] = sems["wt"].n
        glob["eps"] = top.enter_context(nc.sbuf_tensor("g_eps", [128, 1], F32))
        pl.op("vector", lambda v: v.memset(glob["eps"][:, :], 1e-5),
              inc=(sems["ev"], 1))
        glob["eps_tgt"] = sems["ev"].n

        with ExitStack() as st:
            cc_done = phase_A1(nc, pl, st, io, sems, glob)
            pl.barrier(sems["bar"], dma_sems=[sems["ld"], sems["wt"], sems["wtA"], sems["wtB"], sems["sto"]])
        with ExitStack() as st:
            ctx = st.enter_context(nc.sbuf_tensor("a2_ctx", [128, 32 * 1024], BF16))
            with ExitStack() as st2:
                ctx_done = phase_A2(nc, pl, st2, io, sems, glob, cc_done, ctx)
            if debug:
                for hh in range(32):
                    pl.op("sync", lambda s, hh=hh: s.dma_start(
                        out=row128(io["dbg_ctx"], hh),
                        in_=ctx[:, hh * 1024:(hh + 1) * 1024]),
                        waits=[(sems["ev"], ctx_done)] if hh == 0 else (),
                        inc=(sems["st"], 16))
            s1a_stores = phase_A3a(nc, pl, st, io, sems, glob, ctx, ctx_done)
            pl.barrier(sems["bar"], dma_sems=[
                sems["ld"], sems["ld2"], sems["wt"], sems["wtA"], sems["wtB"],
                sems["sto"],
                sems["lh0"], sems["lh1"], sems["lv0"], sems["lv1"]])
        with ExitStack() as st:
            phase_LN(nc, pl, st, io, sems, glob,
                     ["x_tm", "s1a"], "x1", s1a_stores, fold=True)
        if debug:
            for n in io["_dbg"]:
                pl.op("sync", lambda s, n=n: s.dma_start(
                    out=io["dbg_" + n], in_=io[n]),
                    waits=[(sems["st"], sems["st"].n)], inc=(sems["st"], 16))
        pl.wait("gpsimd", sems["st"], sems["st"].n)
        pl.wait("sync", sems["st"], sems["st"].n)
        pl.realize(nc)
    return nc


def declare_x(nc, io, prefix, kind_tm="ExternalInput"):
    io[f"{prefix}_tm"] = nc.dram_tensor(
        f"{prefix}_tm", [NT, 128 * D], F32, kind=kind_tm).ap()
    io[f"{prefix}_fm"] = nc.dram_tensor(
        f"{prefix}_fm", [NC_D, 128 * TPC], BF16, kind=kind_tm).ap()
    io[f"{prefix}f_fm"] = nc.dram_tensor(
        f"{prefix}f_fm", [NT, 128 * TPC], BF16, kind=kind_tm).ap()


def build_program_B(debug=False):
    nc = bass.Bass("TRN2", target_bir_lowering=False)
    io = {}
    if debug:
        io["_dbg"] = []
        def dbg(n, shp, dt):
            io["dbg_" + n] = nc.dram_tensor(
                "dbg_" + n, shp, dt, kind="ExternalOutput").ap()
            io["_dbg"].append(n)
        io["_mkdbg"] = dbg
    declare_x(nc, io, "x1", "ExternalInput")
    io["ident"] = nc.dram_tensor("ident", [128, 128], F32,
                                 kind="ExternalInput").ap()
    declare_dram_B(nc, io, debug=debug)
    io["x2_tm"] = nc.dram_tensor("x2_tm", [NT, 128 * D], F32,
                                 kind="ExternalOutput").ap()
    io["x2_fm"] = nc.dram_tensor("x2_fm", [NC_D, 128 * TPC], BF16,
                                 kind="ExternalOutput").ap()
    pl = Plan()
    with ExitStack() as top:
        sems = make_sems(nc, top)
        glob = {}
        glob["ident"] = top.enter_context(nc.sbuf_tensor("g_ident", [128, 128], BF16))
        pl.op("gpsimd", lambda g: g.dma_start(
            out=glob["ident"][:, :], in_=io["ident"]), inc=(sems["wt"], 16))
        glob["id_tgt"] = sems["wt"].n
        glob["eps"] = top.enter_context(nc.sbuf_tensor("g_eps", [128, 1], F32))
        pl.op("vector", lambda v: v.memset(glob["eps"][:, :], 1e-5),
              inc=(sems["ev"], 1))

        with ExitStack() as st:
            s1b_stores = phase_B(nc, pl, st, io, sems, glob)
            pl.barrier(sems["bar"], dma_sems=[sems["ld"], sems["wt"], sems["wtA"], sems["wtB"], sems["sto"]])
        with ExitStack() as st:
            phase_LN(nc, pl, st, io, sems, glob,
                     ["x1_tm", "s1b"], "x2", s1b_stores, fold=False)
        if debug:
            for n in io["_dbg"]:
                pl.op("sync", lambda s, n=n: s.dma_start(
                    out=io["dbg_" + n], in_=io[n]),
                    waits=[(sems["st"], sems["st"].n), (sems["ld"], sems["ld"].n)],
                    inc=(sems["st"], 16))
        pl.wait("gpsimd", sems["st"], sems["st"].n)
        pl.wait("sync", sems["st"], sems["st"].n)
        pl.realize(nc)
    return nc


def build_program_C(debug=False):
    nc = bass.Bass("TRN2", target_bir_lowering=False)
    io = {}
    if debug:
        io["_dbg"] = []
        def dbg(n, shp, dt):
            io["dbg_" + n] = nc.dram_tensor(
                "dbg_" + n, shp, dt, kind="ExternalOutput").ap()
            io["_dbg"].append(n)
        io["_mkdbg"] = dbg
    declare_x(nc, io, "x2", "ExternalInput")
    io["ident"] = nc.dram_tensor("ident", [128, 128], F32,
                                 kind="ExternalInput").ap()
    declare_dram_C(nc, io, debug=debug)
    io["x3_tm"] = nc.dram_tensor("x3_tm", [NT, 128 * D], F32,
                                 kind="ExternalOutput").ap()
    io["x3_fm"] = nc.dram_tensor("x3_fm", [NC_D, 128 * TPC], BF16,
                                 kind="ExternalOutput").ap()
    pl = Plan()
    with ExitStack() as top:
        sems = make_sems(nc, top)
        glob = {}
        glob["ident"] = top.enter_context(nc.sbuf_tensor("g_ident", [128, 128], BF16))
        pl.op("gpsimd", lambda g: g.dma_start(
            out=glob["ident"][:, :], in_=io["ident"]), inc=(sems["wt"], 16))
        glob["id_tgt"] = sems["wt"].n
        glob["eps"] = top.enter_context(nc.sbuf_tensor("g_eps", [128, 1], F32))
        pl.op("vector", lambda v: v.memset(glob["eps"][:, :], 1e-5),
              inc=(sems["ev"], 1))

        with ExitStack() as st:
            ctx, ctx_done = phase_C(nc, pl, st, io, sems, glob)
            s1c_stores = phase_A3a(nc, pl, st, io, sems, glob, ctx, ctx_done,
                                   w_key="o_w_t", bias_key="zeros_t",
                                   s1a_key="s1c")
            pl.barrier(sems["bar"], dma_sems=[
                sems["ld"], sems["ld2"], sems["wt"], sems["wtA"], sems["wtB"],
                sems["sto"],
                sems["lh0"], sems["lh1"], sems["lv0"], sems["lv1"]])
        with ExitStack() as st:
            phase_LN(nc, pl, st, io, sems, glob,
                     ["x2_tm", "s1c"], "x3", s1c_stores, fold=False)
        if debug:
            for n in io["_dbg"]:
                pl.op("sync", lambda s, n=n: s.dma_start(
                    out=io["dbg_" + n], in_=io[n]),
                    waits=[(sems["st"], sems["st"].n), (sems["sto"], sems["sto"].n)],
                    inc=(sems["st"], 16))
        pl.wait("gpsimd", sems["st"], sems["st"].n)
        pl.wait("sync", sems["st"], sems["st"].n)
        pl.realize(nc)
    return nc




def build_program_D(debug=False):
    nc = bass.Bass("TRN2", target_bir_lowering=False)
    io = {}
    if debug:
        io["_dbg"] = []
        def dbg(n, shp, dt):
            io["dbg_" + n] = nc.dram_tensor(
                "dbg_" + n, shp, dt, kind="ExternalOutput").ap()
            io["_dbg"].append(n)
        io["_mkdbg"] = dbg
    declare_x(nc, io, "x3", "ExternalInput")
    io["ident"] = nc.dram_tensor("ident", [128, 128], F32,
                                 kind="ExternalInput").ap()
    declare_dram_D(nc, io, debug=debug)
    io["x4_tm"] = nc.dram_tensor("x4_tm", [NT, 128 * D], F32,
                                 kind="ExternalOutput").ap()
    pl = Plan()
    with ExitStack() as top:
        sems = make_sems(nc, top)
        glob = {}
        glob["ident"] = top.enter_context(nc.sbuf_tensor("g_ident", [128, 128], BF16))
        pl.op("gpsimd", lambda g: g.dma_start(
            out=glob["ident"][:, :], in_=io["ident"]), inc=(sems["wt"], 16))
        glob["id_tgt"] = sems["wt"].n
        glob["eps"] = top.enter_context(nc.sbuf_tensor("g_eps", [128, 1], F32))
        pl.op("vector", lambda v: v.memset(glob["eps"][:, :], 1e-5),
              inc=(sems["ev"], 1))
        with ExitStack() as st:
            s1d_stores = phase_D(nc, pl, st, io, sems, glob)
            pl.barrier(sems["bar"], dma_sems=[
                sems["ld"], sems["wt"], sems["wtA"], sems["wtB"], sems["sto"],
                sems["lh0"], sems["lh1"]])
        with ExitStack() as st:
            phase_LN(nc, pl, st, io, sems, glob,
                     ["x3_tm", "s1d"], "x4", s1d_stores, fold=False,
                     want_fm=False)
        if debug:
            for n in io["_dbg"]:
                pl.op("sync", lambda s, n=n: s.dma_start(
                    out=io["dbg_" + n], in_=io[n]),
                    waits=[(sems["st"], sems["st"].n), (sems["sto"], sems["sto"].n)],
                    inc=(sems["st"], 16))
        pl.wait("gpsimd", sems["st"], sems["st"].n)
        pl.wait("sync", sems["st"], sems["st"].n)
        pl.realize(nc)
    return nc


def _declare_full(nc, io, debug):
    ei = lambda n, shp, dt=F32: io.__setitem__(
        n, nc.dram_tensor(n, shp, dt, kind="ExternalInput").ap())
    internal = lambda n, shp, dt=BF16: io.__setitem__(
        n, nc.dram_tensor(n, shp, dt, kind="Internal").ap())
    ei("x_tm", [NT, 128 * D])
    ei("x_fm", [NC_D, 128 * TPC])
    ei("ident", [128, 128])
    ei("w_qk", [64, 128 * 32 * 128])
    ei("w_v", [8, 128 * 32 * 512])
    ei("b_qk", [128, 64])
    ei("b_v_t", [128, D])
    ei("w_o", [8, 128 * 32 * 512])
    ei("ob_t", [128, D])
    internal("q_dr", [32, 128 * 1024])
    internal("kv_src", [64, 128 * 1024])
    internal("kv_dst", [128, 128 * 1024])
    internal("s1a", [NT, 128 * D], F32)
    for p in ("x1", "x2", "x3"):
        internal(f"{p}_tm", [NT, 128 * D], F32)
        internal(f"{p}_fm", [NC_D, 128 * TPC], BF16)
        internal(f"{p}f_fm", [NT, 128 * TPC], BF16)
    declare_dram_B(nc, io, debug=debug)
    declare_dram_C(nc, io, debug=debug)
    declare_dram_D(nc, io, debug=debug)
    io["out_tm"] = nc.dram_tensor("out_tm", [NT, 128 * D], F32,
                                  kind="ExternalOutput").ap()


def build_full(debug=False):
    nc = bass.Bass("TRN2", target_bir_lowering=False)
    io = {}
    if debug:
        io["_dbg"] = []
        def dbg(n, shp, dt):
            io["dbg_" + n] = nc.dram_tensor(
                "dbg_" + n, shp, dt, kind="ExternalOutput").ap()
            io["_dbg"].append(n)
        io["_mkdbg"] = dbg
    _declare_full(nc, io, debug)
    pl = Plan()
    with ExitStack() as top:
        sems = make_sems(nc, top)
        glob = {}
        glob["ident"] = top.enter_context(nc.sbuf_tensor("g_ident", [128, 128], BF16))
        pl.op("gpsimd", lambda g: g.dma_start(
            out=glob["ident"][:, :], in_=io["ident"]), inc=(sems["wt"], 16))
        glob["id_tgt"] = sems["wt"].n
        glob["eps"] = top.enter_context(nc.sbuf_tensor("g_eps", [128, 1], F32))
        pl.op("vector", lambda v: v.memset(glob["eps"][:, :], 1e-5),
              inc=(sems["ev"], 1))

        allsems = [sems[k] for k in ("ld", "ld2", "wt", "wtA", "wtB", "sto",
                                     "lh0", "lh1", "lv0", "lv1", "st")]

        def bar():
            pl.barrier(sems["bar"], dma_sems=allsems)

        with ExitStack() as st:
            cc_done = phase_A1(nc, pl, st, io, sems, glob)
            bar()
        with ExitStack() as st:
            ctx = st.enter_context(nc.sbuf_tensor("a2_ctx", [128, 32 * 1024],
                                                  BF16, side="right"))
            with ExitStack() as st2:
                ctx_done = phase_A2(nc, pl, st2, io, sems, glob, cc_done, ctx)
            s1a_stores = phase_A3a(nc, pl, st, io, sems, glob, ctx, ctx_done)
            bar()
        with ExitStack() as st:
            fm1 = phase_LN(nc, pl, st, io, sems, glob,
                           ["x_tm", "s1a"], "x1", s1a_stores, fold=True)
            bar()
        with ExitStack() as st:
            s1b_stores = phase_B(nc, pl, st, io, sems, glob, fm_ready=fm1)
            bar()
        with ExitStack() as st:
            fm2 = phase_LN(nc, pl, st, io, sems, glob,
                           ["x1_tm", "s1b"], "x2", s1b_stores, fold=False)
            bar()
        with ExitStack() as st:
            ctx2, ctx2_done = phase_C(nc, pl, st, io, sems, glob,
                                      x_fm_key="x2_fm", fm_ready=fm2)
            s1c_stores = phase_A3a(nc, pl, st, io, sems, glob, ctx2, ctx2_done,
                                   w_key="o_w_t", bias_key="zeros_t",
                                   s1a_key="s1c")
            bar()
        with ExitStack() as st:
            fm3 = phase_LN(nc, pl, st, io, sems, glob,
                           ["x2_tm", "s1c"], "x3", s1c_stores, fold=False)
            bar()
        with ExitStack() as st:
            s1d_stores = phase_D(nc, pl, st, io, sems, glob,
                                 x_fm_key="x3_fm", fm_ready=fm3)
            bar()
        with ExitStack() as st:
            phase_LN(nc, pl, st, io, sems, glob,
                     ["x3_tm", "s1d"], "out", s1d_stores, fold=False,
                     want_fm=False)
        if debug:
            for n in io["_dbg"]:
                pl.op("sync", lambda s, n=n: s.dma_start(
                    out=io["dbg_" + n], in_=io[n]),
                    waits=[(sems["st"], sems["st"].n),
                           (sems["sto"], sems["sto"].n)],
                    inc=(sems["st"], 16))
        pl.wait("gpsimd", sems["st"], sems["st"].n)
        pl.wait("sync", sems["st"], sems["st"].n)
        pl.realize(nc)
    return nc


def host_prep_full(inputs):
    per_a = host_prep_A(inputs)
    pb = host_prep_B(inputs)
    pc = host_prep_C(inputs)
    pd = host_prep_D(inputs)
    per_core = []
    for c in range(NCORES):
        d = dict(per_a[c])
        d.update(pb)
        d.update(pc)
        d.update(host_prep_C_core(c))
        d.update(pd)
        d.update(host_prep_D_core(c))
        per_core.append(d)
    return per_core


_PROGS = {}


def _prog(name):
    if name not in _PROGS:
        _PROGS[name] = {
            "A": build_program_A, "B": build_program_B,
            "C": build_program_C, "D": build_program_D,
            "full": build_full,
        }[name](debug=False)
    return _PROGS[name]


def kernel(**inputs):
    per_core = host_prep_full(inputs)
    res = run_bass_kernel_spmd(_prog("full"), per_core,
                               core_ids=list(range(NCORES)))
    out = np.zeros((4, 2048, 4096), np.float32)
    for c in range(NCORES):
        b, p = c // 2, c % 2
        tiles = [2 * j + p for j in range(NT)]
        x4 = res.results[c]["out_tm"].reshape(NT, 128, 4096)
        for j, g in enumerate(tiles):
            out[b, g * 128:(g + 1) * 128, :] = x4[j]
    return out


# version 11
# speedup vs baseline: 1.3001x; 1.0612x over previous
import numpy as np
import ml_dtypes
from contextlib import ExitStack
import concourse.bass as bass
import concourse.mybir as mybir
from concourse.bass_utils import run_bass_kernel_spmd

F32 = mybir.dt.float32
BF16 = mybir.dt.bfloat16
AF = mybir.ActivationFunctionType
ALU = mybir.AluOpType
AX = mybir.AxisListType

NCORES = 8
NT = 8
TPC = NT * 128
D = 4096
NC_D = D // 128
BF = ml_dtypes.bfloat16

REPLICA_PAIRS = [[0, 1], [2, 3], [4, 5], [6, 7]]


class Sem:
    def __init__(self, handle, name):
        self.h = handle
        self.name = name
        self.n = 0

    def bump(self, amt):
        self.n += amt
        return self.n


class Plan:
    ENGINES = ("tensor", "vector", "scalar", "gpsimd", "sync")

    def __init__(self):
        self.streams = {e: [] for e in self.ENGINES}
        self.n_ops = 0

    def op(self, engine, fn, waits=(), inc=None):
        wl = []
        for (s, v) in waits:
            assert v <= s.n, f"wait on {s.name}={v} > planned {s.n}"
            wl.append((s.h, v))
        if inc is not None:
            sem, amt = inc
            sem.bump(amt)
            il = (sem.h, amt)
        else:
            il = None
        self.n_ops += 1

        def run(eng):
            for (h, v) in wl:
                eng.wait_ge(h, v)
            ins = fn(eng)
            if il is not None:
                assert ins is not None
                ins.then_inc(il[0], il[1])
        self.streams[engine].append(run)

    def wait(self, engine, sem, val):
        self.op(engine, lambda e: None, waits=[(sem, val)])

    def fence(self, sems, queues=("sync", "gpsimd", "scalar")):
        keys = ["ld", "ld2", "wt", "wtA", "wtB", "sto", "lh0", "lh1",
                "lv0", "lv1", "st", "pe", "ev", "ac"]
        targets = [(sems[k], sems[k].n) for k in keys]
        for q in queues:
            self.op(q, lambda e: None, waits=targets)

    def barrier(self, sems, dma_sems=()):
        engines = self.ENGINES
        dma_targets = [(s, s.n) for s in dma_sems]
        for e in engines:
            for (s, v) in dma_targets:
                self.wait(e, s, v)
        targets = {}
        for e in engines:
            targets[e] = sems[e].bump(1)
        for e in engines:
            s = sems[e]

            def mk(s=s):
                def run(eng):
                    eng.sem_inc(s.h, 1)
                return run
            self.streams[e].append(mk())
            self.n_ops += 1
        for e in engines:
            for o in engines:
                self.wait(e, sems[o], targets[o])

    def realize(self, nc):
        with nc.Block() as block:
            @block.tensor
            def _(t):
                for fn in self.streams["tensor"]:
                    fn(t)

            @block.vector
            def _(v):
                for fn in self.streams["vector"]:
                    fn(v)

            @block.scalar
            def _(s):
                for fn in self.streams["scalar"]:
                    fn(s)

            @block.gpsimd
            def _(g):
                for fn in self.streams["gpsimd"]:
                    fn(g)

            @block.sync
            def _(sy):
                for fn in self.streams["sync"]:
                    fn(sy)


_NAME_SEQ = [0]


def _uname(n):
    _NAME_SEQ[0] += 1
    return f"{n}_{_NAME_SEQ[0]}"


def row128(ap2d, i):
    return ap2d[i:i + 1, :].rearrange("o (p f) -> (o p) f", p=128)



def _tile_lhst(w, nkc, nm):
    i_, o_ = w.shape
    assert i_ == nkc * 128 and o_ == nm * 128
    t = w.reshape(nkc, 128, nm, 128).transpose(2, 1, 0, 3)
    return np.ascontiguousarray(t.reshape(nm, 128 * nkc * 128))


def _tile_rhs(w, nkc, nob, obw=512):
    i_, o_ = w.shape
    assert i_ == nkc * 128 and o_ == nob * obw
    t = w.reshape(nkc, 128, nob, obw).transpose(2, 1, 0, 3)
    return np.ascontiguousarray(t.reshape(nob, 128 * nkc * obw))


def host_prep_A(inputs):
    x = np.asarray(inputs["L0_attn_x"], dtype=np.float32)
    qkvw = np.asarray(inputs["L0_attn_attn_qkvw"], dtype=np.float32)
    qkvb = np.asarray(inputs["L0_attn_attn_qkvb"], dtype=np.float32)
    ow = np.asarray(inputs["L0_attn_attn_ow"], dtype=np.float32)
    ob = np.asarray(inputs["L0_attn_attn_ob"], dtype=np.float32)

    shared = {
        "w_qk": _tile_lhst(qkvw[:, :8192], 32, 64),
        "w_v": _tile_rhs(qkvw[:, 8192:], 32, 8),
        "b_qk": np.ascontiguousarray(qkvb[:8192].reshape(64, 128).T),
        "b_v_t": np.ascontiguousarray(np.tile(qkvb[8192:][None, :], (128, 1))),
        "w_o": _tile_rhs(ow, 32, 8),
        "ob_t": np.ascontiguousarray(np.tile(ob[None, :], (128, 1))),
        "ident": np.eye(128, dtype=np.float32),
    }
    per_core = []
    for c in range(NCORES):
        b, p = c // 2, c % 2
        tiles = [2 * j + p for j in range(NT)]
        idx = np.concatenate([np.arange(g * 128, g * 128 + 128) for g in tiles])
        xs = x[b, idx, :]
        d = dict(shared)
        d["x_tm"] = np.ascontiguousarray(xs.reshape(NT, 128 * D))
        d["x_fm"] = np.ascontiguousarray(xs.T.reshape(NC_D, 128 * TPC))
        per_core.append(d)
    return per_core


def _hperm():
    tiles = []
    for fc in range(21):
        for r in range(4):
            tiles.append(np.arange(2752 * r + 128 * fc, 2752 * r + 128 * fc + 128))
    tiles.append(np.concatenate([np.arange(2752 * 0 + 2688, 2752 * 0 + 2752),
                                 np.arange(2752 * 1 + 2688, 2752 * 1 + 2752)]))
    tiles.append(np.concatenate([np.arange(2752 * 2 + 2688, 2752 * 2 + 2752),
                                 np.arange(2752 * 3 + 2688, 2752 * 3 + 2752)]))
    return np.concatenate(tiles)


def host_prep_B(inputs):
    w1q = np.asarray(inputs["L0_ffn_w1q"], dtype=np.float32)
    a1 = np.asarray(inputs["L0_ffn_a1"], dtype=np.float32)
    b1 = np.asarray(inputs["L0_ffn_b1"], dtype=np.float32)
    w3q = np.asarray(inputs["L0_ffn_w3q"], dtype=np.float32)
    a3 = np.asarray(inputs["L0_ffn_a3"], dtype=np.float32)
    b3 = np.asarray(inputs["L0_ffn_b3"], dtype=np.float32)
    w2q = np.asarray(inputs["L0_ffn_w2q"], dtype=np.float32)
    a2 = np.asarray(inputs["L0_ffn_a2"], dtype=np.float32)
    b2 = np.asarray(inputs["L0_ffn_b2"], dtype=np.float32)

    hp = _hperm()
    d = {}
    d["w1t"] = _tile_lhst(np.ascontiguousarray(w1q.T[:, hp]), 8, 86)
    d["w3t"] = _tile_lhst(np.ascontiguousarray(w3q.T[:, hp]), 8, 86)
    w2p = np.zeros((2816, 4096), dtype=np.float32)
    w2p[:2752] = w2q.T
    d["w2t"] = _tile_rhs(w2p, 22, 8)
    d["a1t"] = np.ascontiguousarray(a1.reshape(2, 32, 128).transpose(2, 1, 0).reshape(128, 64))
    d["a3t"] = np.ascontiguousarray(a3.reshape(2, 32, 128).transpose(2, 1, 0).reshape(128, 64))
    a2p = a2[:, hp]
    d["a2t"] = np.ascontiguousarray(a2p.reshape(2, 86, 128).transpose(2, 1, 0).reshape(128, 172))
    d["b1T"] = np.ascontiguousarray(b1.T[:, hp])
    d["b3T"] = np.ascontiguousarray(b3.T[:, hp])
    d["b2T"] = np.ascontiguousarray(b2.T)
    f2 = np.zeros((128, 64), dtype=np.float32)
    for k in range(128):
        f2[k, k % 64] = 1.0
    d["fold2"] = f2
    return d


def declare_dram_B(nc, io, debug=False):
    ei = lambda n, shp, dt=F32: io.__setitem__(
        n, nc.dram_tensor(n, shp, dt, kind="ExternalInput").ap())
    ei("w1t", [86, 128 * 8 * 128])
    ei("w3t", [86, 128 * 8 * 128])
    ei("w2t", [8, 128 * 22 * 512])
    ei("a1t", [128, 64])
    ei("a3t", [128, 64])
    ei("a2t", [128, 172])
    ei("b1T", [2, 11008])
    ei("b3T", [2, 11008])
    ei("b2T", [2, 4096])
    ei("fold2", [128, 64])
    io["s1b"] = nc.dram_tensor("s1b", [NT, 128 * D], F32, kind="Internal").ap()
    if debug:
        io["_mkdbg"]("s1b", [NT, 128 * D], F32)
        io["dbg_hpfb"] = nc.dram_tensor(
            "dbg_hpfb", [22, 128 * 1024], BF16, kind="ExternalOutput").ap()
        io["dbg_small"] = nc.dram_tensor(
            "dbg_small", [2, 3 * 1024], F32, kind="ExternalOutput").ap()
        io["dbg_pr"] = nc.dram_tensor(
            "dbg_pr", [3, 128 * 1024], BF16, kind="ExternalOutput").ap()


def phase_B(nc, pl, st, io, sems, glob, x_fm_key="x1_fm", xf_key="x1f_fm",
            out_key="s1b", fm_ready=0):
    ld, ld2, wt = sems["ld"], sems["ld2"], sems["wt"]
    pe, ev, ac = sems["pe"], sems["ev"], sems["ac"]
    sto = sems["sto"]
    wtP = [sems["wtA"], sems["wtB"]]

    sb = lambda n, shp, dt=BF16: st.enter_context(nc.sbuf_tensor(_uname(n), shp, dt))
    psa = lambda n, shp, dt=F32: st.enter_context(nc.psum_tensor(_uname(n), shp, dt))

    NF = 22

    xf_bf = sb("b_xf", [128, NT * 1024])
    hpfb = sb("b_hpfb", [128, NF * 1024])
    b2T = sb("b_b2T", [2, 4096])
    h1s = sb("b_h1s", [2, 1024])
    h3s = sb("b_h3s", [2, 1024])
    h2s = sb("b_h2s", [2, 1024])

    pl.op("sync", lambda s: s.dma_start(
        out=xf_bf[:, :].rearrange("p (c f) -> p c f", c=NT),
        in_=io[xf_key].rearrange("c (p f) -> p c f", p=128)),
        waits=[(sems["st"], fm_ready)], inc=(ld, 16))
    pl.op("gpsimd", lambda g: g.dma_start(out=b2T[:, :], in_=io["b2T"]),
          inc=(wt, 16))
    xf_tgt = ld.n
    pl.op("vector", lambda v: v.memset(
        hpfb[:, :].rearrange("p (c f) -> p c f", c=NF)[64:128, 21, :], 0.0),
        waits=[(ev, ev.n)], inc=(ev, 1))

    with ExitStack() as st2:
        x1fm = st2.enter_context(nc.sbuf_tensor("b_x1fm", [128, NC_D * 1024], BF16))
        pl.fence(sems)
        a1t = st2.enter_context(nc.sbuf_tensor("b_a1t", [128, 64], BF16))
        a3t = st2.enter_context(nc.sbuf_tensor("b_a3t", [128, 64], BF16))
        ps_h = st2.enter_context(nc.psum_tensor("b_psh", [2, 2048], F32))
        pl.op("sync", lambda s: s.dma_start(
            out=x1fm[:, :].rearrange("p (c f) -> p c f", c=NC_D),
            in_=io[x_fm_key].rearrange("c (p f) -> p c f", p=128)),
            waits=[(sems["st"], fm_ready)], inc=(ld, 16))
        pl.op("gpsimd", lambda g: g.dma_start(out=a1t[:, :], in_=io["a1t"]),
              inc=(wt, 16))
        pl.op("gpsimd", lambda g: g.dma_start(out=a3t[:, :], in_=io["a3t"]),
              inc=(wt, 16))
        lw = [(ld, ld.n), (wt, wt.n), (ev, ev.n), (ac, ac.n)]
        for at, off in ((a1t, 0), (a3t, 1024)):
            for kc in range(NC_D):
                for h2i in range(2):
                    pl.op("tensor", lambda t, kc=kc, h2i=h2i, at=at, off=off: t.matmul(
                        ps_h[:, off + h2i * 512: off + (h2i + 1) * 512],
                        at[:, kc * 2:kc * 2 + 2],
                        x1fm[:, kc * 1024 + h2i * 512: kc * 1024 + (h2i + 1) * 512],
                        start=(kc == 0), stop=(kc == 31)),
                        waits=lw if (kc == 0 and h2i == 0 and off == 0) else (),
                        inc=(pe, 1) if (kc == 31 and h2i == 1) else None)
        h_tgt = pe.n
        pl.op("vector", lambda v: v.tensor_scalar(
            h1s[:, :], ps_h[:, 0:1024], 0.5, None, ALU.mult, ALU.bypass),
            waits=[(pe, h_tgt), (ev, ev.n)], inc=(ev, 1))
        pl.op("vector", lambda v: v.tensor_scalar(
            h3s[:, :], ps_h[:, 1024:2048], 0.5, None, ALU.mult, ALU.bypass),
            waits=[(ev, ev.n)], inc=(ev, 1))
        h13_done = ev.n

    with ExitStack() as st1:
        sb1 = lambda n, shp, dt=BF16: st1.enter_context(nc.sbuf_tensor(_uname(n), shp, dt))
        ps1 = lambda n, shp: st1.enter_context(nc.psum_tensor(_uname(n), shp, F32))
        b1T = sb1("b_b1T", [2, 11008])
        b3T = sb1("b_b3T", [2, 11008])
        a2t = sb1("b_a2t", [128, 172])
        eyef = sb1("b_eye", [128, 128])
        fold2 = sb1("b_fold2", [128, 64])
        w1sb = [sb1(f"b_w1_{i}", [128, 8 * 128]) for i in range(2)]
        w3sb = [sb1(f"b_w3_{i}", [128, 8 * 128]) for i in range(2)]
        gstg = [sb1(f"b_g{i}", [128, 512]) for i in range(2)]
        ustg = [sb1(f"b_u{i}", [128, 512]) for i in range(2)]
        prod = [sb1(f"b_p{i}", [128, 512]) for i in range(2)]
        ps_g = [ps1(f"b_psg{i}", [128, 512]) for i in range(2)]
        ps_u = [ps1(f"b_psu{i}", [128, 512]) for i in range(2)]
        ps_h2 = ps1("b_psh2", [2, 1024])
        ps_f = ps1("b_psf", [128, 1024])

        pl.fence(sems)
        first_small = True
        for nm, dst in (("b1T", b1T), ("b3T", b3T), ("a2t", a2t),
                        ("fold2", fold2), ("ident", eyef)):
            pl.op("gpsimd", lambda g, nm=nm, dst=dst: g.dma_start(
                out=dst[:, :], in_=io[nm]),
                waits=[(ev, h13_done)] if first_small else (), inc=(wt, 16))
            first_small = False
        smalls_tgt = wt.n

        w_tgt = [0, 0]
        prod_pe = {0: pe.n, 1: pe.n}
        stg_ac = {0: ac.n, 1: ac.n}
        hpfb_evs = []
        mi = 0
        for fc in range(NF):
            nr = 4 if fc < 21 else 2
            for r in range(nr):
                mh = fc * 4 + r if fc < 21 else 84 + r
                buf = mi % 2
                lw = [(pe, prod_pe[buf])]
                pl.op("gpsimd", lambda g, mh=mh, buf=buf: g.dma_start(
                    out=w1sb[buf][:, :], in_=row128(io["w1t"], mh)),
                    waits=lw, inc=(wtP[buf], 16))
                pl.op("gpsimd", lambda g, mh=mh, buf=buf: g.dma_start(
                    out=w3sb[buf][:, :], in_=row128(io["w3t"], mh)),
                    inc=(wtP[buf], 16))
                w_tgt[buf] = wtP[buf].n
                for th in range(2):
                    mw = [(wtP[buf], w_tgt[buf]), (wt, wt.n), (ld, xf_tgt),
                          (ev, h13_done), (ac, stg_ac[buf])]
                    for wsb, hs, bT, pp in ((w1sb, h1s, b1T, ps_g),
                                            (w3sb, h3s, b3T, ps_u)):
                        for kc in range(NT):
                            pl.op("tensor",
                                  lambda t, kc=kc, th=th, buf=buf, wsb=wsb, pp=pp: t.matmul(
                                      pp[buf][:, :],
                                      wsb[buf][:, kc * 128:(kc + 1) * 128],
                                      xf_bf[:, kc * 1024 + th * 512: kc * 1024 + th * 512 + 512],
                                      start=(kc == 0), stop=False),
                                  waits=mw if (kc == 0 and wsb is w1sb) else ())
                        pl.op("tensor",
                              lambda t, mh=mh, th=th, buf=buf, bT=bT, hs=hs, pp=pp: t.matmul(
                                  pp[buf][:, :], bT[:, mh * 128:(mh + 1) * 128],
                                  hs[:, th * 512:th * 512 + 512],
                                  start=False, stop=True),
                              inc=(pe, 1))
                    gu_tgt = pe.n
                    pl.op("scalar", lambda s, buf=buf: s.activation(
                        gstg[buf][:, :], ps_g[buf][:, :], AF.Silu),
                        waits=[(pe, gu_tgt), (ev, ev.n), (ac, ac.n)], inc=(ac, 1))
                    pl.op("scalar", lambda s, buf=buf: s.activation(
                        ustg[buf][:, :], ps_u[buf][:, :], AF.Copy),
                        waits=[(ac, ac.n)], inc=(ac, 1))
                    stg_ac[buf] = ac.n
                    pl.op("vector", lambda v, buf=buf: v.tensor_mul(
                        prod[buf][:, :], gstg[buf][:, :], ustg[buf][:, :]),
                        waits=[(ac, ac.n), (ev, ev.n), (pe, pe.n)], inc=(ev, 1))
                    prod_tgt = ev.n
                    if "dbg_pr" in io and mi == 0:
                        pl.op("sync", lambda s, th=th, buf=buf: s.dma_start(
                            out=row128(io["dbg_pr"], 0)[:, th * 512:th * 512 + 512],
                            in_=prod[buf][:, :]),
                            waits=[(ev, prod_tgt)], inc=(ld, 16))
                        pl.op("sync", lambda s, th=th, buf=buf: s.dma_start(
                            out=row128(io["dbg_pr"], 1)[:, th * 512:th * 512 + 512],
                            in_=gstg[buf][:, :]), inc=(ld, 16))
                        pl.op("sync", lambda s, th=th, buf=buf: s.dma_start(
                            out=row128(io["dbg_pr"], 2)[:, th * 512:th * 512 + 512],
                            in_=ustg[buf][:, :]), inc=(ld, 16))
                    pl.op("tensor", lambda t, mh=mh, th=th, buf=buf: t.matmul(
                        ps_h2[:, th * 512:th * 512 + 512],
                        a2t[:, mh * 2:mh * 2 + 2], prod[buf][:, :],
                        start=(mi == 0), stop=(mi == 85),
                        skip_group_check=True),
                        waits=[(ev, prod_tgt)], inc=(pe, 1))
                    fmat = eyef if fc < 21 else fold2
                    rows = 128 if fc < 21 else 64
                    fold_start = (r == 0)
                    fold_stop = (r == nr - 1)
                    fw = [(ev, prod_tgt)]
                    if fold_start and hpfb_evs:
                        fw.append((ev, hpfb_evs[-1]))
                    pl.op("tensor", lambda t, th=th, buf=buf, fmat=fmat, rows=rows,
                          fold_start=fold_start, fold_stop=fold_stop: t.matmul(
                              ps_f[0:rows, th * 512:th * 512 + 512],
                              fmat[:, :], prod[buf][:, :],
                              start=fold_start, stop=fold_stop,
                              skip_group_check=True),
                          waits=fw, inc=(pe, 1))
                    prod_pe[buf] = pe.n
                if (fc < 21 and r == 3) or (fc == 21 and r == nr - 1):
                    f_tgt = pe.n
                    rows = 128 if fc < 21 else 64
                    pl.op("vector", lambda v, fc=fc, rows=rows: v.tensor_copy(
                        hpfb[0:rows, fc * 1024:(fc + 1) * 1024],
                        ps_f[0:rows, :]),
                        waits=[(pe, f_tgt), (ev, ev.n)], inc=(ev, 1))
                    hpfb_evs.append(ev.n)
                mi += 1
        b1_done_pe = pe.n
        pl.op("vector", lambda v: v.tensor_scalar(
            h2s[:, :], ps_h2[:, :], 0.5, None, ALU.mult, ALU.bypass),
            waits=[(pe, b1_done_pe), (ev, ev.n)], inc=(ev, 1))
        hpfb_done = ev.n
        if "dbg_hpfb" in io:
            for c in range(NF):
                pl.op("sync", lambda s, c=c: s.dma_start(
                    out=row128(io["dbg_hpfb"], c),
                    in_=hpfb[:, c * 1024:(c + 1) * 1024]),
                    waits=[(ev, hpfb_done)] if c == 0 else (), inc=(ld, 16))
            pl.op("gpsimd", lambda g: g.dma_start(
                out=io["dbg_small"][:, 0:1024], in_=h1s[:, :]),
                waits=[(ev, hpfb_done)], inc=(wt, 16))
            pl.op("gpsimd", lambda g: g.dma_start(
                out=io["dbg_small"][:, 1024:2048], in_=h3s[:, :]), inc=(wt, 16))
            pl.op("gpsimd", lambda g: g.dma_start(
                out=io["dbg_small"][:, 2048:3072], in_=h2s[:, :]), inc=(wt, 16))

    with ExitStack() as st3:
        sb3 = lambda n, shp, dt=BF16: st3.enter_context(nc.sbuf_tensor(_uname(n), shp, dt))
        ps3 = lambda n, shp: st3.enter_context(nc.psum_tensor(_uname(n), shp, F32))
        w2sb = [sb3(f"b_w2_{i}", [128, 22 * 512]) for i in range(2)]
        ostg = [sb3(f"b_o{i}", [128, 512], F32) for i in range(4)]
        pl.fence(sems)
        ps_o = [ps3(f"b_pso{i}", [128, 512]) for i in range(4)]
        w2_tgt = [0, 0]
        pe_b2_0 = pe.n
        for obi in range(8):
            buf = obi % 2
            lw = [(ev, hpfb_done)] if obi < 2 else [(pe, max(pe_b2_0, pe.n - 8))]
            pl.op("gpsimd", lambda g, obi=obi, buf=buf: g.dma_start(
                out=w2sb[buf][:, :], in_=row128(io["w2t"], obi)), waits=lw,
                inc=(wtP[buf], 16))
            w2_tgt[buf] = wtP[buf].n
            for jt in range(NT):
                pp = ps_o[jt % 4]
                mmw = [(wtP[buf], w2_tgt[buf]), (ev, hpfb_done), (ev, ev.n)]
                for kc in range(NF):
                    pl.op("tensor", lambda t, jt=jt, kc=kc, pp=pp, buf=buf: t.matmul(
                        pp[:, :],
                        hpfb[:, kc * 1024 + jt * 128: kc * 1024 + (jt + 1) * 128],
                        w2sb[buf][:, kc * 512:(kc + 1) * 512],
                        start=(kc == 0), stop=False),
                        waits=mmw if kc == 0 else ())
                pl.op("tensor", lambda t, jt=jt, obi=obi, pp=pp: t.matmul(
                    pp[:, :], h2s[:, jt * 128:(jt + 1) * 128],
                    b2T[:, obi * 512:(obi + 1) * 512], start=False, stop=True),
                    inc=(pe, 1))
                mm_tgt = pe.n
                dst = ostg[jt % 4]
                pl.op("vector", lambda v, pp=pp, dst=dst: v.tensor_copy(
                    dst[:, :], pp[:, :]),
                    waits=[(pe, mm_tgt), (sto, sto.n), (ev, ev.n)], inc=(ev, 1))
                ev_tgt = ev.n
                pl.op("sync", lambda s, jt=jt, obi=obi, dst=dst: s.dma_start(
                    out=row128(io[out_key], jt)[:, obi * 512:(obi + 1) * 512],
                    in_=dst[:, :]), waits=[(ev, ev_tgt)], inc=(sto, 16))
    return sto.n



H_L1 = 32
NOPE, ROPE, VD = 128, 64, 128
QKD = NOPE + ROPE
Q_RANK, KV_RANK = 1536, 512
IDX_H, IDX_D, TOPK = 16, 128, 1024
SC_QK = float(QKD ** -0.5)
SC_IDX = float((IDX_H ** -0.5) * (IDX_D ** -0.5))
NEG = -1e30


def host_prep_C(inputs, core=None):
    q_a_w = np.asarray(inputs["L1_attn_q_a_w"], np.float32)
    q_a_nw = np.asarray(inputs["L1_attn_q_a_norm_w"], np.float32)
    q_b_w = np.asarray(inputs["L1_attn_q_b_w"], np.float32)
    kv_a_w = np.asarray(inputs["L1_attn_kv_a_w"], np.float32)
    kv_a_nw = np.asarray(inputs["L1_attn_kv_a_norm_w"], np.float32)
    kv_b_w = np.asarray(inputs["L1_attn_kv_b_w"], np.float32)
    o_w = np.asarray(inputs["L1_attn_o_w"], np.float32)
    idx_wq = np.asarray(inputs["L1_attn_idx_wq_w"], np.float32)
    idx_wk = np.asarray(inputs["L1_attn_idx_wk_w"], np.float32)
    idx_ww = np.asarray(inputs["L1_attn_idx_weights_w"], np.float32)

    d = {}
    d["q_a_t"] = _tile_rhs(q_a_w, 32, 3)
    d["qnw"] = np.ascontiguousarray(q_a_nw.reshape(12, 128).T)
    cols_n = np.concatenate([np.arange(h * 192, h * 192 + 128) for h in range(32)])
    cols_r = np.concatenate([np.arange(h * 192 + 128, (h + 1) * 192) for h in range(32)])
    d["qb_n_t"] = _tile_lhst(q_b_w[:, cols_n], 12, 32)
    d["qb_r_t"] = _tile_lhst(q_b_w[:, cols_r], 12, 16)
    d["kv_a_t"] = np.ascontiguousarray(
        kv_a_w.reshape(32, 128, 576).transpose(1, 0, 2).reshape(128, 32 * 576))
    d["kvnw"] = np.ascontiguousarray(kv_a_nw.reshape(4, 128).T)
    cols_kn = np.concatenate([np.arange(h * 256, h * 256 + 128) for h in range(32)])
    cols_v = np.concatenate([np.arange(h * 256 + 128, (h + 1) * 256) for h in range(32)])
    d["kv_b_n_t"] = _tile_lhst(kv_b_w[:, cols_kn], 4, 32)
    d["kv_b_v_t"] = np.ascontiguousarray(
        kv_b_w[:, cols_v].reshape(4, 128, 4096).transpose(1, 0, 2).reshape(128, 4 * 4096))
    d["o_w_t"] = _tile_rhs(o_w, 32, 8)
    d["idx_wq_t"] = _tile_lhst(idx_wq, 32, 16)
    d["idx_wk_t"] = np.ascontiguousarray(
        idx_wk.reshape(32, 128, 128).transpose(1, 0, 2).reshape(128, 32 * 128))
    d["idx_ww_t"] = np.ascontiguousarray(
        idx_ww.reshape(32, 128, 16).transpose(1, 0, 2).reshape(128, 32 * 16))
    R = np.zeros((64, 64), np.float32)
    for f in range(32):
        R[2 * f + 1, 2 * f] = -1.0
        R[2 * f, 2 * f + 1] = 1.0
    R128 = np.zeros((128, 128), np.float32)
    R128[:64, :64] = R
    R128[64:, 64:] = R
    d["R128"] = R128
    d["zeros_t"] = np.zeros((128, D), np.float32)
    return d


def host_prep_C_core(c):
    p = c % 2
    tiles = [2 * j + p for j in range(NT)]
    pos = np.concatenate([np.arange(g * 128, g * 128 + 128) for g in tiles]).astype(np.float64)
    inv = 1.0 / (10000.0 ** (np.arange(0, ROPE, 2, dtype=np.float64) / ROPE))
    ang = pos[None, :] * inv[:, None]
    cosq = np.cos(ang).astype(np.float32)
    sinq = np.sin(ang).astype(np.float32)
    cqi = np.repeat(cosq, 2, axis=0)
    sqi = np.repeat(sinq, 2, axis=0)
    d = {}
    d["cq_t"] = np.ascontiguousarray(np.concatenate([cqi, cqi], axis=0))
    d["sq_t"] = np.ascontiguousarray(np.concatenate([sqi, sqi], axis=0))
    ctm = np.repeat(cosq.T, 2, axis=1)
    stm = np.repeat(sinq.T, 2, axis=1)
    d["ck_t"] = np.ascontiguousarray(ctm.reshape(NT, 128 * ROPE))
    d["sk_t"] = np.ascontiguousarray(stm.reshape(NT, 128 * ROPE))
    cm = np.zeros((NT, 128, 256), np.float32)
    for j in range(NT):
        gq = 2 * j + p
        qpos = np.arange(gq * 128, gq * 128 + 128)
        kpos = np.arange(2 * j * 128, 2 * j * 128 + 256)
        cm[j] = np.where(qpos[:, None] >= kpos[None, :], 0.0, NEG)
    d["cmask"] = np.ascontiguousarray(cm.reshape(NT, 128 * 256))
    return d


def declare_dram_C(nc, io, debug=False):
    ei = lambda n, shp, dt=F32: io.__setitem__(
        n, nc.dram_tensor(n, shp, dt, kind="ExternalInput").ap())
    internal = lambda n, shp, dt=BF16: io.__setitem__(
        n, nc.dram_tensor(n, shp, dt, kind="Internal").ap())
    ei("q_a_t", [3, 128 * 32 * 512])
    ei("qnw", [128, 12])
    ei("qb_n_t", [32, 128 * 12 * 128])
    ei("qb_r_t", [16, 128 * 12 * 128])
    ei("kv_a_t", [128, 32 * 576])
    ei("kvnw", [128, 4])
    ei("kv_b_n_t", [32, 128 * 4 * 128])
    ei("kv_b_v_t", [128, 4 * 4096])
    ei("o_w_t", [8, 128 * 32 * 512])
    ei("idx_wq_t", [16, 128 * 32 * 128])
    ei("idx_wk_t", [128, 32 * 128])
    ei("idx_ww_t", [128, 32 * 16])
    ei("R128", [128, 128])
    ei("zeros_t", [128, D])
    ei("cq_t", [128, TPC])
    ei("sq_t", [128, TPC])
    ei("ck_t", [NT, 128 * ROPE])
    ei("sk_t", [NT, 128 * ROPE])
    ei("cmask", [NT, 128 * 256])
    internal("qn_dr", [32, 128 * 1024])
    internal("qr_dr", [16, 128 * 1024])
    internal("kn_dr", [32, 128 * 2048])
    internal("v_dr", [16, 128 * 4096])
    internal("cx_src", [NT, 128 * 704])
    internal("cx_dst", [2 * NT, 128 * 704])
    internal("s1c", [NT, 128 * D], F32)
    if debug:
        io["_mkdbg"]("s1c", [NT, 128 * D], F32)
        io["_mkdbg"]("qn_dr", [32, 128 * 1024], BF16)
        io["_mkdbg"]("qr_dr", [16, 128 * 1024], BF16)
        io["_mkdbg"]("kn_dr", [32, 128 * 2048], BF16)
        io["_mkdbg"]("v_dr", [16, 128 * 4096], BF16)
        io["dbg_m01"] = nc.dram_tensor(
            "dbg_m01", [NT, 128 * 2048], BF16, kind="ExternalOutput").ap()
        io["dbg_cidx"] = nc.dram_tensor(
            "dbg_cidx", [NT, 128 * 2048], F32, kind="ExternalOutput").ap()


def phase_C(nc, pl, st, io, sems, glob, x_fm_key="x2_fm", out_key="s1c",
            fm_ready=0):
    ld, ld2, wt = sems["ld"], sems["ld2"], sems["wt"]
    pe, ev, ac, cc, sto = (sems["pe"], sems["ev"], sems["ac"], sems["cc"],
                           sems["sto"])
    lh = [sems["lh0"], sems["lh1"]]
    lv = [sems["lv0"], sems["lv1"]]
    wtP = [sems["wtA"], sems["wtB"]]
    ident = glob["ident"]
    eps = glob["eps"]

    sb = lambda n, shp, dt=BF16: st.enter_context(nc.sbuf_tensor(_uname(n), shp, dt))

    kpe_fm = sb("c_kpe", [64, 2048])

    def vchain(fn, extra=()):
        pl.op("vector", fn, waits=[(ev, ev.n)] + list(extra), inc=(ev, 1))

    with ExitStack() as st1:
        sb1 = lambda n, shp, dt=BF16: st1.enter_context(nc.sbuf_tensor(_uname(n), shp, dt))
        x2fm = sb1("c_x2fm", [128, NC_D * 1024])
        ik_fm = sb1("c_ikfm", [128, 2048])
        pl.op("sync", lambda s: s.dma_start(
            out=x2fm[:, :].rearrange("p (c f) -> p c f", c=NC_D),
            in_=io[x_fm_key].rearrange("c (p f) -> p c f", p=128)),
            waits=[(sems["st"], fm_ready)], inc=(ld, 16))
        x2_tgt = ld.n

        pl.fence(sems)
        st12 = st1.enter_context(ExitStack())
        qlat_fm = st12.enter_context(
            nc.sbuf_tensor("c_qlatfm", [128, 12 * 1024], BF16))
        with ExitStack() as st2:
            sb2 = lambda n, shp, dt=BF16: st2.enter_context(nc.sbuf_tensor(_uname(n), shp, dt))
            ps2 = lambda n, shp: st2.enter_context(nc.psum_tensor(_uname(n), shp, F32))
            qa0 = sb2("c_qa0", [128, 32 * 512])
            qa = [qa0, qa0]
            qlat_tm = sb2("c_qlat_tm", [128, NT * 1536], F32)
            qnw = sb2("c_qnw", [128, 12], F32)
            qlat_bf = sb2("c_qlatbf", [128, 1536])
            c1dmp = sb2("c_dmp", [128, 1536], F32)
            c1ss = sb2("c_ss", [128, 1], F32)
            c1rstd = sb2("c_rstd", [128, 1], F32)
            ps_q = [ps2(f"c_psq{i}", [128, 512]) for i in range(2)]
            ps_t = ps2("c_pst", [128, 512])

            pl.op("sync", lambda s: s.dma_start(out=qnw[:, :], in_=io["qnw"]),
                  inc=(ld, 16))
            qnw_tgt = ld.n
            for ob3 in range(3):
                buf = 0
                lw = [(pe, pe.n)]
                pl.op("gpsimd", lambda g, ob3=ob3, buf=buf: g.dma_start(
                    out=qa[buf][:, :], in_=row128(io["q_a_t"], ob3)),
                    waits=lw, inc=(wtP[buf], 16))
                w_tgt = wtP[buf].n
                for jt in range(NT):
                    pp = ps_q[jt % 2]
                    mw = [(wtP[0], w_tgt), (ld, ld.n), (ev, ev.n)]
                    for kc in range(NC_D):
                        pl.op("tensor", lambda t, kc=kc, jt=jt, buf=buf, pp=pp: t.matmul(
                            pp[:, :],
                            x2fm[:, kc * 1024 + jt * 128: kc * 1024 + (jt + 1) * 128],
                            qa[buf][:, kc * 512:(kc + 1) * 512],
                            start=(kc == 0), stop=(kc == 31)),
                            waits=mw if kc == 0 else (),
                            inc=(pe, 1) if kc == 31 else None)
                    mm_tgt = pe.n
                    pl.op("vector", lambda v, jt=jt, ob3=ob3, pp=pp: v.tensor_copy(
                        qlat_tm[:, jt * 1536 + ob3 * 512: jt * 1536 + (ob3 + 1) * 512],
                        pp[:, :]), waits=[(pe, mm_tgt), (ev, ev.n)], inc=(ev, 1))
            for jt in range(NT):
                qsl = qlat_tm[:, jt * 1536:(jt + 1) * 1536]
                vchain(lambda v, qsl=qsl: v.scalar_tensor_tensor(
                    c1dmp[:, :], qsl, 1.0, qsl, ALU.mult, ALU.mult,
                    accum_out=c1ss[:, :]))
                vchain(lambda v: v.tensor_scalar(
                    c1ss[:, :], c1ss[:, :], 1.0 / Q_RANK, None, ALU.mult, ALU.bypass))
                ss_tgt = ev.n
                pl.op("scalar", lambda s: s.activation(
                    c1rstd[:, :], c1ss[:, :], AF.Ln, bias=eps[:, :]),
                    waits=[(ev, ss_tgt), (ac, ac.n)], inc=(ac, 1))
                pl.op("scalar", lambda s: s.activation(
                    c1rstd[:, :], c1rstd[:, :], AF.Exp, scale=-0.5),
                    waits=[(ac, ac.n)], inc=(ac, 1))
                r_tgt = ac.n
                vchain(lambda v, qsl=qsl: v.tensor_scalar(
                    qlat_bf[:, :], qsl, c1rstd[:, :], None, ALU.mult, ALU.bypass),
                    extra=[(ac, r_tgt), (pe, pe.n)])
                cast_tgt = ev.n
                for c2 in range(12):
                    half = c2 % 2
                    pl.op("tensor", lambda t, c2=c2, half=half: t.matmul(
                        ps_t[:, (c2 % 4) * 128:(c2 % 4) * 128 + 128],
                        qlat_bf[:, c2 * 128:(c2 + 1) * 128], ident[:, :],
                        start=True, stop=True),
                        waits=[(ev, cast_tgt), (ev, ev.n)] if c2 % 4 == 0 else (),
                        inc=(pe, 1) if c2 % 4 == 3 else None)
                    if c2 % 4 == 3:
                        tr_tgt = pe.n
                        for q4 in range(4):
                            cc2 = c2 - 3 + q4
                            pl.op("vector", lambda v, cc2=cc2, q4=q4, jt=jt: v.tensor_scalar(
                                qlat_fm[:, cc2 * 1024 + jt * 128: cc2 * 1024 + (jt + 1) * 128],
                                ps_t[:, q4 * 128:(q4 + 1) * 128],
                                qnw[:, cc2:cc2 + 1], None, ALU.mult, ALU.bypass),
                                waits=[(pe, tr_tgt), (ld, qnw_tgt)] if q4 == 0 else (),
                                inc=(ev, 1) if q4 == 3 else None)
            qlat_done = ev.n

        pl.fence(sems)
        with ExitStack() as st2:
            sb2 = lambda n, shp, dt=BF16: st2.enter_context(nc.sbuf_tensor(_uname(n), shp, dt))
            ps2 = lambda n, shp: st2.enter_context(nc.psum_tensor(_uname(n), shp, F32))
            qb = [sb2(f"c_qb{i}", [128, 12 * 128]) for i in range(2)]
            qstg = [sb2(f"c_qstg{i}", [128, 1024]) for i in range(2)]
            r128 = sb2("c_r128", [128, 128])
            cqt = sb2("c_cqt", [128, 1024], F32)
            sqt = sb2("c_sqt", [128, 1024], F32)
            qr_f = sb2("c_qrf", [128, 1024], F32)
            qr_b = sb2("c_qrb", [128, 1024])
            t1b = sb2("c_t1", [128, 1024], F32)
            t2b = sb2("c_t2", [128, 1024], F32)
            ps_m = [ps2(f"c_psm{i}", [128, 1024]) for i in range(2)]
            ps_rot = ps2("c_psrot", [128, 1024])

            pl.op("gpsimd", lambda g: g.dma_start(out=r128[:, :], in_=io["R128"]),
                  inc=(wt, 16))
            pl.op("sync", lambda s: s.dma_start(out=cqt[:, :], in_=io["cq_t"]),
                  inc=(ld, 16))
            pl.op("sync", lambda s: s.dma_start(out=sqt[:, :], in_=io["sq_t"]),
                  inc=(ld, 16))
            tab_tgt = ld.n

            for kind in range(2):
                nm = 32 if kind == 0 else 16
                wsrc = "qb_n_t" if kind == 0 else "qb_r_t"
                for m in range(nm):
                    buf = m % 2
                    pl.op("gpsimd", lambda g, m=m, buf=buf, wsrc=wsrc: g.dma_start(
                        out=qb[buf][:, :], in_=row128(io[wsrc], m)),
                        waits=[(pe, pe.n)], inc=(wtP[buf], 16))
                    w_tgt = wtP[buf].n
                    pp = ps_m[buf]
                    mw = [(wtP[buf], w_tgt), (ev, qlat_done), (ac, ac.n), (ev, ev.n)]
                    for kc in range(12):
                        for h2 in range(2):
                            pl.op("tensor", lambda t, kc=kc, h2=h2, buf=buf, pp=pp: t.matmul(
                                pp[:, h2 * 512:(h2 + 1) * 512],
                                qb[buf][:, kc * 128:(kc + 1) * 128],
                                qlat_fm[:, kc * 1024 + h2 * 512: kc * 1024 + (h2 + 1) * 512],
                                start=(kc == 0), stop=(kc == 11)),
                                waits=mw if (kc == 0 and h2 == 0) else (),
                                inc=(pe, 1) if (kc == 11 and h2 == 1) else None)
                    mm_tgt = pe.n
                    if kind == 0:
                        dst = qstg[buf]
                        pl.op("vector", lambda v, pp=pp, dst=dst: v.tensor_scalar(
                            dst[:, :], pp[:, :], SC_QK, None, ALU.mult, ALU.bypass),
                            waits=[(pe, mm_tgt), (sto, sto.n), (ev, ev.n)], inc=(ev, 1))
                        e_tgt = ev.n
                        pl.op("sync", lambda s, m=m, dst=dst: s.dma_start(
                            out=row128(io["qn_dr"], m), in_=dst[:, :]),
                            waits=[(ev, e_tgt)], inc=(sto, 16))
                    else:
                        vchain(lambda v, pp=pp: v.tensor_scalar(
                            qr_f[:, :], pp[:, :], SC_QK, None, ALU.mult, ALU.bypass),
                            extra=[(pe, mm_tgt)])
                        vchain(lambda v: v.tensor_copy(qr_b[:, :], qr_f[:, :]))
                        qrb_tgt = ev.n
                        pl.op("tensor", lambda t: t.matmul(
                            ps_rot[:, 0:512], r128[:, :], qr_b[:, 0:512],
                            start=True, stop=True),
                            waits=[(ev, qrb_tgt), (wt, wt.n)])
                        pl.op("tensor", lambda t: t.matmul(
                            ps_rot[:, 512:1024], r128[:, :], qr_b[:, 512:1024],
                            start=True, stop=True), inc=(pe, 1))
                        rot_tgt = pe.n
                        vchain(lambda v: v.tensor_mul(
                            t2b[:, :], ps_rot[:, :], sqt[:, :]),
                            extra=[(pe, rot_tgt), (ld, tab_tgt)])
                        vchain(lambda v: v.tensor_mul(t1b[:, :], qr_f[:, :], cqt[:, :]))
                        dst = qstg[buf]
                        vchain(lambda v, dst=dst: v.tensor_add(
                            dst[:, :], t1b[:, :], t2b[:, :]),
                            extra=[(sto, sto.n)])
                        e_tgt = ev.n
                        pl.op("sync", lambda s, m=m, dst=dst: s.dma_start(
                            out=row128(io["qr_dr"], m), in_=dst[:, :]),
                            waits=[(ev, e_tgt)], inc=(sto, 16))

        st12.close()

        pl.fence(sems)
        with ExitStack() as st2:
            sb2 = lambda n, shp, dt=BF16: st2.enter_context(nc.sbuf_tensor(_uname(n), shp, dt))
            ps2 = lambda n, shp: st2.enter_context(nc.psum_tensor(_uname(n), shp, F32))
            kva = sb2("c_kva", [128, 32 * 576])
            wk = sb2("c_wk", [128, 32 * 128])
            ckv_f = sb2("c_ckvf", [128, 512], F32)
            kpe_f = sb2("c_kpef", [128, 64], F32)
            rot_f = sb2("c_rotf", [128, 64], F32)
            t1k = sb2("c_t1k", [128, 64], F32)
            ckt = sb2("c_ckt", [128, 64], F32)
            skt = sb2("c_skt", [128, 64], F32)
            cx_stg = [sb2(f"c_cxs{i}", [128, 704]) for i in range(2)]
            c3dmp = sb2("c_dmp2", [128, 512], F32)
            c3ss = sb2("c_ss2", [128, 1], F32)
            c3rstd = sb2("c_rstd2", [128, 1], F32)
            ps_c3 = [ps2(f"c_psc{i}", [128, 576]) for i in range(2)]
            ps_ik = ps2("c_psik", [128, 128])

            pl.op("gpsimd", lambda g: g.dma_start(out=kva[:, :], in_=io["kv_a_t"]),
                  inc=(wt, 16))
            pl.op("gpsimd", lambda g: g.dma_start(out=wk[:, :], in_=io["idx_wk_t"]),
                  inc=(wt, 16))
            kva_tgt = wt.n
            for jt in range(NT):
                buf = jt % 2
                pp = ps_c3[buf]
                pl.op("sync", lambda s, jt=jt: s.dma_start(
                    out=ckt[:, :], in_=row128(io["ck_t"], jt)),
                    waits=[(ev, ev.n)], inc=(ld, 16))
                pl.op("sync", lambda s, jt=jt: s.dma_start(
                    out=skt[:, :], in_=row128(io["sk_t"], jt)), inc=(ld, 16))
                tab_tgt = ld.n
                mw = [(wt, kva_tgt), (ld, ld.n), (ev, ev.n), (ac, ac.n)]
                for kc in range(NC_D):
                    for sl, off, w in ((0, 0, 512), (1, 512, 64)):
                        pl.op("tensor", lambda t, kc=kc, jt=jt, pp=pp, off=off, w=w: t.matmul(
                            pp[:, off:off + w],
                            x2fm[:, kc * 1024 + jt * 128: kc * 1024 + (jt + 1) * 128],
                            kva[:, kc * 576 + off: kc * 576 + off + w],
                            start=(kc == 0), stop=(kc == 31)),
                            waits=mw if (kc == 0 and sl == 0) else (),
                            inc=(pe, 1) if (kc == 31 and sl == 1) else None)
                for kc in range(NC_D):
                    pl.op("tensor", lambda t, kc=kc, jt=jt: t.matmul(
                        ps_ik[:, :],
                        x2fm[:, kc * 1024 + jt * 128: kc * 1024 + (jt + 1) * 128],
                        wk[:, kc * 128:(kc + 1) * 128],
                        start=(kc == 0), stop=(kc == 31)),
                        inc=(pe, 1) if kc == 31 else None)
                mm_tgt = pe.n
                vchain(lambda v, pp=pp: v.tensor_copy(ckv_f[:, :], pp[:, 0:512]),
                       extra=[(pe, mm_tgt), (sto, sto.n)])
                vchain(lambda v: v.scalar_tensor_tensor(
                    c3dmp[:, :], ckv_f[:, :], 1.0, ckv_f[:, :], ALU.mult, ALU.mult,
                    accum_out=c3ss[:, :]))
                vchain(lambda v: v.tensor_scalar(
                    c3ss[:, :], c3ss[:, :], 1.0 / KV_RANK, None, ALU.mult, ALU.bypass))
                ss_tgt = ev.n
                pl.op("scalar", lambda s: s.activation(
                    c3rstd[:, :], c3ss[:, :], AF.Ln, bias=eps[:, :]),
                    waits=[(ev, ss_tgt), (ac, ac.n)], inc=(ac, 1))
                pl.op("scalar", lambda s: s.activation(
                    c3rstd[:, :], c3rstd[:, :], AF.Exp, scale=-0.5),
                    waits=[(ac, ac.n)], inc=(ac, 1))
                r_tgt = ac.n
                cxs = cx_stg[buf]
                vchain(lambda v, cxs=cxs: v.tensor_scalar(
                    cxs[:, 0:512], ckv_f[:, :], c3rstd[:, :], None, ALU.mult, ALU.bypass),
                    extra=[(ac, r_tgt), (sto, sto.n)])
                vchain(lambda v, pp=pp: v.tensor_copy(kpe_f[:, :], pp[:, 512:576]))
                vchain(lambda v: v.tensor_scalar(
                    rot_f[:, :].rearrange("p (a b) -> p a b", b=2)[:, :, 0],
                    kpe_f[:, :].rearrange("p (a b) -> p a b", b=2)[:, :, 1],
                    -1.0, None, ALU.mult, ALU.bypass))
                vchain(lambda v: v.tensor_copy(
                    rot_f[:, :].rearrange("p (a b) -> p a b", b=2)[:, :, 1],
                    kpe_f[:, :].rearrange("p (a b) -> p a b", b=2)[:, :, 0]))
                vchain(lambda v: v.tensor_mul(t1k[:, :], kpe_f[:, :], ckt[:, :]),
                       extra=[(ld, tab_tgt)])
                vchain(lambda v: v.tensor_mul(rot_f[:, :], rot_f[:, :], skt[:, :]))
                vchain(lambda v, cxs=cxs: v.tensor_add(
                    cxs[:, 512:576], t1k[:, :], rot_f[:, :]))
                vchain(lambda v, cxs=cxs, pp=pp: v.tensor_copy(
                    cxs[:, 576:704], ps_ik[:, :]), extra=[(pe, mm_tgt)])
                e_tgt = ev.n
                pl.op("sync", lambda s, jt=jt, cxs=cxs: s.dma_start(
                    out=row128(io["cx_src"], jt), in_=cxs[:, :]),
                    waits=[(ev, e_tgt)], inc=(sto, 16))
            pl.op("gpsimd", lambda g: g.collective_compute(
                "AllGather", ALU.bypass, replica_groups=REPLICA_PAIRS,
                ins=[io["cx_src"].opt()], outs=[io["cx_dst"].opt()]),
                waits=[(sto, sto.n), (ld, ld.n)], inc=(cc, 1))
            cx_done = cc.n
            pl.wait("sync", cc, cx_done)
            pl.wait("scalar", cc, cx_done)

        pl.fence(sems)
        with ExitStack() as st2:
            sb2 = lambda n, shp, dt=BF16: st2.enter_context(nc.sbuf_tensor(_uname(n), shp, dt))
            ps2 = lambda n, shp: st2.enter_context(nc.psum_tensor(_uname(n), shp, F32))
            cx_t = [sb2(f"c_cxt{i}", [128, 704]) for i in range(2)]
            kvlat_fm = sb2("c_kvlat", [128, 4 * 2048])
            kvnw = sb2("c_kvnw", [128, 4], F32)
            kbn = [sb2(f"c_kbn{i}", [128, 4 * 128]) for i in range(2)]
            kbv = sb2("c_kbv", [128, 4 * 4096])
            knstg = [sb2(f"c_knst{i}", [128, 2048]) for i in range(2)]
            vstg = [sb2(f"c_vst{i}", [128, 512]) for i in range(2)]
            ps_t4 = ps2("c_pst4", [128, 512])
            ps_kn = ps2("c_pskn", [128, 2048])
            ps_v = [ps2(f"c_psv{i}", [128, 512]) for i in range(2)]

            pl.op("sync", lambda s: s.dma_start(out=kvnw[:, :], in_=io["kvnw"]),
                  waits=[(cc, cx_done)], inc=(ld, 16))
            kvnw_tgt = ld.n
            for g16 in range(16):
                r, j = g16 % 2, g16 // 2
                buf = g16 % 2
                pl.op("sync", lambda s, r=r, j=j, buf=buf: s.dma_start(
                    out=cx_t[buf][:, :], in_=row128(io["cx_dst"], r * 8 + j)),
                    waits=[(cc, cx_done), (pe, pe.n)], inc=(lh[buf], 16))
                c_tgt = lh[buf].n
                for c4 in range(4):
                    pl.op("tensor", lambda t, c4=c4, buf=buf: t.matmul(
                        ps_t4[:, (c4 % 4) * 128:(c4 % 4) * 128 + 128],
                        cx_t[buf][:, c4 * 128:(c4 + 1) * 128], ident[:, :],
                        start=True, stop=True),
                        waits=[(lh[buf], c_tgt), (ev, ev.n), (ac, ac.n)] if c4 == 0 else (),
                        inc=(pe, 1) if c4 == 3 else None)
                tr_tgt = pe.n
                for c4 in range(4):
                    pl.op("vector", lambda v, c4=c4, g16=g16: v.tensor_scalar(
                        kvlat_fm[:, c4 * 2048 + g16 * 128: c4 * 2048 + (g16 + 1) * 128],
                        ps_t4[:, c4 * 128:(c4 + 1) * 128],
                        kvnw[:, c4:c4 + 1], None, ALU.mult, ALU.bypass),
                        waits=[(pe, tr_tgt), (ld, kvnw_tgt), (ev, ev.n)] if c4 == 0 else (),
                        inc=(ev, 1) if c4 == 3 else None)
                kv_ev = ev.n
                pl.op("tensor", lambda t, buf=buf: t.matmul(
                    ps_t4[0:64, 0:128], cx_t[buf][:, 512:576], ident[:, :],
                    start=True, stop=True), waits=[(ev, kv_ev)])
                pl.op("tensor", lambda t, buf=buf: t.matmul(
                    ps_t4[:, 128:256], cx_t[buf][:, 576:704], ident[:, :],
                    start=True, stop=True), inc=(pe, 1))
                tr2_tgt = pe.n
                pl.op("scalar", lambda s, g16=g16: s.activation(
                    kpe_fm[:, g16 * 128:(g16 + 1) * 128], ps_t4[0:64, 0:128],
                    AF.Copy), waits=[(pe, tr2_tgt), (ac, ac.n)], inc=(ac, 1))
                pl.op("scalar", lambda s, g16=g16: s.activation(
                    ik_fm[:, g16 * 128:(g16 + 1) * 128], ps_t4[:, 128:256],
                    AF.Copy), waits=[(ac, ac.n)], inc=(ac, 1))
            fm_done = (ev.n, ac.n)
            for m in range(H_L1):
                buf = m % 2
                pl.op("gpsimd", lambda g, m=m, buf=buf: g.dma_start(
                    out=kbn[buf][:, :], in_=row128(io["kv_b_n_t"], m)),
                    waits=[(pe, pe.n)], inc=(wtP[buf], 16))
                w_tgt = wtP[buf].n
                mw = [(wtP[buf], w_tgt), (ev, fm_done[0]), (ac, ac.n), (ev, ev.n)]
                for kc in range(4):
                    for n5 in range(4):
                        pl.op("tensor", lambda t, kc=kc, n5=n5, buf=buf: t.matmul(
                            ps_kn[:, n5 * 512:(n5 + 1) * 512],
                            kbn[buf][:, kc * 128:(kc + 1) * 128],
                            kvlat_fm[:, kc * 2048 + n5 * 512: kc * 2048 + (n5 + 1) * 512],
                            start=(kc == 0), stop=(kc == 3)),
                            waits=mw if (kc == 0 and n5 == 0) else (),
                            inc=(pe, 1) if (kc == 3 and n5 == 3) else None)
                mm_tgt = pe.n
                dst = knstg[buf]
                pl.op("vector", lambda v, dst=dst: v.tensor_copy(
                    dst[:, :], ps_kn[:, :]),
                    waits=[(pe, mm_tgt), (sto, sto.n), (ev, ev.n)], inc=(ev, 1))
                e_tgt = ev.n
                pl.op("sync", lambda s, m=m, dst=dst: s.dma_start(
                    out=row128(io["kn_dr"], m), in_=dst[:, :]),
                    waits=[(ev, e_tgt)], inc=(sto, 16))
            pl.op("gpsimd", lambda g: g.dma_start(out=kbv[:, :], in_=io["kv_b_v_t"]),
                  inc=(wt, 16))
            kbv_tgt = wt.n
            for kt in range(16):
                for obv in range(8):
                    buf = (kt * 8 + obv) % 2
                    pp = ps_v[buf]
                    mw = [(wt, kbv_tgt), (ev, fm_done[0]), (ev, ev.n)]
                    for kc in range(4):
                        pl.op("tensor", lambda t, kc=kc, kt=kt, obv=obv, pp=pp: t.matmul(
                            pp[:, :],
                            kvlat_fm[:, kc * 2048 + kt * 128: kc * 2048 + (kt + 1) * 128],
                            kbv[:, kc * 4096 + obv * 512: kc * 4096 + (obv + 1) * 512],
                            start=(kc == 0), stop=(kc == 3)),
                            waits=mw if kc == 0 else (),
                            inc=(pe, 1) if kc == 3 else None)
                    mm_tgt = pe.n
                    dst = vstg[buf]
                    pl.op("vector", lambda v, dst=dst, pp=pp: v.tensor_copy(
                        dst[:, :], pp[:, :]),
                        waits=[(pe, mm_tgt), (sto, sto.n), (ev, ev.n)], inc=(ev, 1))
                    e_tgt = ev.n
                    pl.op("sync", lambda s, kt=kt, obv=obv, dst=dst: s.dma_start(
                        out=row128(io["v_dr"], kt)[:, obv * 512:(obv + 1) * 512],
                        in_=dst[:, :]), waits=[(ev, e_tgt)], inc=(sto, 16))

        pl.fence(sems)
        m01 = st.enter_context(
            nc.sbuf_tensor("c_m01", [128, NT * 2048], BF16, side="right"))
        with ExitStack() as st2:
            sb2 = lambda n, shp, dt=BF16: st2.enter_context(nc.sbuf_tensor(_uname(n), shp, dt))
            ps2 = lambda n, shp: st2.enter_context(nc.psum_tensor(_uname(n), shp, F32))
            iq_fm = sb2("c_iqfm", [128, 16 * 1024])
            iw_sb = sb2("c_iwsb", [128, NT * 16], F32)
            wwt = sb2("c_wwt", [128, 32 * 16])
            wq = [sb2(f"c_wq{i}", [128, 32 * 128]) for i in range(2)]
            cidx = sb2("c_cidx", [128, 2048], F32)
            rn_sb = [sb2("c_rn", [128, 2048]) for _ in range(2)]
            dmp5 = sb2("c_dmp5", [128, 2048], F32)
            cmask_sb = sb2("c_cm", [128, 256], F32)
            lo = sb2("c_lo", [128, 1], F32)
            hi = sb2("c_hi", [128, 1], F32)
            mid = sb2("c_mid", [128, 1], F32)
            cnt = sb2("c_cnt", [128, 1], F32)
            cond = sb2("c_cond", [128, 1], F32)
            tda = sb2("c_tda", [128, 1], F32)
            ps_i = [ps2(f"c_psi{i}", [128, 2048]) for i in range(2)]

            pl.op("gpsimd", lambda g: g.dma_start(out=wwt[:, :], in_=io["idx_ww_t"]),
                  inc=(wt, 16))
            ww_tgt = wt.n
            for m in range(16):
                buf = m % 2
                pl.op("gpsimd", lambda g, m=m, buf=buf: g.dma_start(
                    out=wq[buf][:, :], in_=row128(io["idx_wq_t"], m)),
                    waits=[(pe, pe.n)], inc=(wtP[buf], 16))
                w_tgt = wtP[buf].n
                pp = ps_i[buf]
                mw = [(wtP[buf], w_tgt), (ld, ld.n), (ac, ac.n), (ev, ev.n)]
                for kc in range(NC_D):
                    for h2 in range(2):
                        pl.op("tensor", lambda t, kc=kc, h2=h2, buf=buf, pp=pp: t.matmul(
                            pp[:, h2 * 512:(h2 + 1) * 512],
                            wq[buf][:, kc * 128:(kc + 1) * 128],
                            x2fm[:, kc * 1024 + h2 * 512: kc * 1024 + (h2 + 1) * 512],
                            start=(kc == 0), stop=(kc == 31)),
                            waits=mw if (kc == 0 and h2 == 0) else (),
                            inc=(pe, 1) if (kc == 31 and h2 == 1) else None)
                mm_tgt = pe.n
                pl.op("scalar", lambda s, m=m, pp=pp: s.activation(
                    iq_fm[:, m * 1024:(m + 1) * 1024], pp[:, 0:1024], AF.Copy),
                    waits=[(pe, mm_tgt), (ac, ac.n)], inc=(ac, 1))
            for jt in range(NT):
                pp = ps_i[jt % 2]
                mw = [(wt, ww_tgt), (ac, ac.n), (ev, ev.n)]
                for kc in range(NC_D):
                    pl.op("tensor", lambda t, kc=kc, jt=jt, pp=pp: t.matmul(
                        pp[:, 0:16],
                        x2fm[:, kc * 1024 + jt * 128: kc * 1024 + (jt + 1) * 128],
                        wwt[:, kc * 16:(kc + 1) * 16],
                        start=(kc == 0), stop=(kc == 31)),
                        waits=mw if kc == 0 else (),
                        inc=(pe, 1) if kc == 31 else None)
                mm_tgt = pe.n
                pl.op("vector", lambda v, jt=jt, pp=pp: v.tensor_scalar(
                    iw_sb[:, jt * 16:(jt + 1) * 16], pp[:, 0:16], SC_IDX, None,
                    ALU.mult, ALU.bypass),
                    waits=[(pe, mm_tgt), (ev, ev.n)], inc=(ev, 1))
            iw_ev = ev.n
            relu_prev = {0: ac.n, 1: ac.n}
            comb_prev = {0: ev.n, 1: ev.n}
            iq_done = (ac.n, ev.n)
            for jq in range(NT):
                ncau = 256 * (jq + 1)
                nch = (ncau + 511) // 512
                pl.op("sync", lambda s, jq=jq: s.dma_start(
                    out=cmask_sb[:, :], in_=row128(io["cmask"], jq)),
                    waits=[(ev, ev.n)], inc=(ld, 16))
                cm_tgt = ld.n
                for n in range(16):
                    buf = n % 2
                    pp = ps_i[buf]
                    mw = [(ac, relu_prev[buf]), (ev, iw_ev)] if (
                        jq == 0 and n < 2) else [(ac, relu_prev[buf])]
                    for n5 in range(nch):
                        w5 = min(512, ncau - n5 * 512)
                        pl.op("tensor", lambda t, n=n, jq=jq, n5=n5, w5=w5, pp=pp: t.matmul(
                            pp[:, n5 * 512:n5 * 512 + w5],
                            iq_fm[:, n * 1024 + jq * 128: n * 1024 + (jq + 1) * 128],
                            ik_fm[:, n5 * 512:n5 * 512 + w5],
                            start=True, stop=True),
                            waits=mw if n5 == 0 else (),
                            inc=(pe, 1) if n5 == nch - 1 else None)
                    mm_tgt = pe.n
                    pl.op("scalar", lambda s, ncau=ncau, pp=pp, buf=buf: s.activation(
                        rn_sb[buf][:, 0:ncau], pp[:, 0:ncau], AF.Relu),
                        waits=[(pe, mm_tgt), (ev, comb_prev[buf])], inc=(ac, 1))
                    relu_prev[buf] = ac.n
                    r_tgt = ac.n
                    if n == 0:
                        vchain(lambda v, jq=jq, ncau=ncau, buf=buf: v.tensor_scalar(
                            cidx[:, 0:ncau], rn_sb[buf][:, 0:ncau],
                            iw_sb[:, jq * 16:jq * 16 + 1], None, ALU.mult, ALU.bypass),
                            extra=[(ac, r_tgt)])
                    else:
                        vchain(lambda v, jq=jq, n=n, ncau=ncau, buf=buf: v.scalar_tensor_tensor(
                            cidx[:, 0:ncau], rn_sb[buf][:, 0:ncau],
                            iw_sb[:, jq * 16 + n:jq * 16 + n + 1],
                            cidx[:, 0:ncau], ALU.mult, ALU.add),
                            extra=[(ac, r_tgt)])
                    comb_prev[buf] = ev.n
                vchain(lambda v, ncau=ncau: v.tensor_reduce(
                    hi[:, :], cidx[:, 0:ncau], AX.X, ALU.max))
                vchain(lambda v, ncau=ncau: v.tensor_reduce(
                    lo[:, :], cidx[:, 0:ncau], AX.X, ALU.min))
                vchain(lambda v: v.tensor_scalar(
                    hi[:, :], hi[:, :], 1.0, None, ALU.add, ALU.bypass))
                vchain(lambda v: v.tensor_scalar(
                    lo[:, :], lo[:, :], 1.0, None, ALU.subtract, ALU.bypass))
                if ncau < 2048:
                    vchain(lambda v, ncau=ncau: v.memset(cidx[:, ncau:2048], NEG))
                vchain(lambda v, ncau=ncau: v.tensor_add(
                    cidx[:, ncau - 256:ncau], cidx[:, ncau - 256:ncau],
                    cmask_sb[:, :]), extra=[(ld, cm_tgt)])
                for itb in range(26):
                    vchain(lambda v: v.tensor_add(mid[:, :], lo[:, :], hi[:, :]))
                    vchain(lambda v: v.tensor_scalar(
                        mid[:, :], mid[:, :], 0.5, None, ALU.mult, ALU.bypass))
                    vchain(lambda v: v.tensor_scalar(
                        dmp5[:, :], cidx[:, :], mid[:, :], None, ALU.is_ge,
                        ALU.add, accum_out=cnt[:, :]))
                    vchain(lambda v: v.tensor_scalar(
                        cond[:, :], cnt[:, :], float(TOPK), None, ALU.is_ge,
                        ALU.bypass))
                    vchain(lambda v: v.tensor_sub(tda[:, :], mid[:, :], lo[:, :]))
                    vchain(lambda v: v.scalar_tensor_tensor(
                        lo[:, :], tda[:, :], cond[:, :], lo[:, :], ALU.mult, ALU.add))
                    vchain(lambda v: v.tensor_sub(tda[:, :], hi[:, :], mid[:, :]))
                    vchain(lambda v: v.scalar_tensor_tensor(
                        hi[:, :], tda[:, :], cond[:, :], mid[:, :], ALU.mult, ALU.add))
                vchain(lambda v, jq=jq: v.tensor_scalar(
                    m01[:, jq * 2048:(jq + 1) * 2048], cidx[:, :], lo[:, :], None,
                    ALU.is_ge, ALU.bypass))
                if "dbg_m01" in io:
                    e_tgt = ev.n
                    pl.op("sync", lambda s, jq=jq: s.dma_start(
                        out=row128(io["dbg_m01"], jq),
                        in_=m01[:, jq * 2048:(jq + 1) * 2048]),
                        waits=[(ev, e_tgt)], inc=(sto, 16))
                    pl.op("sync", lambda s, jq=jq: s.dma_start(
                        out=row128(io["dbg_cidx"], jq), in_=cidx[:, :]),
                        waits=[(ev, e_tgt)], inc=(sto, 16))
                    vchain(lambda v: v.engine_nop(), extra=[(sto, sto.n)])
            m01_done = ev.n

    pl.fence(sems)
    ctx = st.enter_context(
        nc.sbuf_tensor("c_ctx", [128, 32 * 1024], BF16, side="right"))
    with ExitStack() as st2:
        sb2 = lambda n, shp, dt=BF16: st2.enter_context(nc.sbuf_tensor(_uname(n), shp, dt))
        ps2 = lambda n, shp: st2.enter_context(nc.psum_tensor(_uname(n), shp, F32))
        knh = [sb2(f"c_knh{i}", [128, 2048]) for i in range(2)]
        vhb = [sb2(f"c_vh{i}", [128, 2048]) for i in range(2)]
        qnh = [sb2(f"c_qnh{i}", [128, 1024]) for i in range(2)]
        qrh = [sb2(f"c_qrh{i}", [64, 1024]) for i in range(2)]
        p_raw = [sb2("c_praw", [128, 2048]) for _ in range(2)]
        p_m = [sb2("c_pm", [128, 2048]) for _ in range(2)]
        p_sb = [sb2("c_psb", [128, 2048]) for _ in range(2)]
        s_sb6 = [sb2("c_ssb6", [128, 2048], F32) for _ in range(2)]
        pt_h = sb2("c_pt", [128, 16 * 1024])
        cm6 = sb2("c_cm6", [128, NT * 256], F32)
        nmax = [sb2("c_nmax", [128, 1], F32) for _ in range(2)]
        pmax6 = [sb2("c_pmax6", [128, 1], F32) for _ in range(2)]
        sums = [sb2("c_sums", [128, 1], F32) for _ in range(2)]
        recip = [sb2("c_recip", [128, 1], F32) for _ in range(2)]
        ps_cx = ps2("c_pscx", [128, 512])
        ps_s = ps2("c_pss", [128, 2048])
        ps_pt = ps2("c_pspt", [128, 1024])

        pl.op("sync", lambda s: s.dma_start(
            out=cm6[:, :].rearrange("p (j f) -> p j f", j=NT),
            in_=io["cmask"].rearrange("j (p f) -> p j f", p=128)),
            waits=[(ev, m01_done)], inc=(ld, 16))
        cm6_tgt = ld.n

        def load_head6(h, slot):
            g = [(pe, pe.n)]
            pl.op("sync", lambda s, h=h, slot=slot: s.dma_start(
                out=knh[slot][:, :], in_=row128(io["kn_dr"], h)),
                waits=g, inc=(lh[slot], 16))
            pl.op("sync", lambda s, h=h, slot=slot: s.dma_start(
                out=qnh[slot][:, :], in_=row128(io["qn_dr"], h)),
                inc=(lh[slot], 16))
            pl.op("sync", lambda s, h=h, slot=slot: s.dma_start(
                out=qrh[slot][:, :],
                in_=row128(io["qr_dr"], h // 2)[(h % 2) * 64:(h % 2) * 64 + 64, :]),
                inc=(lh[slot], 16))
            src = io["v_dr"].rearrange("k (t f) -> t k f", t=128)[
                :, :, h * 128:(h + 1) * 128]
            pl.op("scalar", lambda s, src=src, slot=slot: s.dma_start(
                out=vhb[slot][:, :].rearrange("t (k d) -> t k d", k=16), in_=src),
                waits=g, inc=(lv[slot], 16))
            return lh[slot].n, lv[slot].n

        mxev6 = [ev.n]
        ptv6_prev = [ac.n]
        exp6 = {0: ac.n, 1: ac.n}
        tgts = [None, None]
        tgts[0] = load_head6(0, 0)

        def emit_pt6(pjb, pnrm, pjq, pslot):
            nkt = 2 * (pjq + 1)
            for rnd in range((nkt + 7) // 8):
                k0 = rnd * 8
                kn8 = min(8, nkt - k0)
                tw = [(ev, pnrm), (ac, ptv6_prev[0])]
                for q4 in range(kn8):
                    kt = k0 + q4
                    pl.op("tensor", lambda t, kt=kt, q4=q4, pjb=pjb: t.matmul(
                        ps_pt[:, q4 * 128:(q4 + 1) * 128],
                        p_sb[pjb][:, kt * 128:(kt + 1) * 128], ident[:, :],
                        start=True, stop=True),
                        waits=tw if q4 == 0 else (),
                        inc=(pe, 1) if q4 == kn8 - 1 else None)
                tr_tgt = pe.n
                dst = pt_h[:, :].rearrange("p (kc q) -> p kc q", kc=16)[
                    :, k0:k0 + kn8, pjq * 128:(pjq + 1) * 128]
                pl.op("scalar", lambda s, dst=dst, kn8=kn8: s.activation(
                    dst, ps_pt[:, 0:kn8 * 128].rearrange(
                        "p (a b) -> p a b", a=kn8), AF.Copy),
                    waits=[(pe, tr_tgt)], inc=(ac, 1))
                ptv6_prev[0] = ac.n
            pt_done = ac.n
            cw = [(ac, pt_done), (ev, ev.n)]
            for kt in range(nkt):
                pl.op("tensor", lambda t, kt=kt, pjq=pjq, pslot=pslot: t.matmul(
                    ps_cx[:, 0:128],
                    vhb[pslot][:, kt * 128:(kt + 1) * 128],
                    pt_h[:, kt * 1024 + pjq * 128: kt * 1024 + (pjq + 1) * 128],
                    start=(kt == 0), stop=(kt == nkt - 1)),
                    waits=cw if kt == 0 else (),
                    inc=(pe, 1) if kt == nkt - 1 else None)
            cx_tgt = pe.n
            pl.op("vector", lambda v, pjq=pjq, hc=hcur[0]: v.tensor_copy(
                ctx[:, hc * 1024 + pjq * 128: hc * 1024 + (pjq + 1) * 128],
                ps_cx[:, 0:128]),
                waits=[(pe, cx_tgt), (ev, ev.n)], inc=(ev, 1))

        hcur = [0]
        pending = None
        for h in range(H_L1):
            slot = h % 2
            ld_t, lv_t = tgts[slot]
            if h + 1 < H_L1:
                tgts[(h + 1) % 2] = load_head6(h + 1, (h + 1) % 2)
            for jq in range(NT):
                ncau = 256 * (jq + 1)
                nch = (ncau + 511) // 512
                w0 = [(lh[slot], ld_t), (lv[slot], lv_t), (ld, cm6_tgt),
                      (ev, mxev6[0])]
                for n5 in range(nch):
                    w5 = min(512, ncau - n5 * 512)
                    pl.op("tensor", lambda t, jq=jq, n5=n5, w5=w5, slot=slot: t.matmul(
                        ps_s[:, n5 * 512:n5 * 512 + w5],
                        qnh[slot][:, jq * 128:(jq + 1) * 128],
                        knh[slot][:, n5 * 512:n5 * 512 + w5],
                        start=True, stop=False),
                        waits=w0 if n5 == 0 else ())
                    pl.op("tensor", lambda t, jq=jq, n5=n5, w5=w5, slot=slot: t.matmul(
                        ps_s[:, n5 * 512:n5 * 512 + w5],
                        qrh[slot][:, jq * 128:(jq + 1) * 128],
                        kpe_fm[:, n5 * 512:n5 * 512 + w5],
                        start=False, stop=True),
                        inc=(pe, 1) if n5 == nch - 1 else None)
                s_tgt = pe.n
                if pending is not None:
                    emit_pt6(*pending)
                    pending = None
                jb = jq % 2
                vchain(lambda v, jq=jq, ncau=ncau: v.tensor_add(
                    ps_s[:, ncau - 256:ncau], ps_s[:, ncau - 256:ncau],
                    cm6[:, jq * 256:(jq + 1) * 256]), extra=[(pe, s_tgt)])
                vchain(lambda v, ncau=ncau, jb=jb: v.tensor_scalar(
                    s_sb6[jb][:, 0:ncau], ps_s[:, 0:ncau], 1.0, None,
                    ALU.mult, ALU.max, accum_out=pmax6[jb][:, :]),
                    extra=[(ac, exp6[jb])])
                mxev6[0] = ev.n
                vchain(lambda v, jb=jb: v.tensor_scalar(
                    nmax[jb][:, :], pmax6[jb][:, :], -1.0, None,
                    ALU.mult, ALU.bypass))
                mx_tgt = ev.n
                pl.op("scalar", lambda s, ncau=ncau, jb=jb: s.activation(
                    p_raw[jb][:, 0:ncau], s_sb6[jb][:, 0:ncau], AF.Exp,
                    bias=nmax[jb][:, :]),
                    waits=[(ev, mx_tgt)], inc=(ac, 1))
                exp6[jb] = ac.n
                exp_tgt = ac.n
                vchain(lambda v, jq=jq, ncau=ncau, jb=jb: v.scalar_tensor_tensor(
                    p_m[jb][:, 0:ncau], m01[:, jq * 2048:jq * 2048 + ncau], 1.0,
                    p_raw[jb][:, 0:ncau], ALU.mult, ALU.mult,
                    accum_out=sums[jb][:, :]), extra=[(ac, exp_tgt)])
                vchain(lambda v, jb=jb: v.reciprocal(recip[jb][:, :], sums[jb][:, :]))
                vchain(lambda v, ncau=ncau, jb=jb: v.tensor_scalar(
                    p_sb[jb][:, 0:ncau], p_m[jb][:, 0:ncau], recip[jb][:, :], None,
                    ALU.mult, ALU.bypass))
                pending = (jb, ev.n, jq, slot)
            emit_pt6(*pending)
            pending = None
            hcur[0] = h + 1
        ctx_done = ev.n
    return ctx, ctx_done



def host_prep_D(inputs):
    W0 = np.asarray(inputs["L1_ffn_W0"], np.float32)
    b0 = np.asarray(inputs["L1_ffn_b0"], np.float32)
    A = np.asarray(inputs["L1_ffn_A"], np.float32)
    B = np.asarray(inputs["L1_ffn_B"], np.float32)
    d = {}
    d["W0T"] = _tile_rhs(np.ascontiguousarray(W0.T), 32, 8)
    d["b0_t"] = np.ascontiguousarray(np.tile(b0[None, :], (128, 1)))
    d["a_lt"] = np.ascontiguousarray(
        A.reshape(4, 16, 32, 128).transpose(3, 0, 2, 1).reshape(128, 4 * 32 * 16))
    d["BlT"] = np.ascontiguousarray(
        np.concatenate([B[l].T for l in range(4)], axis=1))
    return d


_M8_CACHE = {}


def host_prep_D_core(c):
    if "m8" not in _M8_CACHE:
        import jax
        with jax.default_device(jax.devices("cpu")[0]):
            k42 = jax.random.key(42, impl="threefry2x32")
            u = np.asarray(jax.random.uniform(k42, (4, 2048, 4096)))
        m8 = np.zeros((4, 4, 2048, 4096), np.float32)
        for l in range(4):
            lo_b, hi_b = 0.25 * l, 0.25 * (l + 1)
            m8[l] = ((u >= lo_b) & (u < hi_b)) * 8.0
        _M8_CACHE["m8"] = m8.astype(BF)
    m8 = _M8_CACHE["m8"]
    b, p = c // 2, c % 2
    tiles = [2 * j + p for j in range(NT)]
    idx = np.concatenate([np.arange(g * 128, g * 128 + 128) for g in tiles])
    sl = m8[:, b, idx, :].transpose(1, 0, 2).reshape(NT, 128, 4 * 4096)
    return {"m8": np.ascontiguousarray(sl.reshape(NT, 128 * 4 * 4096))}


def declare_dram_D(nc, io, debug=False):
    ei = lambda n, shp, dt=F32: io.__setitem__(
        n, nc.dram_tensor(n, shp, dt, kind="ExternalInput").ap())
    ei("W0T", [8, 128 * 32 * 512])
    ei("b0_t", [128, D])
    ei("a_lt", [128, 4 * 32 * 16])
    ei("BlT", [16, 4 * 4096])
    ei("m8", [NT, 128 * 4 * 4096], BF16)
    io["s1d"] = nc.dram_tensor("s1d", [NT, 128 * D], F32, kind="Internal").ap()
    if debug:
        io["_mkdbg"]("s1d", [NT, 128 * D], F32)


def phase_D(nc, pl, st, io, sems, glob, x_fm_key="x3_fm", out_key="s1d",
            fm_ready=0):
    ld, wt, pe, ev, ac, sto = (sems["ld"], sems["wt"], sems["pe"], sems["ev"],
                               sems["ac"], sems["sto"])
    lh = [sems["lh0"], sems["lh1"]]
    wtP = [sems["wtA"], sems["wtB"]]
    sb = lambda n, shp, dt=BF16: st.enter_context(nc.sbuf_tensor(_uname(n), shp, dt))
    psa = lambda n, shp: st.enter_context(nc.psum_tensor(_uname(n), shp, F32))

    pl.fence(sems)
    x3fm = sb("d_x3fm", [128, NC_D * 1024])
    hl = sb("d_hl", [16, 4 * 1024])
    blt = sb("d_blt", [16, 4 * 4096])
    alt = sb("d_alt", [128, 4 * 32 * 16])
    b0t = sb("d_b0t", [128, D], F32)
    w0 = [sb(f"d_w0_{i}", [128, 32 * 512]) for i in range(2)]
    msl = [sb(f"d_m{i}", [128, 4 * 512]) for i in range(2)]
    acc = [sb(f"d_acc{i}", [128, 512], F32) for i in range(4)]
    tl = sb("d_tl", [128, 512], F32)
    ps_o = None
    ps_y = None

    pl.op("sync", lambda s: s.dma_start(
        out=x3fm[:, :].rearrange("p (c f) -> p c f", c=NC_D),
        in_=io[x_fm_key].rearrange("c (p f) -> p c f", p=128)),
        waits=[(sems["st"], fm_ready)], inc=(ld, 16))
    pl.op("sync", lambda s: s.dma_start(out=b0t[:, :], in_=io["b0_t"]),
          inc=(ld, 16))
    pl.op("gpsimd", lambda g: g.dma_start(out=blt[:, :], in_=io["BlT"]),
          inc=(wt, 16))
    pl.op("gpsimd", lambda g: g.dma_start(out=alt[:, :], in_=io["a_lt"]),
          inc=(wt, 16))
    x_tgt = ld.n
    sm_tgt = wt.n

    with ExitStack() as sth:
        ps_h = sth.enter_context(nc.psum_tensor("d_psh", [16, 4096], F32))
        lw = [(ld, x_tgt), (wt, sm_tgt), (ev, ev.n), (ac, ac.n)]
        for l in range(4):
            for kc in range(NC_D):
                for h2 in range(2):
                    pl.op("tensor", lambda t, l=l, kc=kc, h2=h2: t.matmul(
                        ps_h[:, l * 1024 + h2 * 512: l * 1024 + (h2 + 1) * 512],
                        alt[:, (l * 32 + kc) * 16: (l * 32 + kc + 1) * 16],
                        x3fm[:, kc * 1024 + h2 * 512: kc * 1024 + h2 * 512 + 512],
                        start=(kc == 0), stop=(kc == 31)),
                        waits=lw if (l == 0 and kc == 0 and h2 == 0) else (),
                        inc=(pe, 1) if (kc == 31 and h2 == 1) else None)
        h_tgt = pe.n
        pl.op("vector", lambda v: v.tensor_copy(hl[:, :], ps_h[:, :]),
              waits=[(pe, h_tgt), (ev, ev.n)], inc=(ev, 1))
        hl_done = ev.n
    ps_o = [psa(f"d_pso{i}", [128, 512]) for i in range(2)]
    ps_y = [psa(f"d_psy{i}", [128, 512]) for i in range(2)]

    w_tgt = [0, 0]
    for obi in range(8):
        buf = obi % 2
        pl.op("gpsimd", lambda g, obi=obi, buf=buf: g.dma_start(
            out=w0[buf][:, :], in_=row128(io["W0T"], obi)),
            waits=[(pe, pe.n)], inc=(wtP[buf], 16))
        w_tgt[buf] = wtP[buf].n
        for jt in range(NT):
            mbuf = jt % 2
            pl.op("sync", lambda s, jt=jt, obi=obi, mbuf=mbuf: s.dma_start(
                out=msl[mbuf][:, :].rearrange("p (l f) -> p l f", l=4),
                in_=row128(io["m8"], jt).rearrange(
                    "p (l f) -> p l f", l=4)[:, :, obi * 512:(obi + 1) * 512]),
                waits=[(ev, ev.n)], inc=(lh[mbuf], 16))
            m_tgt = lh[mbuf].n
            pp = ps_o[buf]
            mw = [(wtP[buf], w_tgt[buf]), (ev, hl_done), (ev, ev.n)]
            for kc in range(NC_D):
                pl.op("tensor", lambda t, jt=jt, kc=kc, pp=pp, buf=buf: t.matmul(
                    pp[:, :],
                    x3fm[:, kc * 1024 + jt * 128: kc * 1024 + (jt + 1) * 128],
                    w0[buf][:, kc * 512:(kc + 1) * 512],
                    start=(kc == 0), stop=(kc == 31)),
                    waits=mw if kc == 0 else (),
                    inc=(pe, 1) if kc == 31 else None)
            mm_tgt = pe.n
            av = acc[(obi * NT + jt) % 4]
            pl.op("vector", lambda v, pp=pp, av=av, obi=obi: v.scalar_tensor_tensor(
                av[:, :], pp[:, :], 1.0, b0t[:, obi * 512:(obi + 1) * 512],
                ALU.mult, ALU.add),
                waits=[(pe, mm_tgt), (sto, sto.n), (ev, ev.n)], inc=(ev, 1))
            for l in range(4):
                ybuf = l % 2
                yw = [(ev, ev.n)]
                pl.op("tensor", lambda t, l=l, jt=jt, obi=obi, ybuf=ybuf: t.matmul(
                    ps_y[ybuf][:, :],
                    hl[:, l * 1024 + jt * 128: l * 1024 + (jt + 1) * 128],
                    blt[:, l * 4096 + obi * 512: l * 4096 + (obi + 1) * 512],
                    start=True, stop=True),
                    waits=[(ev, hl_done), (ev, ev.n)], inc=(pe, 1))
                y_tgt = pe.n
                pl.op("vector", lambda v, l=l, ybuf=ybuf, mbuf=mbuf: v.tensor_mul(
                    tl[:, :], ps_y[ybuf][:, :],
                    msl[mbuf][:, l * 512:(l + 1) * 512]),
                    waits=[(pe, y_tgt), (lh[mbuf], m_tgt), (ev, ev.n)], inc=(ev, 1))
                pl.op("vector", lambda v, av=av: v.tensor_add(
                    av[:, :], av[:, :], tl[:, :]),
                    waits=[(ev, ev.n)], inc=(ev, 1))
            e_tgt = ev.n
            pl.op("sync", lambda s, jt=jt, obi=obi, av=av: s.dma_start(
                out=row128(io[out_key], jt)[:, obi * 512:(obi + 1) * 512],
                in_=av[:, :]), waits=[(ev, e_tgt)], inc=(sto, 16))
    return sto.n



def phase_A1(nc, pl, st, io, sems, glob):
    ld, wt, pe, ev, cc = sems["ld"], sems["wt"], sems["pe"], sems["ev"], sems["cc"]
    sto = sems["sto"]
    wtP = [sems["wtA"], sems["wtB"]]

    sb = lambda n, shp, dt=BF16: st.enter_context(nc.sbuf_tensor(_uname(n), shp, dt))
    psa = lambda n, shp: st.enter_context(nc.psum_tensor(_uname(n), shp, F32))

    x_fm = sb("a_x_fm", [128, NC_D * TPC])
    wqk = [sb(f"a_wqk{i}", [128, 32 * 128]) for i in range(2)]
    wv = [sb(f"a_wv{i}", [128, 32 * 512]) for i in range(2)]
    bqk = sb("a_bqk", [128, 64], F32)
    bv = sb("a_bv", [128, D], F32)
    stg = [sb(f"a_stg{i}", [128, 512]) for i in range(4)]
    ps_a = [psa(f"a_ps{i}", [128, 1024]) for i in range(4)]

    pl.op("gpsimd", lambda g: g.dma_start(
        out=x_fm[:, :].rearrange("p (c f) -> p c f", c=NC_D),
        in_=io["x_fm"].rearrange("c (p f) -> p c f", p=128)),
        inc=(wt, 16))
    x_ready = wt.n
    pl.op("sync", lambda s: s.dma_start(out=bqk[:, :], in_=io["b_qk"]), inc=(ld, 16))
    pl.op("sync", lambda s: s.dma_start(out=bv[:, :], in_=io["b_v_t"]), inc=(ld, 16))
    b_ready = ld.n

    SC = float(1.0 / np.sqrt(128.0))
    n_ev0 = ev.n
    for m in range(64):
        buf = m % 2
        wsb = wqk[buf]
        waits = [(ev, max(n_ev0, ev.n - 1))] if m >= 2 else []
        pl.op("gpsimd", lambda g, m=m, wsb=wsb: g.dma_start(
            out=wsb[:, :], in_=row128(io["w_qk"], m)),
            waits=waits, inc=(wtP[buf], 16))
        w_tgt = wtP[buf].n
        pba, pbb = ps_a[2 * buf], ps_a[2 * buf + 1]
        pe_waits = [(wtP[buf], w_tgt), (wt, wt.n),
                    (ev, max(n_ev0, ev.n - 1))]
        for kc in range(32):
            for h2 in range(2):
                pp = pba if h2 == 0 else pbb
                pl.op("tensor", lambda t, kc=kc, h2=h2, wsb=wsb, pp=pp: t.matmul(
                    pp[:, h2 * 0:512] if False else pp[:, 0:512],
                    wsb[:, kc * 128:(kc + 1) * 128],
                    x_fm[:, kc * TPC + h2 * 512: kc * TPC + (h2 + 1) * 512],
                    start=(kc == 0), stop=(kc == 31)),
                    waits=pe_waits if (kc == 0 and h2 == 0) else (),
                    inc=(pe, 1) if (kc == 31 and h2 == 1) else None)
        mm_tgt = pe.n
        is_q = m < 32
        for h2 in range(2):
            pp = pba if h2 == 0 else pbb
            dst = stg[2 * buf + h2]
            ew = [(pe, mm_tgt), (ld, b_ready), (sto, sto.n)]
            if is_q:
                pl.op("vector", lambda v, pp=pp, dst=dst, m=m: v.tensor_scalar(
                    dst[:, :], pp[:, 0:512], bqk[:, m:m + 1], SC, ALU.add, ALU.mult),
                    waits=ew if h2 == 0 else (),
                    inc=(ev, 1) if h2 == 1 else None)
            else:
                pl.op("vector", lambda v, pp=pp, dst=dst, m=m: v.tensor_scalar(
                    dst[:, :], pp[:, 0:512], bqk[:, m:m + 1], None, ALU.add, ALU.bypass),
                    waits=ew if h2 == 0 else (),
                    inc=(ev, 1) if h2 == 1 else None)
        ev_tgt = ev.n
        drow = row128(io["q_dr"], m) if is_q else row128(io["kv_src"], m - 32)
        pl.op("sync", lambda s, drow=drow, a=stg[2 * buf]: s.dma_start(
            out=drow[:, 0:512], in_=a[:, :]), waits=[(ev, ev_tgt)], inc=(sto, 16))
        pl.op("sync", lambda s, drow=drow, b=stg[2 * buf + 1]: s.dma_start(
            out=drow[:, 512:1024], in_=b[:, :]), inc=(sto, 16))

    for obv in range(8):
        buf = obv % 2
        wsb = wv[buf]
        waits = ([(ev, max(n_ev0, ev.n - 8))] if obv >= 2
                 else [(ev, ev.n)])
        pl.op("gpsimd", lambda g, obv=obv, wsb=wsb: g.dma_start(
            out=wsb[:, :], in_=row128(io["w_v"], obv)),
            waits=waits, inc=(wtP[buf], 16))
        w_tgt = wtP[buf].n
        for jt in range(NT):
            pp = ps_a[2 * buf + (jt % 2)]
            if obv < 2 and jt < 2:
                pe_waits = [(wtP[buf], w_tgt), (ev, ev.n)]
            else:
                pe_waits = [(wtP[buf], w_tgt), (ev, ev.n - 1)]
            for kc in range(32):
                pl.op("tensor", lambda t, jt=jt, kc=kc, wsb=wsb, pp=pp: t.matmul(
                    pp[:, 0:512],
                    x_fm[:, kc * TPC + jt * 128: kc * TPC + (jt + 1) * 128],
                    wsb[:, kc * 512:(kc + 1) * 512],
                    start=(kc == 0), stop=(kc == 31)),
                    waits=pe_waits if kc == 0 else (),
                    inc=(pe, 1) if kc == 31 else None)
            mm_tgt = pe.n
            sidx = 2 * buf + (jt % 2)
            dst = stg[sidx]
            pl.op("vector", lambda v, pp=pp, dst=dst, obv=obv: v.scalar_tensor_tensor(
                dst[:, :], pp[:, 0:512], 1.0, bv[:, obv * 512:(obv + 1) * 512],
                ALU.mult, ALU.add),
                waits=[(pe, mm_tgt), (sto, sto.n)], inc=(ev, 1))
            ev_tgt = ev.n
            vrow = row128(io["kv_src"], 32 + jt * 4 + obv // 2)
            pl.op("sync", lambda s, vrow=vrow, dst=dst, obv=obv: s.dma_start(
                out=vrow[:, (obv % 2) * 512:(obv % 2) * 512 + 512], in_=dst[:, :]),
                waits=[(ev, ev_tgt)], inc=(sto, 16))

    pl.op("gpsimd", lambda g: g.collective_compute(
        "AllGather", ALU.bypass, replica_groups=REPLICA_PAIRS,
        ins=[io["kv_src"].opt()], outs=[io["kv_dst"].opt()]),
        waits=[(ld, ld.n), (sto, sto.n), (ev, ev.n)], inc=(cc, 1))
    return cc.n


def phase_A2(nc, pl, st, io, sems, glob, cc_done, ctx):
    ld, ld2, wt = sems["ld"], sems["ld2"], sems["wt"]
    pe, ev, ac, cc = sems["pe"], sems["ev"], sems["ac"], sems["cc"]

    sb = lambda n, shp, dt=BF16: st.enter_context(nc.sbuf_tensor(_uname(n), shp, dt))
    psa = lambda n, shp: st.enter_context(nc.psum_tensor(_uname(n), shp, F32))

    qh = [sb(f"a2_q{i}", [128, 1024]) for i in range(2)]
    kh = [sb(f"a2_k{i}", [128, 2048]) for i in range(2)]
    vh = [sb(f"a2_v{i}", [128, 2048]) for i in range(2)]
    p_raw = [sb("a2_praw", [128, 2048]) for _ in range(2)]
    p_nrm = [sb("a2_pnrm", [128, 2048]) for _ in range(2)]
    s_sb = [sb("a2_ssb", [128, 2048], F32) for _ in range(2)]
    pt_h = sb("a2_pt", [128, 16 * 1024])
    nmax = [sb("a2_nmax", [128, 1], F32) for _ in range(2)]
    pmax = [sb("a2_pmax", [128, 1], F32) for _ in range(2)]
    sums = [sb("a2_sums", [128, 1], F32) for _ in range(2)]
    recip = [sb("a2_recip", [128, 1], F32) for _ in range(2)]

    ps_s = psa("a2_ps_s", [128, 2048])
    ps_pt = psa("a2_ps_pt", [128, 2048])
    ps_c = ps_pt[:, 0:1024]

    ident = glob["ident"]
    id_tgt = glob["id_tgt"]
    kv_dst = io["kv_dst"]

    lh = [sems["lh0"], sems["lh1"]]
    lv = [sems["lv0"], sems["lv1"]]
    idw = wt.n
    wtP = [sems["wtA"], sems["wtB"]]

    def load_head(h, slot):
        g = [(pe, pe.n)]
        for r in range(2):
            pl.op("sync", lambda s, h=h, r=r, slot=slot: s.dma_start(
                out=kh[slot][:, r * 1024:(r + 1) * 1024],
                in_=row128(kv_dst, r * 64 + h)),
                waits=g if r == 0 else (), inc=(lh[slot], 16))
        for r in range(2):
            src = kv_dst[r * 64 + 32 + h // 8: r * 64 + 64: 4, :].rearrange(
                "jk (t f) -> t jk f", t=128)[:, :, (h % 8) * 128:(h % 8) * 128 + 128]
            pl.op("scalar", lambda s, src=src, r=r, slot=slot: s.dma_start(
                out=vh[slot][:, r * 1024:(r + 1) * 1024].rearrange(
                    "t (jk d) -> t jk d", jk=8), in_=src),
                waits=g if r == 0 else (), inc=(lv[slot], 16))
        pl.op("sync", lambda s, h=h, slot=slot: s.dma_start(
            out=qh[slot][:, :], in_=row128(io["q_dr"], h)), inc=(lh[slot], 16))
        return lh[slot].n, lv[slot].n

    pl.fence(sems)
    pl.wait("sync", sems["cc"], cc_done)
    pl.wait("scalar", sems["cc"], cc_done)
    mxev_tgt = [sems["ev"].n]
    ptv_prev = [sems["ac"].n]
    ctxev_prev = [sems["ev"].n]
    exp_prev = {0: sems["ac"].n, 1: sems["ac"].n}
    tgts = [None, None]
    tgts[0] = load_head(0, 0)

    def emit_pt(pjb, pnrm, pjq):
        tw = [(ev, pnrm), (ac, ptv_prev[0]), (ev, ctxev_prev[0])]
        for kc in range(16):
            pl.op("tensor", lambda t, kc=kc, pjb=pjb: t.matmul(
                ps_pt[:, kc * 128:(kc + 1) * 128],
                p_nrm[pjb][:, kc * 128:(kc + 1) * 128],
                ident[:, :], start=True, stop=True),
                waits=tw if kc == 0 else (),
                inc=(pe, 1) if kc == 15 else None)
        tr_tgt = pe.n
        dst = pt_h[:, :].rearrange("p (kc q) -> p kc q", kc=16)[
            :, :, pjq * 128:(pjq + 1) * 128]
        pl.op("scalar", lambda s, dst=dst: s.activation(
            dst, ps_pt[:, :].rearrange("p (a b) -> p a b", a=16), AF.Copy),
            waits=[(pe, tr_tgt)], inc=(ac, 1))
        ptv_prev[0] = ac.n

    pending = None
    for h in range(32):
        slot = h % 2
        ld_t, ld2_t = tgts[slot]
        if h + 1 < 32:
            tgts[(h + 1) % 2] = load_head(h + 1, (h + 1) % 2)
        for jq in range(NT):
            w0 = [(lh[slot], ld_t), (lv[slot], ld2_t), (wt, idw),
                  (ev, mxev_tgt[0])]
            for n5 in range(4):
                pl.op("tensor", lambda t, jq=jq, n5=n5, slot=slot: t.matmul(
                    ps_s[:, n5 * 512:(n5 + 1) * 512],
                    qh[slot][:, jq * 128:(jq + 1) * 128],
                    kh[slot][:, n5 * 512:(n5 + 1) * 512],
                    start=True, stop=True),
                    waits=w0 if n5 == 0 else (),
                    inc=(pe, 1) if n5 == 3 else None)
            s_tgt = pe.n
            if pending is not None:
                emit_pt(*pending)
                pending = None
            jb = jq % 2
            pl.op("vector", lambda v, jb=jb: v.tensor_scalar(
                s_sb[jb][:, :], ps_s[:, :], 1.0, None, ALU.mult, ALU.max,
                accum_out=pmax[jb][:, :]),
                waits=[(pe, s_tgt), (ac, exp_prev[jb]), (ev, ev.n)], inc=(ev, 1))
            mxev_tgt[0] = ev.n
            pl.op("vector", lambda v, jb=jb: v.tensor_scalar(
                nmax[jb][:, :], pmax[jb][:, :], -1.0, None, ALU.mult, ALU.bypass),
                waits=[(ev, ev.n)], inc=(ev, 1))
            mx_tgt = ev.n
            pl.op("scalar", lambda s, jb=jb: s.activation(
                p_raw[jb][:, :], s_sb[jb][:, :], AF.Exp, bias=nmax[jb][:, :],
                accum_out=sums[jb][:, :]),
                waits=[(ev, mx_tgt)], inc=(ac, 1))
            exp_prev[jb] = ac.n
            exp_tgt = ac.n
            pl.op("vector", lambda v, jb=jb: v.reciprocal(
                recip[jb][:, :], sums[jb][:, :]),
                waits=[(ac, exp_tgt)], inc=(ev, 1))
            pl.op("vector", lambda v, jb=jb: v.tensor_scalar(
                p_nrm[jb][:, :], p_raw[jb][:, :], recip[jb][:, :], None,
                ALU.mult, ALU.bypass),
                waits=[(ev, ev.n), (pe, pe.n)], inc=(ev, 1))
            pending = (jb, ev.n, jq)
        emit_pt(*pending)
        pending = None
        pt_done = ac.n
        cw = [(ac, pt_done), (ev, ev.n)]
        for kc in range(16):
            for q2 in range(2):
                pl.op("tensor", lambda t, kc=kc, q2=q2, slot=slot: t.matmul(
                    ps_c[:, q2 * 512:(q2 + 1) * 512],
                    vh[slot][:, kc * 128:(kc + 1) * 128],
                    pt_h[:, kc * 1024 + q2 * 512: kc * 1024 + (q2 + 1) * 512],
                    start=(kc == 0), stop=(kc == 15)),
                    waits=cw if (kc == 0 and q2 == 0) else (),
                    inc=(pe, 1) if (kc == 15 and q2 == 1) else None)
        ctx_tgt = pe.n
        pl.op("vector", lambda v, h=h: v.tensor_copy(
            ctx[:, h * 1024:(h + 1) * 1024], ps_c[:, :]),
            waits=[(pe, ctx_tgt)], inc=(ev, 1))
        ctxev_prev[0] = ev.n
    return ev.n


def phase_A3a(nc, pl, st, io, sems, glob, ctx, ctx_done, w_key="w_o",
              bias_key="ob_t", s1a_key="s1a"):
    ld, wt, pe, ev = sems["ld"], sems["wt"], sems["pe"], sems["ev"]
    sto = sems["sto"]
    wtP = [sems["wtA"], sems["wtB"]]

    sb = lambda n, shp, dt=BF16: st.enter_context(nc.sbuf_tensor(_uname(n), shp, dt))
    psa = lambda n, shp: st.enter_context(nc.psum_tensor(_uname(n), shp, F32))

    wo = [sb(f"a3_wo{i}", [128, 32 * 512]) for i in range(2)]
    obt = sb("a3_obt", [128, D], F32)
    pl.fence(sems)
    stg = [sb(f"a3_stg{i}", [128, 512], F32) for i in range(4)]
    ps_o = [psa(f"a3_ps{i}", [128, 512]) for i in range(4)]

    pl.op("sync", lambda s: s.dma_start(out=obt[:, :], in_=io[bias_key]),
          waits=[(ev, ctx_done)], inc=(ld, 16))
    obt_tgt = ld.n

    wo_tgt = [None, None]
    pe0 = pe.n
    for obi in range(8):
        buf = obi % 2
        lw = [(ev, ctx_done)] if obi < 2 else [(pe, max(pe0, pe.n - 8))]
        pl.op("gpsimd", lambda g, obi=obi, buf=buf: g.dma_start(
            out=wo[buf][:, :], in_=row128(io[w_key], obi)), waits=lw,
            inc=(wtP[buf], 16))
        wo_tgt[buf] = wtP[buf].n
        for jt in range(NT):
            pp = ps_o[jt % 4]
            mmw = [(wtP[buf], wo_tgt[buf]), (ev, ctx_done), (ev, ev.n)]
            for kc in range(32):
                pl.op("tensor", lambda t, jt=jt, kc=kc, pp=pp, buf=buf: t.matmul(
                    pp[:, :],
                    ctx[:, kc * 1024 + jt * 128: kc * 1024 + (jt + 1) * 128],
                    wo[buf][:, kc * 512:(kc + 1) * 512],
                    start=(kc == 0), stop=(kc == 31)),
                    waits=mmw if kc == 0 else (),
                    inc=(pe, 1) if kc == 31 else None)
            mm_tgt = pe.n
            dst = stg[jt % 4]
            pl.op("vector", lambda v, pp=pp, dst=dst, obi=obi: v.scalar_tensor_tensor(
                dst[:, :], pp[:, :], 1.0, obt[:, obi * 512:(obi + 1) * 512],
                ALU.mult, ALU.add),
                waits=[(pe, mm_tgt), (ld, obt_tgt), (sto, sto.n)], inc=(ev, 1))
            ev_tgt = ev.n
            pl.op("sync", lambda s, jt=jt, obi=obi, dst=dst: s.dma_start(
                out=row128(io[s1a_key], jt)[:, obi * 512:(obi + 1) * 512],
                in_=dst[:, :]), waits=[(ev, ev_tgt)], inc=(sto, 16))
    return sto.n


def phase_LN(nc, pl, st, io, sems, glob, in_keys, out_prefix, s1a_stores,
             fold=False, want_fm=True):
    ld, wt, pe, ev, ac, stq = (sems["ld"], sems["wt"], sems["pe"], sems["ev"],
                               sems["ac"], sems["st"])

    sb = lambda n, shp, dt=BF16: st.enter_context(nc.sbuf_tensor(_uname(n), shp, dt))
    psa = lambda n, shp: st.enter_context(nc.psum_tensor(_uname(n), shp, F32))

    xa = [sb(f"ln_xa{i}", [128, D], F32) for i in range(2)]
    xbb = [sb(f"ln_xb{i}", [128, D], F32) for i in range(2)]
    s1 = sb("ln_s1", [128, D], F32)
    xn = [sb(f"ln_xn{i}", [128, D], F32) for i in range(2)]
    xnb = [sb(f"ln_xnb{i}", [128, D]) for i in range(2)]
    xf = sb("ln_xf", [128, 1024], F32)
    xfb = [sb(f"ln_xfb{i}", [128, 1024]) for i in range(2)]
    fsc = sb("ln_fsc", [128, 1024], F32)
    fm_stg = [sb(f"ln_fmstg{i}", [128, NC_D * 128]) for i in range(2)]
    ff_stg = [sb(f"ln_ffstg{i}", [128, NT * 128]) for i in range(2)]
    msum = sb("ln_msum", [128, 1], F32)
    mean = sb("ln_mean", [128, 1], F32)
    ssum = sb("ln_ssum", [128, 1], F32)
    var = sb("ln_var", [128, 1], F32)
    rstd = sb("ln_rstd", [128, 1], F32)
    tmp1 = sb("ln_tmp1", [128, 1], F32)
    ps_t = psa("ln_ps_t", [128, 1024])

    ident = glob["ident"]
    ngrp = (10 if fold else 8) if want_fm else 0

    sto = sems["sto"]
    pl.fence(sems)
    add_tgt = {}
    xnst = {0: stq.n, 1: stq.n}
    fmst = {0: stq.n, 1: stq.n}
    ffst = {0: stq.n, 1: stq.n}
    trdone = {0: pe.n, 1: pe.n}
    pl.op("sync", lambda s: s.dma_start(
        out=xa[0][:, :], in_=row128(io[in_keys[0]], 0)),
        waits=[(sto, s1a_stores), (ev, ev.n), (ac, ac.n)], inc=(ld, 16))
    pl.op("sync", lambda s: s.dma_start(
        out=xbb[0][:, :], in_=row128(io[in_keys[1]], 0)), inc=(ld, 16))
    x_tgt = {0: ld.n}
    for jt in range(NT):
        buf = jt % 2
        def vstep(fn, extra=()):
            pl.op("vector", fn, waits=[(ev, ev.n)] + list(extra), inc=(ev, 1))

        pl.op("vector", lambda v, buf=buf: v.tensor_add(
            s1[:, :], xa[buf][:, :], xbb[buf][:, :]),
            waits=[(ld, x_tgt[jt]), (ev, ev.n)], inc=(ev, 1))
        add_tgt[jt] = ev.n
        vstep(lambda v: v.reduce_sum(msum[:, :], s1[:, :], AX.X))
        if jt + 1 < NT:
            pw = [(ev, add_tgt[jt - 1])] if jt >= 1 else []
            pl.op("sync", lambda s, jt=jt, buf=buf: s.dma_start(
                out=xa[buf ^ 1][:, :], in_=row128(io[in_keys[0]], jt + 1)),
                waits=pw, inc=(ld, 16))
            pl.op("sync", lambda s, jt=jt, buf=buf: s.dma_start(
                out=xbb[buf ^ 1][:, :], in_=row128(io[in_keys[1]], jt + 1)),
                inc=(ld, 16))
            x_tgt[jt + 1] = ld.n
        vstep(lambda v: v.tensor_scalar(
            mean[:, :], msum[:, :], 1.0 / D, None, ALU.mult, ALU.bypass))
        vstep(lambda v, buf=buf: v.scalar_tensor_tensor(
            xn[buf][:, :], s1[:, :], 1.0, s1[:, :], ALU.mult, ALU.mult,
            accum_out=ssum[:, :]),
            extra=[(stq, xnst[buf])])
        vstep(lambda v: v.scalar_tensor_tensor(
            tmp1[:, :], mean[:, :], 1.0, mean[:, :], ALU.mult, ALU.mult))
        vstep(lambda v: v.tensor_scalar(
            var[:, :], ssum[:, :], 1.0 / D, None, ALU.mult, ALU.bypass))
        vstep(lambda v: v.tensor_sub(var[:, :], var[:, :], tmp1[:, :]))
        var_tgt = ev.n
        pl.op("scalar", lambda s: s.activation(
            rstd[:, :], var[:, :], AF.Ln, bias=glob["eps"][:, :]),
            waits=[(ev, var_tgt), (ac, ac.n)], inc=(ac, 1))
        pl.op("scalar", lambda s: s.activation(
            rstd[:, :], rstd[:, :], AF.Exp, scale=-0.5),
            waits=[(ac, ac.n)], inc=(ac, 1))
        rstd_tgt = ac.n
        vstep(lambda v, buf=buf: v.tensor_scalar(
            xn[buf][:, :], s1[:, :], mean[:, :], rstd[:, :],
            ALU.subtract, ALU.mult),
            extra=[(ac, rstd_tgt)])
        xn_tgt = ev.n
        pl.op("sync", lambda s, jt=jt, buf=buf: s.dma_start(
            out=row128(io[f"{out_prefix}_tm"], jt), in_=xn[buf][:, :]),
            waits=[(ev, xn_tgt)], inc=(stq, 16))
        xnst[buf] = stq.n
        vstep(lambda v, buf=buf: v.tensor_copy(xnb[buf][:, :], xn[buf][:, :]),
              extra=[(pe, trdone[buf])])
        if fold:
            vstep(lambda v, buf=buf: v.tensor_add(
                xf[:, :], xn[buf][:, 0:1024], xn[buf][:, 1024:2048]))
            vstep(lambda v, buf=buf: v.tensor_add(
                fsc[:, :], xn[buf][:, 2048:3072], xn[buf][:, 3072:4096]))
            vstep(lambda v: v.tensor_add(
                xf[:, :], xf[:, :], fsc[:, :]))
            vstep(lambda v, buf=buf: v.tensor_copy(xfb[buf][:, :], xf[:, :]),
                  extra=[(pe, trdone[buf])])
        cast_tgt = ev.n
        evprev = ac.n
        evac_tgt = {}
        for grp in range(ngrp):
            half = grp % 2
            tw = [(ev, cast_tgt),
                  (ac, evac_tgt[grp - 2] if grp >= 2 else evprev)]
            for q4 in range(4):
                cch = grp * 4 + q4
                src = xnb[buf] if cch < 32 else xfb[buf]
                coff = cch * 128 if cch < 32 else (cch - 32) * 128
                pl.op("tensor", lambda t, src=src, coff=coff, half=half, q4=q4: t.matmul(
                    ps_t[:, half * 512 + q4 * 128: half * 512 + (q4 + 1) * 128],
                    src[:, coff:coff + 128], ident[:, :], start=True, stop=True),
                    waits=tw if q4 == 0 else (),
                    inc=(pe, 1) if q4 == 3 else None)
            tr_tgt = pe.n
            dsts = fm_stg[buf] if grp < 8 else ff_stg[buf]
            doff = grp * 512 if grp < 8 else (grp - 8) * 512
            ew = [(pe, tr_tgt)]
            if grp == 0:
                ew.append((stq, fmst[buf]))
            if grp == 8:
                ew.append((stq, ffst[buf]))
            pl.op("scalar", lambda s, dsts=dsts, doff=doff, half=half: s.activation(
                dsts[:, doff:doff + 512],
                ps_t[:, half * 512:(half + 1) * 512], AF.Copy),
                waits=ew, inc=(ac, 1))
            evac_tgt[grp] = ac.n
        if want_fm:
            trdone[buf] = pe.n
            pl.op("sync", lambda s, jt=jt, buf=buf: s.dma_start(
                out=io[f"{out_prefix}_fm"].rearrange(
                    "c (p f) -> p c f", p=128)[:, :, jt * 128:(jt + 1) * 128],
                in_=fm_stg[buf][:, :].rearrange("p (c f) -> p c f", c=NC_D)),
                waits=[(ac, evac_tgt[7])], inc=(stq, 16))
            fmst[buf] = stq.n
        if fold:
            pl.op("sync", lambda s, jt=jt, buf=buf: s.dma_start(
                out=io[f"{out_prefix}f_fm"].rearrange(
                    "c (p f) -> p c f", p=128)[:, :, jt * 128:(jt + 1) * 128],
                in_=ff_stg[buf][:, :].rearrange("p (c f) -> p c f", c=NT)),
                waits=[(ac, evac_tgt[9])], inc=(stq, 16))
            ffst[buf] = stq.n
    return stq.n



def make_sems(nc, st):
    names = ["ld", "ld2", "wt", "wtA", "wtB", "pe", "ev", "ac", "cc", "st",
             "sto", "lh0", "lh1", "lv0", "lv1",
             "b_tensor", "b_vector", "b_scalar", "b_gpsimd", "b_sync"]
    sems = {}
    for n in names:
        h = st.enter_context(nc.semaphore(n))
        sems[n] = Sem(h, n)
    sems_b = {e: sems[f"b_{e}"] for e in Plan.ENGINES}
    sems["bar"] = sems_b
    return sems


def declare_dram_A(nc, debug=False):
    io = {}
    ei = lambda n, shp, dt=F32: io.__setitem__(
        n, nc.dram_tensor(n, shp, dt, kind="ExternalInput").ap())
    eo = lambda n, shp, dt=F32: io.__setitem__(
        n, nc.dram_tensor(n, shp, dt, kind="ExternalOutput").ap())
    internal = lambda n, shp, dt=BF16: io.__setitem__(
        n, nc.dram_tensor(n, shp, dt, kind="Internal").ap())
    if debug:
        io["_dbg"] = []
        def dbg(n, shp, dt):
            io.__setitem__("dbg_" + n,
                           nc.dram_tensor("dbg_" + n, shp, dt,
                                          kind="ExternalOutput").ap())
            io["_dbg"].append(n)
        io["_mkdbg"] = dbg
    ei("x_tm", [NT, 128 * D])
    ei("x_fm", [NC_D, 128 * TPC])
    ei("w_qk", [64, 128 * 32 * 128])
    ei("w_v", [8, 128 * 32 * 512])
    ei("b_qk", [128, 64])
    ei("b_v_t", [128, D])
    ei("w_o", [8, 128 * 32 * 512])
    ei("ob_t", [128, D])
    ei("ident", [128, 128])
    internal("q_dr", [32, 128 * 1024])
    internal("kv_src", [64, 128 * 1024])
    internal("kv_dst", [128, 128 * 1024])
    internal("s1a", [NT, 128 * D], F32)
    eo("x1_tm", [NT, 128 * D])
    eo("x1_fm", [NC_D, 128 * TPC], BF16)
    eo("x1f_fm", [NT, 128 * TPC], BF16)
    if debug:
        io["_mkdbg"]("q_dr", [32, 128 * 1024], BF16)
        io["_mkdbg"]("kv_src", [64, 128 * 1024], BF16)
        io["_mkdbg"]("s1a", [NT, 128 * D], F32)
        io["dbg_ctx"] = nc.dram_tensor(
            "dbg_ctx", [32, 128 * 1024], BF16, kind="ExternalOutput").ap()
    return io


def build_program_A(debug=False):
    nc = bass.Bass("TRN2", target_bir_lowering=False)
    io = declare_dram_A(nc, debug=debug)
    pl = Plan()
    with ExitStack() as top:
        sems = make_sems(nc, top)
        glob = {}
        glob["ident"] = top.enter_context(nc.sbuf_tensor("g_ident", [128, 128], BF16))
        pl.op("gpsimd", lambda g: g.dma_start(
            out=glob["ident"][:, :], in_=io["ident"]), inc=(sems["wt"], 16))
        glob["id_tgt"] = sems["wt"].n
        glob["eps"] = top.enter_context(nc.sbuf_tensor("g_eps", [128, 1], F32))
        pl.op("vector", lambda v: v.memset(glob["eps"][:, :], 1e-5),
              inc=(sems["ev"], 1))
        glob["eps_tgt"] = sems["ev"].n

        with ExitStack() as st:
            cc_done = phase_A1(nc, pl, st, io, sems, glob)
            pl.barrier(sems["bar"], dma_sems=[sems["ld"], sems["wt"], sems["wtA"], sems["wtB"], sems["sto"]])
        with ExitStack() as st:
            ctx = st.enter_context(nc.sbuf_tensor("a2_ctx", [128, 32 * 1024], BF16))
            with ExitStack() as st2:
                ctx_done = phase_A2(nc, pl, st2, io, sems, glob, cc_done, ctx)
            if debug:
                for hh in range(32):
                    pl.op("sync", lambda s, hh=hh: s.dma_start(
                        out=row128(io["dbg_ctx"], hh),
                        in_=ctx[:, hh * 1024:(hh + 1) * 1024]),
                        waits=[(sems["ev"], ctx_done)] if hh == 0 else (),
                        inc=(sems["st"], 16))
            s1a_stores = phase_A3a(nc, pl, st, io, sems, glob, ctx, ctx_done)
            pl.barrier(sems["bar"], dma_sems=[
                sems["ld"], sems["ld2"], sems["wt"], sems["wtA"], sems["wtB"],
                sems["sto"],
                sems["lh0"], sems["lh1"], sems["lv0"], sems["lv1"]])
        with ExitStack() as st:
            phase_LN(nc, pl, st, io, sems, glob,
                     ["x_tm", "s1a"], "x1", s1a_stores, fold=True)
        if debug:
            for n in io["_dbg"]:
                pl.op("sync", lambda s, n=n: s.dma_start(
                    out=io["dbg_" + n], in_=io[n]),
                    waits=[(sems["st"], sems["st"].n)], inc=(sems["st"], 16))
        pl.wait("gpsimd", sems["st"], sems["st"].n)
        pl.wait("sync", sems["st"], sems["st"].n)
        pl.realize(nc)
    return nc


def declare_x(nc, io, prefix, kind_tm="ExternalInput"):
    io[f"{prefix}_tm"] = nc.dram_tensor(
        f"{prefix}_tm", [NT, 128 * D], F32, kind=kind_tm).ap()
    io[f"{prefix}_fm"] = nc.dram_tensor(
        f"{prefix}_fm", [NC_D, 128 * TPC], BF16, kind=kind_tm).ap()
    io[f"{prefix}f_fm"] = nc.dram_tensor(
        f"{prefix}f_fm", [NT, 128 * TPC], BF16, kind=kind_tm).ap()


def build_program_B(debug=False):
    nc = bass.Bass("TRN2", target_bir_lowering=False)
    io = {}
    if debug:
        io["_dbg"] = []
        def dbg(n, shp, dt):
            io["dbg_" + n] = nc.dram_tensor(
                "dbg_" + n, shp, dt, kind="ExternalOutput").ap()
            io["_dbg"].append(n)
        io["_mkdbg"] = dbg
    declare_x(nc, io, "x1", "ExternalInput")
    io["ident"] = nc.dram_tensor("ident", [128, 128], F32,
                                 kind="ExternalInput").ap()
    declare_dram_B(nc, io, debug=debug)
    io["x2_tm"] = nc.dram_tensor("x2_tm", [NT, 128 * D], F32,
                                 kind="ExternalOutput").ap()
    io["x2_fm"] = nc.dram_tensor("x2_fm", [NC_D, 128 * TPC], BF16,
                                 kind="ExternalOutput").ap()
    pl = Plan()
    with ExitStack() as top:
        sems = make_sems(nc, top)
        glob = {}
        glob["ident"] = top.enter_context(nc.sbuf_tensor("g_ident", [128, 128], BF16))
        pl.op("gpsimd", lambda g: g.dma_start(
            out=glob["ident"][:, :], in_=io["ident"]), inc=(sems["wt"], 16))
        glob["id_tgt"] = sems["wt"].n
        glob["eps"] = top.enter_context(nc.sbuf_tensor("g_eps", [128, 1], F32))
        pl.op("vector", lambda v: v.memset(glob["eps"][:, :], 1e-5),
              inc=(sems["ev"], 1))

        with ExitStack() as st:
            s1b_stores = phase_B(nc, pl, st, io, sems, glob)
            pl.barrier(sems["bar"], dma_sems=[sems["ld"], sems["wt"], sems["wtA"], sems["wtB"], sems["sto"]])
        with ExitStack() as st:
            phase_LN(nc, pl, st, io, sems, glob,
                     ["x1_tm", "s1b"], "x2", s1b_stores, fold=False)
        if debug:
            for n in io["_dbg"]:
                pl.op("sync", lambda s, n=n: s.dma_start(
                    out=io["dbg_" + n], in_=io[n]),
                    waits=[(sems["st"], sems["st"].n), (sems["ld"], sems["ld"].n)],
                    inc=(sems["st"], 16))
        pl.wait("gpsimd", sems["st"], sems["st"].n)
        pl.wait("sync", sems["st"], sems["st"].n)
        pl.realize(nc)
    return nc


def build_program_C(debug=False):
    nc = bass.Bass("TRN2", target_bir_lowering=False)
    io = {}
    if debug:
        io["_dbg"] = []
        def dbg(n, shp, dt):
            io["dbg_" + n] = nc.dram_tensor(
                "dbg_" + n, shp, dt, kind="ExternalOutput").ap()
            io["_dbg"].append(n)
        io["_mkdbg"] = dbg
    declare_x(nc, io, "x2", "ExternalInput")
    io["ident"] = nc.dram_tensor("ident", [128, 128], F32,
                                 kind="ExternalInput").ap()
    declare_dram_C(nc, io, debug=debug)
    io["x3_tm"] = nc.dram_tensor("x3_tm", [NT, 128 * D], F32,
                                 kind="ExternalOutput").ap()
    io["x3_fm"] = nc.dram_tensor("x3_fm", [NC_D, 128 * TPC], BF16,
                                 kind="ExternalOutput").ap()
    pl = Plan()
    with ExitStack() as top:
        sems = make_sems(nc, top)
        glob = {}
        glob["ident"] = top.enter_context(nc.sbuf_tensor("g_ident", [128, 128], BF16))
        pl.op("gpsimd", lambda g: g.dma_start(
            out=glob["ident"][:, :], in_=io["ident"]), inc=(sems["wt"], 16))
        glob["id_tgt"] = sems["wt"].n
        glob["eps"] = top.enter_context(nc.sbuf_tensor("g_eps", [128, 1], F32))
        pl.op("vector", lambda v: v.memset(glob["eps"][:, :], 1e-5),
              inc=(sems["ev"], 1))

        with ExitStack() as st:
            ctx, ctx_done = phase_C(nc, pl, st, io, sems, glob)
            s1c_stores = phase_A3a(nc, pl, st, io, sems, glob, ctx, ctx_done,
                                   w_key="o_w_t", bias_key="zeros_t",
                                   s1a_key="s1c")
            pl.barrier(sems["bar"], dma_sems=[
                sems["ld"], sems["ld2"], sems["wt"], sems["wtA"], sems["wtB"],
                sems["sto"],
                sems["lh0"], sems["lh1"], sems["lv0"], sems["lv1"]])
        with ExitStack() as st:
            phase_LN(nc, pl, st, io, sems, glob,
                     ["x2_tm", "s1c"], "x3", s1c_stores, fold=False)
        if debug:
            for n in io["_dbg"]:
                pl.op("sync", lambda s, n=n: s.dma_start(
                    out=io["dbg_" + n], in_=io[n]),
                    waits=[(sems["st"], sems["st"].n), (sems["sto"], sems["sto"].n)],
                    inc=(sems["st"], 16))
        pl.wait("gpsimd", sems["st"], sems["st"].n)
        pl.wait("sync", sems["st"], sems["st"].n)
        pl.realize(nc)
    return nc




def build_program_D(debug=False):
    nc = bass.Bass("TRN2", target_bir_lowering=False)
    io = {}
    if debug:
        io["_dbg"] = []
        def dbg(n, shp, dt):
            io["dbg_" + n] = nc.dram_tensor(
                "dbg_" + n, shp, dt, kind="ExternalOutput").ap()
            io["_dbg"].append(n)
        io["_mkdbg"] = dbg
    declare_x(nc, io, "x3", "ExternalInput")
    io["ident"] = nc.dram_tensor("ident", [128, 128], F32,
                                 kind="ExternalInput").ap()
    declare_dram_D(nc, io, debug=debug)
    io["x4_tm"] = nc.dram_tensor("x4_tm", [NT, 128 * D], F32,
                                 kind="ExternalOutput").ap()
    pl = Plan()
    with ExitStack() as top:
        sems = make_sems(nc, top)
        glob = {}
        glob["ident"] = top.enter_context(nc.sbuf_tensor("g_ident", [128, 128], BF16))
        pl.op("gpsimd", lambda g: g.dma_start(
            out=glob["ident"][:, :], in_=io["ident"]), inc=(sems["wt"], 16))
        glob["id_tgt"] = sems["wt"].n
        glob["eps"] = top.enter_context(nc.sbuf_tensor("g_eps", [128, 1], F32))
        pl.op("vector", lambda v: v.memset(glob["eps"][:, :], 1e-5),
              inc=(sems["ev"], 1))
        with ExitStack() as st:
            s1d_stores = phase_D(nc, pl, st, io, sems, glob)
            pl.barrier(sems["bar"], dma_sems=[
                sems["ld"], sems["wt"], sems["wtA"], sems["wtB"], sems["sto"],
                sems["lh0"], sems["lh1"]])
        with ExitStack() as st:
            phase_LN(nc, pl, st, io, sems, glob,
                     ["x3_tm", "s1d"], "x4", s1d_stores, fold=False,
                     want_fm=False)
        if debug:
            for n in io["_dbg"]:
                pl.op("sync", lambda s, n=n: s.dma_start(
                    out=io["dbg_" + n], in_=io[n]),
                    waits=[(sems["st"], sems["st"].n), (sems["sto"], sems["sto"].n)],
                    inc=(sems["st"], 16))
        pl.wait("gpsimd", sems["st"], sems["st"].n)
        pl.wait("sync", sems["st"], sems["st"].n)
        pl.realize(nc)
    return nc


def _declare_full(nc, io, debug):
    ei = lambda n, shp, dt=F32: io.__setitem__(
        n, nc.dram_tensor(n, shp, dt, kind="ExternalInput").ap())
    internal = lambda n, shp, dt=BF16: io.__setitem__(
        n, nc.dram_tensor(n, shp, dt, kind="Internal").ap())
    ei("x_tm", [NT, 128 * D])
    ei("x_fm", [NC_D, 128 * TPC])
    ei("ident", [128, 128])
    ei("w_qk", [64, 128 * 32 * 128])
    ei("w_v", [8, 128 * 32 * 512])
    ei("b_qk", [128, 64])
    ei("b_v_t", [128, D])
    ei("w_o", [8, 128 * 32 * 512])
    ei("ob_t", [128, D])
    internal("q_dr", [32, 128 * 1024])
    internal("kv_src", [64, 128 * 1024])
    internal("kv_dst", [128, 128 * 1024])
    internal("s1a", [NT, 128 * D], F32)
    for p in ("x1", "x2", "x3"):
        internal(f"{p}_tm", [NT, 128 * D], F32)
        internal(f"{p}_fm", [NC_D, 128 * TPC], BF16)
        internal(f"{p}f_fm", [NT, 128 * TPC], BF16)
    declare_dram_B(nc, io, debug=debug)
    declare_dram_C(nc, io, debug=debug)
    declare_dram_D(nc, io, debug=debug)
    io["out_tm"] = nc.dram_tensor("out_tm", [NT, 128 * D], F32,
                                  kind="ExternalOutput").ap()


def build_full(debug=False):
    nc = bass.Bass("TRN2", target_bir_lowering=False)
    io = {}
    if debug:
        io["_dbg"] = []
        def dbg(n, shp, dt):
            io["dbg_" + n] = nc.dram_tensor(
                "dbg_" + n, shp, dt, kind="ExternalOutput").ap()
            io["_dbg"].append(n)
        io["_mkdbg"] = dbg
    _declare_full(nc, io, debug)
    pl = Plan()
    with ExitStack() as top:
        sems = make_sems(nc, top)
        glob = {}
        glob["ident"] = top.enter_context(nc.sbuf_tensor("g_ident", [128, 128], BF16))
        pl.op("gpsimd", lambda g: g.dma_start(
            out=glob["ident"][:, :], in_=io["ident"]), inc=(sems["wt"], 16))
        glob["id_tgt"] = sems["wt"].n
        glob["eps"] = top.enter_context(nc.sbuf_tensor("g_eps", [128, 1], F32))
        pl.op("vector", lambda v: v.memset(glob["eps"][:, :], 1e-5),
              inc=(sems["ev"], 1))

        allsems = [sems[k] for k in ("ld", "ld2", "wt", "wtA", "wtB", "sto",
                                     "lh0", "lh1", "lv0", "lv1", "st")]

        def bar():
            pl.barrier(sems["bar"], dma_sems=allsems)

        with ExitStack() as st:
            cc_done = phase_A1(nc, pl, st, io, sems, glob)
            bar()
        with ExitStack() as st:
            ctx = st.enter_context(nc.sbuf_tensor("a2_ctx", [128, 32 * 1024],
                                                  BF16, side="right"))
            with ExitStack() as st2:
                ctx_done = phase_A2(nc, pl, st2, io, sems, glob, cc_done, ctx)
            s1a_stores = phase_A3a(nc, pl, st, io, sems, glob, ctx, ctx_done)
            bar()
        with ExitStack() as st:
            fm1 = phase_LN(nc, pl, st, io, sems, glob,
                           ["x_tm", "s1a"], "x1", s1a_stores, fold=True)
            bar()
        with ExitStack() as st:
            s1b_stores = phase_B(nc, pl, st, io, sems, glob, fm_ready=fm1)
            bar()
        with ExitStack() as st:
            fm2 = phase_LN(nc, pl, st, io, sems, glob,
                           ["x1_tm", "s1b"], "x2", s1b_stores, fold=False)
            bar()
        with ExitStack() as st:
            ctx2, ctx2_done = phase_C(nc, pl, st, io, sems, glob,
                                      x_fm_key="x2_fm", fm_ready=fm2)
            s1c_stores = phase_A3a(nc, pl, st, io, sems, glob, ctx2, ctx2_done,
                                   w_key="o_w_t", bias_key="zeros_t",
                                   s1a_key="s1c")
            bar()
        with ExitStack() as st:
            fm3 = phase_LN(nc, pl, st, io, sems, glob,
                           ["x2_tm", "s1c"], "x3", s1c_stores, fold=False)
            bar()
        with ExitStack() as st:
            s1d_stores = phase_D(nc, pl, st, io, sems, glob,
                                 x_fm_key="x3_fm", fm_ready=fm3)
            bar()
        with ExitStack() as st:
            phase_LN(nc, pl, st, io, sems, glob,
                     ["x3_tm", "s1d"], "out", s1d_stores, fold=False,
                     want_fm=False)
        if debug:
            for n in io["_dbg"]:
                pl.op("sync", lambda s, n=n: s.dma_start(
                    out=io["dbg_" + n], in_=io[n]),
                    waits=[(sems["st"], sems["st"].n),
                           (sems["sto"], sems["sto"].n)],
                    inc=(sems["st"], 16))
        pl.wait("gpsimd", sems["st"], sems["st"].n)
        pl.wait("sync", sems["st"], sems["st"].n)
        pl.realize(nc)
    return nc


def host_prep_full(inputs):
    per_a = host_prep_A(inputs)
    pb = host_prep_B(inputs)
    pc = host_prep_C(inputs)
    pd = host_prep_D(inputs)
    per_core = []
    for c in range(NCORES):
        d = dict(per_a[c])
        d.update(pb)
        d.update(pc)
        d.update(host_prep_C_core(c))
        d.update(pd)
        d.update(host_prep_D_core(c))
        per_core.append(d)
    return per_core


_PROGS = {}


def _prog(name):
    if name not in _PROGS:
        _PROGS[name] = {
            "A": build_program_A, "B": build_program_B,
            "C": build_program_C, "D": build_program_D,
            "full": build_full,
        }[name](debug=False)
    return _PROGS[name]


def kernel(**inputs):
    per_core = host_prep_full(inputs)
    res = run_bass_kernel_spmd(_prog("full"), per_core,
                               core_ids=list(range(NCORES)))
    out = np.zeros((4, 2048, 4096), np.float32)
    for c in range(NCORES):
        b, p = c // 2, c % 2
        tiles = [2 * j + p for j in range(NT)]
        x4 = res.results[c]["out_tm"].reshape(NT, 128, 4096)
        for j, g in enumerate(tiles):
            out[b, g * 128:(g + 1) * 128, :] = x4[j]
    return out
